# version 07
# baseline (speedup 1.0000x reference)
import numpy as np
from contextlib import ExitStack

import concourse.bass as bass
import concourse.mybir as mybir
from concourse.bass_utils import run_bass_kernel_spmd
from concourse.aot_env import allow_ici_all_to_all

F32 = mybir.dt.float32
I32 = mybir.dt.int32
AF = mybir.ActivationFunctionType
ALU = mybir.AluOpType
AX = mybir.AxisListType

T = 2048
D = 1024
NDT = D // 128
NCORES = 8
TSC = 512
NTC = T // TSC
GENW = 17664

C1 = 1.5 * 2.0 ** 23
C2 = 1.5 * 2.0 ** 52

_STAGES = None

CONST_BASE = 384
_CONST_VALS = []

ENGS = ("sy", "po", "ve", "se", "te")


class B:
    def __init__(self, nc, es):
        self.nc = nc
        self.es = es
        self.prog = {e: [] for e in ENGS}
        self.sem = {}
        for name in ("s_sy", "s_po", "s_ve", "s_ve2", "s_ve3",
                     "s_se", "s_te", "s_te2", "s_cc"):
            self.sem[name] = es.enter_context(nc.semaphore(name))
        self.cnt = {k: 0 for k in self.sem}
        self.engsem = {"sy": "s_sy", "po": "s_po", "ve": "s_ve",
                       "se": "s_se", "te": "s_te"}
        self.streams = {}
        self.psum = [es.enter_context(nc.psum_tensor(f"PS{i}", [128, 512],
                                                     F32))
                     for i in range(8)]
        self.psum_tok = [None] * 8
        self.psum_rr = 0
        self.names = set()
        self.ve_prev = None
        self.ve_flip = 0
        self.te_flip = 0

    def emit(self, eng, fn, deps=(), inc=0, sem=None, hard=False):
        if eng == "ve":
            prev = self.ve_prev
            if inc == 0 and sem is None:
                inc = 1
            if sem is None:
                sem = ("s_ve", "s_ve2", "s_ve3")[self.ve_flip % 3]
                self.ve_flip += 1
            if prev is not None:
                deps = tuple(deps) + (prev,)
            hard = True
        elif eng == "te" and inc and sem is None:
            sem = ("s_te", "s_te2")[self.te_flip % 2]
            self.te_flip += 1
        semname = sem or self.engsem[eng]
        tok = None
        if inc:
            self.cnt[semname] += inc
            tok = (semname, self.cnt[semname])
        if eng == "ve" and semname.startswith("s_ve"):
            self.ve_prev = tok
        self.prog[eng].append((tuple(d for d in deps if d is not None),
                               fn, semname if inc else None, inc, hard))
        return tok

    def stream_sem(self, name):
        if name not in self.streams:
            semname = "sd_" + name
            self.sem[semname] = self.es.enter_context(
                self.nc.semaphore(semname))
            self.cnt[semname] = 0
            self.streams[name] = semname
        return self.streams[name]

    def dma(self, eng, out, in_, deps=(), slow=False, stream=None):
        semname = self.stream_sem(stream if stream else "q_" + eng)
        if slow:
            return self.emit(eng, lambda h: h.dma_start(
                out=out, in_=in_, allow_slow_non_contiguous=True),
                deps=deps, inc=16, sem=semname)
        return self.emit(eng, lambda h: h.dma_start(out=out, in_=in_),
                         deps=deps, inc=16, sem=semname)

    def take_bank(self):
        i = self.psum_rr % 8
        self.psum_rr += 1
        dep = self.psum_tok[i]
        return i, ((dep,) if dep else ())

    def bank_done(self, i, tok):
        self.psum_tok[i] = tok

    def sb(self, name, shape, dtype=F32):
        assert name not in self.names, name
        self.names.add(name)
        return self.es.enter_context(self.nc.sbuf_tensor(name, shape, dtype))

    def finalize(self, block):
        def run_list(h, eng):
            waited = {}
            own = {"ve": ("s_ve", "s_ve2", "s_ve3"),
                   "te": ("s_te", "s_te2")}.get(eng, (self.engsem[eng],))
            for deps, fn, semname, inc, hard in self.prog[eng]:
                for (sname, val) in deps:
                    if sname in own and not hard:
                        continue
                    if waited.get(sname, -1) >= val:
                        continue
                    h.wait_ge(self.sem[sname], val)
                    waited[sname] = val
                inst = fn(h)
                if semname is not None:
                    inst.then_inc(self.sem[semname], inc)

        @block.sync
        def _(h):
            run_list(h, "sy")

        @block.gpsimd
        def _(h):
            run_list(h, "po")

        @block.vector
        def _(h):
            run_list(h, "ve")

        @block.scalar
        def _(h):
            run_list(h, "se")

        @block.tensor
        def _(h):
            run_list(h, "te")



def ccol(st, val):
    val = float(val)
    if val in (0.0, 1.0):
        return val
    if val not in _CONST_VALS:
        _CONST_VALS.append(val)
        i = CONST_BASE + _CONST_VALS.index(val)
        st["b"].nc.const_aps.register(F32, val,
                                      st["CONSTS_T"][:, i:i + 1])
    return val


def ve_flush(st, deps=(), hard=False):
    b = st["b"]
    return b.emit("ve", lambda h: h.tensor_scalar(
        st["VEDUMMY"], st["cn"]["ones_row"][0:1, 0:64], 1.0, None,
        ALU.mult), deps=deps, inc=1, hard=hard)


def se_flush(st, deps=()):
    b = st["b"]
    return b.emit("se", lambda h: h.activation(
        st["SEDUMMY"], st["cn"]["ones_row"][0:1, 0:64], AF.Copy,
        bias=0.0, scale=1.0), deps=deps, inc=1)


def ccol_ap(st, val):
    val = float(val)
    if val not in _CONST_VALS:
        _CONST_VALS.append(val)
        i = CONST_BASE + _CONST_VALS.index(val)
        st["b"].nc.const_aps.register(F32, val,
                                      st["CONSTS_T"][:, i:i + 1])
    i = CONST_BASE + _CONST_VALS.index(val)
    return st["CONSTS_T"][:, i:i + 1]


def rne(st, out_ap, src_ap, u1, u2, deps=(), post_scale=None):
    b = st["b"]
    cn = st["cn"]
    b.emit("ve", lambda h: h.tensor_scalar(u2, src_ap, C2, C2, ALU.add,
                                           ALU.subtract), deps=deps)
    tchi = b.emit("ve", lambda h: h.tensor_scalar(u2, u2, cn["chi64"], None,
                                                  ALU.mult), inc=1)
    b.emit("ve", lambda h: h.tensor_scalar(u1, src_ap, C1, C1, ALU.add,
                                           ALU.subtract))
    if post_scale is None:
        return b.emit("ve", lambda h: h.scalar_tensor_tensor(
            out_ap, u1, cn["chi32"], u2, ALU.mult, ALU.add),
            deps=(tchi,), hard=True, inc=1)
    tst = b.emit("ve", lambda h: h.scalar_tensor_tensor(
        out_ap, u1, cn["chi32"], u2, ALU.mult, ALU.add),
        deps=(tchi,), hard=True, inc=1)
    return b.emit("ve", lambda h: h.tensor_scalar(
        out_ap, out_ap, float(post_scale), None, ALU.mult),
        deps=(tst,), hard=True, inc=1)


def ln_stats_chunk(st, get_tile, deps=(), eps=1e-5):
    b = st["b"]
    cn = st["cn"]
    SQ = st["SQ"]
    bk1, war1 = b.take_bank()
    bk2, war2 = b.take_bank()
    p1, p2 = b.psum[bk1], b.psum[bk2]
    tsums = None
    for dt in range(NDT):
        d = tuple(deps) + war1 if dt == 0 else ()
        tsums = b.emit("te", lambda h, dt=dt, p1=p1: h.matmul(
            p1[:, 0:TSC], cn["ones128"], get_tile(dt),
            start=(dt == 0), stop=(dt == NDT - 1)), deps=d,
            inc=1 if dt == NDT - 1 else 0)
    def sq_ap(dt):
        return SQ[:, (dt % 2) * TSC:(dt % 2 + 1) * TSC]

    def emit_sq(dt, extra=()):
        dd = tuple(extra)
        if st["sq_war"][dt % 2]:
            dd = dd + (st["sq_war"][dt % 2],)
        return b.emit("se", lambda h, dt=dt: h.activation(
            sq_ap(dt), get_tile(dt), AF.Square), deps=dd, inc=1)

    tsq = [None] * (NDT + 1)
    tsq[0] = emit_sq(0, extra=deps)
    tsq[1] = emit_sq(1, extra=deps)
    tsq2 = None
    for dt in range(NDT):
        gate = tsq[dt + 1] if dt < NDT - 1 else tsq[NDT]
        d2 = (gate,) + (war2 if dt == 0 else ())
        tsq2 = b.emit("te", lambda h, p2=p2, dt=dt: h.matmul(
            p2[:, 0:TSC], cn["ones128"], sq_ap(dt),
            start=(dt == 0), stop=(dt == NDT - 1)), deps=d2, inc=1)
        st["sq_war"][dt % 2] = tsq2
        if dt + 2 < NDT:
            tsq[dt + 2] = emit_sq(dt + 2)
        elif dt + 2 == NDT:
            tsq[NDT] = se_flush(st)
    tmp = SQ[:, 0:TSC]
    varb = SQ[:, TSC:2 * TSC]
    if st.get("dbg_sq2") is not None:
        b.emit("ve", lambda h, p2=p2: h.tensor_copy(
            st["dbg_sq2"], p2[:, 0:TSC]), deps=(tsq2,))
    b.emit("ve", lambda h, p1=p1: h.tensor_scalar(
        p1[:, 0:TSC], p1[:, 0:TSC], 1.0 / D, None, ALU.mult),
        deps=(tsums,), inc=1)
    tdiv = ve_flush(st)
    tvar = b.emit("ve", lambda h, p2=p2, varb=varb: h.tensor_scalar(
        varb, p2[:, 0:TSC], 1.0 / D, float(eps), ALU.mult, ALU.add),
        deps=(tsq2, st["sq_war"][1]), inc=1)
    b.bank_done(bk2, tvar)
    b.emit("se", lambda h, p1=p1, tmp=tmp: h.activation(
        tmp, p1[:, 0:TSC], AF.Square),
        deps=(st["sq_war"][0], tdiv), inc=1)
    tmusq = se_flush(st)
    if st.get("dbg_musq") is not None:
        b.emit("ve", lambda h, tmp=tmp: h.tensor_copy(
            st["dbg_musq"], tmp), deps=(tmusq,))
    b.emit("ve", lambda h, varb=varb, tmp=tmp: h.tensor_tensor(
        varb, varb, tmp, ALU.subtract), deps=(tmusq,))
    if st.get("dbg_var") is not None:
        b.emit("ve", lambda h, varb=varb: h.tensor_copy(
            st["dbg_var"], varb))
    trcp = b.emit("ve", lambda h, varb=varb: h.reciprocal(varb, varb),
                  inc=1)
    b.emit("se", lambda h, varb=varb, tmp=tmp: h.activation(
        tmp, varb, AF.Sqrt), deps=(trcp,), inc=1)
    tok = se_flush(st)
    st["sq_war"][1] = tok
    return bk1, tmp, tok


def bcast_row(st, row_ap, pbank, width, deps=()):
    b = st["b"]
    cn = st["cn"]
    assert width <= 512
    return b.emit("te", lambda h: h.matmul(
        pbank[:, 0:width], cn["ones_row"], row_ap, start=True, stop=True),
        deps=deps, inc=1)


def apply_ln_chunk(st, get_src, get_dst, bk_mu, rstd_ap, deps=()):
    b = st["b"]
    pm = b.psum[bk_mu]
    tok = None
    for dt in range(NDT):
        dst = get_dst(dt)
        b.emit("ve", lambda h, dt=dt, dst=dst: h.tensor_tensor(
            dst, get_src(dt), pm[:, 0:TSC], ALU.subtract),
            deps=deps if dt == 0 else ())
        tok = b.emit("ve", lambda h, dst=dst, rstd_ap=rstd_ap:
                     h.tensor_tensor(dst, dst, rstd_ap, ALU.mult), inc=1)
    b.bank_done(bk_mu, tok)
    st["sq_war"][0] = tok
    return tok


def postln_residual_chunk(st, tc, get_tile, deps):
    b = st["b"]
    XT = st["XT"]
    t0 = tc * TSC
    bk_mu, rstd_ap, tstat = ln_stats_chunk(st, get_tile, deps=deps)
    pm = b.psum[bk_mu]
    tok = None
    for dt in range(NDT):
        src_ap = get_tile(dt)
        b.emit("ve", lambda h, src_ap=src_ap: h.tensor_tensor(
            src_ap, src_ap, pm[:, 0:TSC], ALU.subtract),
            deps=(tstat,) if dt == 0 else ())
        b.emit("ve", lambda h, src_ap=src_ap, rstd_ap=rstd_ap:
               h.tensor_tensor(src_ap, src_ap, rstd_ap, ALU.mult))
        tok = b.emit("ve", lambda h, dt=dt, src_ap=src_ap: h.tensor_tensor(
            XT[:, dt * T + t0:dt * T + t0 + TSC],
            XT[:, dt * T + t0:dt * T + t0 + TSC], src_ap, ALU.add), inc=1)
    b.bank_done(bk_mu, tok)
    st["sq_war"][0] = tok
    return tok


def preln_chunk(st, tc, get_dst, deps=()):
    XT = st["XT"]
    t0 = tc * TSC
    bk_mu, rstd_ap, tstat = ln_stats_chunk(
        st, lambda dt: XT[:, dt * T + t0:dt * T + t0 + TSC], deps=deps)
    return apply_ln_chunk(
        st, lambda dt: XT[:, dt * T + t0:dt * T + t0 + TSC],
        get_dst, bk_mu, rstd_ap, deps=(tstat,))



def emit_pwl_ffn(st, pfx, src_fn, dest_fn, stage_deps=()):
    b = st["b"]
    cn = st["cn"]
    hbm = st["hbm"]
    FF = 4096
    NOT = FF // 128
    GEN = st["GEN"]
    HAR = st["BIG"][:, 0:NOT * TSC]
    XHC = GEN[:, 0:NDT * TSC]
    OUTC = GEN[:, NDT * TSC:2 * NDT * TSC]
    W1B = GEN[:, 2 * NDT * TSC:2 * NDT * TSC + 2048]
    W2B = GEN[:, 2 * NDT * TSC + 2048:2 * NDT * TSC + 4096]
    SCR = GEN[:, 2 * NDT * TSC + 4096:2 * NDT * TSC + 4096 + 5 * TSC]
    BCOL = GEN[:, 2 * NDT * TSC + 4096 + 5 * TSC:
               2 * NDT * TSC + 4096 + 5 * TSC + 40]

    w1 = hbm[pfx + "_w1"]
    b1 = hbm[pfx + "_b1"]
    w2 = hbm[pfx + "_w2"]
    b2 = hbm[pfx + "_b2"]
    B1C = BCOL[:, 0:NOT]
    B2C = BCOL[:, NOT:NOT + NDT]

    tb1 = b.dma("sy", B1C, b1.ap().rearrange("(o p) -> p o", p=128),
                slow=True, stream="bias1", deps=stage_deps)
    tb2 = b.dma("sy", B2C, b2.ap().rearrange("(o p) -> p o", p=128),
                slow=True, stream="bias2", deps=stage_deps)

    w1t = w1.ap().rearrange("(kt p) o -> kt p o", p=128)
    w2t = w2.ap().rearrange("(kt p) o -> kt p o", p=128)

    wtokA = [None, None]
    wtokB = [None, None]
    har_tok = [None] * NOT
    arena_war = None
    out_tok = None
    xh_war = None
    for tc in range(NTC):
        xhc, txh = src_fn(tc, XHC, xh_war)
        for ot in range(NOT):
            par = ot % 2
            w1sl = W1B[:, par * NDT * 128:(par + 1) * NDT * 128]
            lw1 = b.dma("sy",
                        w1sl.rearrange("p (kt o) -> p kt o", kt=NDT),
                        w1t[:, :, ot * 128:(ot + 1) * 128].rearrange(
                            "kt p o -> p kt o"),
                        deps=(wtokA[par],) if wtokA[par] else stage_deps,
                        stream=f"w1_{par}")
            bk, war = b.take_bank()
            pa = b.psum[bk]
            tmm = None
            for kt in range(NDT):
                d = war + (lw1, txh) if kt == 0 else ()
                tmm = b.emit("te", lambda h, kt=kt, pa=pa, w1sl=w1sl, xhc=xhc:
                             h.matmul(pa[:, 0:TSC],
                                      w1sl[:, kt * 128:(kt + 1) * 128],
                                      xhc[:, kt * TSC:(kt + 1) * TSC],
                                      start=(kt == 0), stop=(kt == NDT - 1)),
                             deps=d, inc=1 if kt == NDT - 1 else 0)
            wtokA[par] = tmm
            z = SCR[:, 0:TSC]
            u1 = SCR[:, TSC:2 * TSC]
            u2 = SCR[:, 2 * TSC:3 * TSC]
            xq = SCR[:, 3 * TSC:4 * TSC]
            gl = SCR[:, 0:TSC]
            gh = SCR[:, 4 * TSC:5 * TSC]
            b.emit("ve", lambda h, pa=pa, ot=ot, z=z: h.tensor_scalar(
                z, pa[:, 0:TSC], B1C[:, ot:ot + 1], 256.0, ALU.add, ALU.mult),
                deps=(tmm, tb1))
            tz = b.emit("ve", lambda h: h.engine_nop(), inc=1)
            b.bank_done(bk, tz)
            rne(st, xq, z, u1, u2)
            b.emit("ve", lambda h, xq=xq, u1=u1: h.tensor_scalar(
                u1, xq, -1024.0, 1024.0, ALU.max, ALU.min))
            b.emit("ve", lambda h, u1=u1, u2=u2: h.tensor_scalar(
                u2, u1, 1.0 / 64.0, 15.5, ALU.mult, ALU.add))
            b.emit("ve", lambda h, u2=u2, gl=gl: h.tensor_scalar(
                gl, u2, C2, C2, ALU.add, ALU.subtract))
            b.emit("ve", lambda h, gl=gl: h.tensor_scalar(
                gl, gl, cn["chi64"], None, ALU.mult))
            b.emit("ve", lambda h, u2=u2, gh=gh: h.tensor_scalar(
                gh, u2, C1, C1, ALU.add, ALU.subtract))
            b.emit("ve", lambda h, gh=gh, gl=gl, u1=u1:
                   h.scalar_tensor_tensor(u1, gh, cn["chi32"], gl,
                                          ALU.mult, ALU.add))
            b.emit("ve", lambda h, u1=u1, u2=u2: h.scalar_tensor_tensor(
                u2, u1, -1.0, u2, ALU.mult, ALU.add))
            ti = b.emit("ve", lambda h: h.engine_nop(), inc=1)
            b.emit("se", lambda h, u1=u1, gl=gl: h.activation(
                gl, u1, AF.Gelu, bias=ccol(st, -4.0), scale=ccol(st, 0.25)),
                deps=(ti,))
            b.emit("se", lambda h, u1=u1, gh=gh: h.activation(
                gh, u1, AF.Gelu, bias=ccol(st, -3.75),
                scale=ccol(st, 0.25)), inc=1)
            tg = se_flush(st)
            b.emit("ve", lambda h, gh=gh, gl=gl: h.tensor_tensor(
                gh, gh, gl, ALU.subtract), deps=(tg,))
            b.emit("ve", lambda h, u2=u2, gh=gh: h.scalar_tensor_tensor(
                gh, u2, 0.5, gh, ALU.add, ALU.mult))
            b.emit("ve", lambda h, gl=gl, gh=gh: h.tensor_tensor(
                gl, gl, gh, ALU.add))
            b.emit("ve", lambda h, xq=xq, gh=gh: h.scalar_tensor_tensor(
                gh, xq, 1024.0, xq, ALU.is_gt, ALU.mult))
            b.emit("ve", lambda h, xq=xq, gl=gl: h.scalar_tensor_tensor(
                gl, xq, 1024.0, gl, ALU.is_le, ALU.mult))
            b.emit("ve", lambda h, xq=xq, gl=gl: h.scalar_tensor_tensor(
                gl, xq, -1024.0, gl, ALU.is_ge, ALU.mult))
            b.emit("ve", lambda h, u1=u1, gl=gl, gh=gh:
                   h.scalar_tensor_tensor(u1, gl, 256.0, gh, ALU.mult,
                                          ALU.add))
            hdst = HAR[:, ot * TSC:(ot + 1) * TSC]
            dw = (arena_war,) if (tc > 0 and ot == 0) else ()
            b.emit("ve", lambda h, u1=u1, u2=u2: h.tensor_scalar(
                u2, u1, C2, C2, ALU.add, ALU.subtract), deps=dw)
            b.emit("ve", lambda h, u2=u2: h.tensor_scalar(
                u2, u2, cn["chi64"], None, ALU.mult))
            b.emit("ve", lambda h, u1=u1, gh=gh: h.tensor_scalar(
                gh, u1, C1, C1, ALU.add, ALU.subtract))
            b.emit("ve", lambda h, gh=gh, u2=u2, hdst=hdst:
                   h.scalar_tensor_tensor(hdst, gh, cn["chi32"], u2,
                                          ALU.mult, ALU.add))
            har_tok[ot] = b.emit("ve", lambda h, hdst=hdst: h.tensor_scalar(
                hdst, hdst, 1.0 / 256.0, None, ALU.mult), inc=1)
        xh_war = wtokA[(NOT - 1) % 2]
        tmm2 = None
        for dt in range(NDT):
            bk, war = b.take_bank()
            pa = b.psum[bk]
            for kg in range(NOT // 8):
                par = (dt * (NOT // 8) + kg) % 2
                w2b = W2B[:, par * 8 * 128:(par + 1) * 8 * 128]
                lw2 = b.dma("sy",
                            w2b.rearrange("p (kt o) -> p kt o", kt=8),
                            w2t[kg * 8:(kg + 1) * 8, :,
                                dt * 128:(dt + 1) * 128].rearrange(
                                "kt p o -> p kt o"),
                            deps=(wtokB[par],) if wtokB[par] else stage_deps,
                            stream=f"w2_{par}")
                for kk in range(8):
                    kt = kg * 8 + kk
                    d = [lw2, har_tok[kt]]
                    if kt == 0:
                        d += list(war)
                    tmm2 = b.emit(
                        "te", lambda h, kt=kt, kk=kk, pa=pa, w2b=w2b:
                        h.matmul(pa[:, 0:TSC],
                                 w2b[:, kk * 128:(kk + 1) * 128],
                                 HAR[:, kt * TSC:(kt + 1) * TSC],
                                 start=(kt == 0), stop=(kt == NOT - 1)),
                        deps=tuple(d),
                        inc=1 if (kt == NOT - 1 or kt % 8 == 7) else 0)
                wtokB[par] = tmm2
            z = SCR[:, 0:TSC]
            u1 = SCR[:, TSC:2 * TSC]
            u2 = SCR[:, 2 * TSC:3 * TSC]
            b.emit("ve", lambda h, pa=pa, dt=dt, z=z: h.tensor_scalar(
                z, pa[:, 0:TSC], B2C[:, dt:dt + 1], 256.0, ALU.add,
                ALU.mult), deps=(tmm2, tb2))
            tz = b.emit("ve", lambda h: h.engine_nop(), inc=1)
            b.bank_done(bk, tz)
            od = OUTC[:, dt * TSC:(dt + 1) * TSC]
            rne(st, od, z, u1, u2, post_scale=1.0 / 256.0)
        arena_war = tmm2
        tall = b.emit("ve", lambda h: h.engine_nop(), inc=1)
        out_tok = dest_fn(tc, OUTC, tall)
    return out_tok


def sandwich_pwl_ffn(st, pfx, deps):
    first = [True]

    def src_fn(tc, XHC, war):
        d = tuple(deps) if first[0] else ()
        first[0] = False
        if war:
            d = d + (war,)
        return XHC, preln_chunk(
            st, tc, lambda dt: XHC[:, dt * TSC:(dt + 1) * TSC], deps=d)

    def dest_fn(tc, outc, tok):
        return postln_residual_chunk(
            st, tc, lambda dt: outc[:, dt * TSC:(dt + 1) * TSC], (tok,))

    return emit_pwl_ffn(st, pfx, src_fn, dest_fn, stage_deps=tuple(deps))



def emit_l1_attn(st, deps):
    b = st["b"]
    nc = st["nc"]
    cn = st["cn"]
    hbm = st["hbm"]
    XT = st["XT"]
    BIG = st["BIG"]
    GEN = st["GEN"]
    N_ST = 16

    off = [0]

    def carve(w):
        ap = GEN[:, off[0]:off[0] + w]
        off[0] += w
        return ap

    XDBL = carve(T)
    DELTPRE = XDBL[0:64, :]
    BCS = XDBL[64:96, :]
    DELT = carve(T)
    DU = carve(T)
    DA = carve(T)
    DBU = carve(T)
    HT = carve(T)
    YACC = carve(T)
    BROW = st["RRT"][0:1, 128:128 + T]
    WXPF = carve(NDT * 128)
    WDTF = carve(NDT * 128)
    ACOLS = carve(N_ST * NDT)
    BDTC = carve(NDT)
    DCOL = carve(NDT)
    assert off[0] <= GENW - 2 * TSC, off[0]

    bc_hbm = nc.dram_tensor("l1_bc_scratch", [32, T], F32)

    wxp = hbm["L1_attn_W_xprojT"]
    wdt = hbm["L1_attn_W_dtT"]
    bdt = hbm["L1_attn_b_dt"]
    alog = hbm["L1_attn_A_log"]
    dpar = hbm["L1_attn_D_param"]

    xh_toks = []
    for tc in range(NTC):
        t0 = tc * TSC
        tok = preln_chunk(
            st, tc, lambda dt, t0=t0: BIG[:, dt * T + t0:dt * T + t0 + TSC],
            deps=deps if tc == 0 else ())
        xh_toks.append(tok)
    txh = xh_toks[-1]

    tbdt = b.dma("sy", BDTC, bdt.ap().rearrange("(o p) -> p o", p=128),
                 slow=True, stream="bias1", deps=(txh,))
    tdp = b.dma("sy", DCOL, dpar.ap().rearrange("(o p) -> p o", p=128),
                slow=True, stream="bias2", deps=(txh,))
    tal = b.dma("sy",
                ACOLS.rearrange("p (dt n) -> p dt n", dt=NDT),
                alog.ap().rearrange("(dt p) n -> p dt n", p=128),
                stream="acols", deps=deps)
    b.emit("se", lambda h: h.activation(ACOLS, ACOLS, AF.Exp), deps=(tal,))
    tac = b.emit("se", lambda h: h.activation(
        ACOLS, ACOLS, AF.Copy, bias=0.0, scale=ccol(st, -1.0)), inc=1)

    wxpt = wxp.ap().rearrange("(kt p) o -> kt p o", p=128)
    tdpre = None
    tbc = None
    twxpf = b.dma("sy", WXPF.rearrange("p (kt o) -> p kt o", kt=NDT),
                  wxp.ap().rearrange("(kt p) o -> p kt o", p=128),
                  stream="w1_0", deps=deps)
    twdtf = b.dma("sy", WDTF[0:64, :], wdt[:, :], stream="w2_0", deps=deps)
    for tc in range(NTC):
        t0 = tc * TSC
        bk1, war1 = b.take_bank()
        bk2, war2 = b.take_bank()
        pd, pb = b.psum[bk1], b.psum[bk2]
        tmm = None
        for kt in range(NDT):
            wsl = WXPF[:, kt * 128:(kt + 1) * 128]
            d1 = (twxpf, txh) + (war1 if kt == 0 else ())
            tmm = b.emit("te", lambda h, kt=kt, t0=t0, pd=pd, wsl=wsl:
                         h.matmul(pd[:, 0:TSC], wsl,
                                  BIG[:, kt * T + t0:kt * T + t0 + TSC],
                                  start=(kt == 0), stop=(kt == NDT - 1)),
                         deps=d1, inc=1 if kt == NDT - 1 else 0)
        tdpre = b.emit("ve", lambda h, pd=pd, t0=t0: h.tensor_copy(
            XDBL[0:96, t0:t0 + TSC], pd[0:96, 0:TSC]), deps=(tmm,),
            inc=1)
        b.bank_done(bk1, tdpre)
        tbc = tdpre
        b.bank_done(bk2, tbc)
    tbcf = ve_flush(st, deps=(tbc,))
    tbc_out = b.dma("po", bc_hbm[:, :], BCS[:, :], deps=(tbcf,),
                    stream="bcout")

    ytoks = []
    brow_war = None
    tfin = None
    for dt in range(NDT):
        wsl = WDTF[0:64, dt * 128:(dt + 1) * 128]
        tdel = None
        for tc in range(NTC):
            t0 = tc * TSC
            bk, war = b.take_bank()
            pa = b.psum[bk]
            tm = b.emit("te", lambda h, dt=dt, t0=t0, pa=pa, wsl=wsl:
                        h.matmul(pa[:, 0:TSC], wsl,
                                 DELTPRE[0:64, t0:t0 + TSC],
                                 start=True, stop=True),
                        deps=(twdtf, tdpre) + war, inc=1)
            te_ = b.emit("se", lambda h, pa=pa, dt=dt, t0=t0: h.activation(
                DELT[:, t0:t0 + TSC], pa[:, 0:TSC], AF.Exp,
                bias=BDTC[:, dt:dt + 1]), deps=(tm, tbdt), inc=1)
            b.bank_done(bk, te_)
            b.emit("se", lambda h, dt=dt, t0=t0: h.activation(
                DELT[:, t0:t0 + TSC], DELT[:, t0:t0 + TSC], AF.Ln,
                bias=1.0), inc=1)
            tdel = se_flush(st)
        tdu = b.emit("ve", lambda h, dt=dt: h.tensor_tensor(
            DU[:, :], DELT[:, :], BIG[:, dt * T:(dt + 1) * T], ALU.mult),
            deps=(tdel, tfin), inc=1)
        for n in range(N_ST):
            b.emit("se", lambda h, dt=dt, n=n: h.activation(
                DA[:, :], DELT[:, :], AF.Exp,
                scale=ACOLS[:, dt * N_ST + n:dt * N_ST + n + 1]),
                deps=(tac, tdel), inc=1)
            ta = se_flush(st)
            trow = b.dma("se", BROW, bc_hbm[n:n + 1, :],
                         deps=(tbc_out, brow_war), stream="brow")
            tdbu = None
            for tc in range(NTC):
                t0 = tc * TSC
                bk, war = b.take_bank()
                pa = b.psum[bk]
                tbb = bcast_row(st, BROW[0:1, t0:t0 + TSC], pa, TSC,
                                deps=(trow,) + war)
                tdbu = b.emit("ve", lambda h, t0=t0, pa=pa: h.tensor_tensor(
                    DBU[:, t0:t0 + TSC], DU[:, t0:t0 + TSC], pa[:, 0:TSC],
                    ALU.mult), deps=(tbb, tdu), inc=1)
                b.bank_done(bk, tdbu)
            tsc_ = b.emit("ve", lambda h: h.tensor_tensor_scan(
                HT[:, :], DA[:, :], DBU[:, :], 0.0, ALU.mult, ALU.add),
                deps=(ta,), inc=1)
            trowc = b.dma("se", BROW, bc_hbm[16 + n:17 + n, :],
                          deps=(tsc_,), stream="brow")
            thc = None
            tcb = None
            for tc in range(NTC):
                t0 = tc * TSC
                bk, war = b.take_bank()
                pa = b.psum[bk]
                tcb = bcast_row(st, BROW[0:1, t0:t0 + TSC], pa, TSC,
                                deps=(trowc,) + war)
                thc = b.emit("ve", lambda h, t0=t0, pa=pa: h.tensor_tensor(
                    DBU[:, t0:t0 + TSC], HT[:, t0:t0 + TSC], pa[:, 0:TSC],
                    ALU.mult), deps=(tcb,), inc=1)
                b.bank_done(bk, thc)
            brow_war = tcb
            if n == 0:
                tyn = b.emit("ve", lambda h: h.tensor_copy(
                    YACC[:, :], DBU[:, :]), inc=1)
            else:
                tyn = b.emit("ve", lambda h: h.tensor_tensor(
                    YACC[:, :], YACC[:, :], DBU[:, :], ALU.add), inc=1)
        if st.get("l1_dbg") and dt == 0:
            XTd = st["XT"]
            for j, src_ap in enumerate([DELT[:, :], DU[:, :], DA[:, :],
                                        DBU[:, :], HT[:, :], YACC[:, :]]):
                b.emit("ve", lambda h, j=j, src_ap=src_ap: h.tensor_copy(
                    XTd[:, j * T:(j + 1) * T], src_ap), deps=(tyn,))
            b.emit("ve", lambda h: h.tensor_copy(
                XTd[0:64, 6 * T:7 * T], DELTPRE[0:64, :]))
            tdbg = b.emit("ve", lambda h: h.engine_nop(), inc=1)
            return tdbg
        tfin = b.emit("ve", lambda h, dt=dt: h.scalar_tensor_tensor(
            BIG[:, dt * T:(dt + 1) * T], BIG[:, dt * T:(dt + 1) * T],
            DCOL[:, dt:dt + 1], YACC[:, :], ALU.mult, ALU.add),
            deps=(tyn, tdp), inc=1)
        ytoks.append(tfin)

    tok = None
    for tc in range(NTC):
        t0 = tc * TSC
        tok = postln_residual_chunk(
            st, tc, lambda dt, t0=t0: BIG[:, dt * T + t0:dt * T + t0 + TSC],
            (ytoks[-1],) if tc == 0 else ())
    return tok




_L3_PHASE = 4
_L3_NTB = 16
_L3_LOCKSTEP = False


def emit_fp6_quant(st, xap, scap, scr, width, deps=(), par=0):
    b = st["b"]
    cn = st["cn"]
    nb = width // 32
    ay = scr[:, 0:width]
    t2 = scr[:, width:2 * width]
    t3 = scr[:, 2 * width:3 * width]
    am = st["AM"][:, par * 32:par * 32 + nb]
    aml = st["AML"][:, 0:nb]
    b.emit("ve", lambda h: h.tensor_scalar(ay, xap, -1.0, None, ALU.mult),
           deps=deps)
    b.emit("ve", lambda h: h.tensor_tensor(ay, ay, xap, ALU.max))
    trds = ()
    for c0 in range(0, width, 512):
        nbc = min(512, width - c0) // 32
        amc = am[:, c0 // 32:c0 // 32 + nbc]
        x3 = bass.AP(xap.tensor, xap.offset + c0,
                     [xap.ap[0], [32, nbc], [1, 32]])
        trds = trds + (b.emit("ve", lambda h, amc=amc, x3=x3:
                              h.tensor_reduce(amc, x3, AX.X, ALU.max,
                                              apply_absolute_value=True),
                              deps=deps if c0 == 0 else (), inc=1),)
    tse = b.emit("se", lambda h: h.activation(
        aml, am, AF.Copy, bias=0.0, scale=1.0), deps=trds, inc=1)
    if st.get("amdump") is not None:
        b.emit("ve", lambda h: h.tensor_copy(st["amdump"], aml),
               deps=(se_flush(st, deps=(tse,)),))
    tfa = se_flush(st, deps=(tse,))
    r0blk = st["CONSTS_T"][:, 480:512]
    rf = st["R75F"]
    e32 = st["E32"]
    if not st.get("r75_done"):
        t_e = b.emit("ve", lambda h: h.tensor_scalar(
            e32, r0blk, -7.5, 2.0, ALU.mult, ALU.add), inc=1)
        st["r75_done"] = b.emit("ve", lambda h: h.tensor_tensor(
            rf, e32, r0blk, ALU.mult), deps=(t_e,), hard=True, inc=1)
    tr75 = st["r75_done"]
    b.emit("ve", lambda h: h.tensor_tensor(scap, aml, rf[:, 0:nb],
                                           ALU.mult), deps=(tr75, tfa))
    if st.get("amdump") is not None:
        b.emit("ve", lambda h: h.tensor_copy(st["amdump2"], rf))
        b.emit("ve", lambda h: h.tensor_copy(st["amdump3"], scap))
    tcl = b.emit("ve", lambda h: h.tensor_scalar(
        scap, scap, 1e-30, None, ALU.max), inc=1)

    def replicate(dst, seed, seed_tok):
        t = b.emit("ve", lambda h: h.tensor_copy(
            bass.AP(dst.tensor, dst.offset, [dst.ap[0], [32, nb]]),
            seed), deps=seed_tok)
        for j in range(1, 32):
            t = b.emit("ve", lambda h, j=j: h.tensor_copy(
                bass.AP(dst.tensor, dst.offset + j, [dst.ap[0], [32, nb]]),
                bass.AP(dst.tensor, dst.offset, [dst.ap[0], [32, nb]])),
                inc=(1 if j == 31 else 0))
        return t

    scf = st["SCF"][:, 0:width]
    replicate(scf, scap, ())
    sraw = st["SINV"][:, par * 32:par * 32 + nb]
    sinvl = st["SINVL"][:, 0:nb]
    tsv0 = b.emit("ve", lambda h: h.reciprocal(sraw, scap), deps=(tcl,),
                  inc=1)
    tsv = b.emit("se", lambda h: h.activation(
        sinvl, sraw, AF.Copy, bias=0.0, scale=1.0), deps=(tsv0,), inc=1)
    tsf0 = se_flush(st, deps=(tsv,))
    sif = st["SINVF"][:, 0:width]
    tsf = replicate(sif, sinvl, (tsf0,))
    b.emit("ve", lambda h: h.tensor_tensor(xap, xap, sif, ALU.mult),
           deps=(tsf,))
    b.emit("ve", lambda h: h.tensor_tensor(ay, ay, sif, ALU.mult))
    b.emit("ve", lambda h: h.tensor_scalar(t2, ay, 2.0, None, ALU.is_ge))
    b.emit("ve", lambda h: h.tensor_scalar(t3, ay, 4.0, None, ALU.is_ge))
    b.emit("ve", lambda h: h.tensor_scalar(ay, t2, -4.0, 8.0, ALU.mult,
                                           ALU.add))
    b.emit("ve", lambda h: h.scalar_tensor_tensor(ay, t3, -2.0, ay,
                                                  ALU.mult, ALU.add))
    b.emit("ve", lambda h: h.tensor_tensor(xap, xap, ay, ALU.mult))
    b.emit("ve", lambda h: h.tensor_scalar(t2, t2, 0.125, 0.125, ALU.mult,
                                           ALU.add))
    b.emit("ve", lambda h: h.scalar_tensor_tensor(t2, t3, 0.25, t2,
                                                  ALU.mult, ALU.add))
    trne = rne(st, xap, xap, ay, t3)
    b.emit("ve", lambda h: h.tensor_tensor(xap, xap, t2, ALU.mult),
           deps=(trne,), hard=True)
    return b.emit("ve", lambda h: h.tensor_tensor(xap, xap, scf,
                                                  ALU.mult), inc=1)


def emit_l3_ffn(st, deps):
    b = st["b"]
    nc = st["nc"]
    cn = st["cn"]
    hbm = st["hbm"]
    XT = st["XT"]
    BIG = st["BIG"]
    GEN = st["GEN"]
    w = hbm["L3_ffn_weight"]
    bias = hbm["L3_ffn_biasT"]

    off = [0]

    def carve(width):
        ap = GEN[:, off[0]:off[0] + width]
        off[0] += width
        return ap
    XTOK = carve(1024)
    QSCR = carve(3 * 1024)
    SCX = carve(32)
    SCW = carve(32)
    st["R75"] = carve(2)
    st["R75F"] = carve(32)
    st["E32"] = carve(32)
    st["SINV"] = carve(128)
    st["SINVL"] = carve(32)
    st["AM"] = carve(128)
    st["AML"] = carve(32)
    st["SCF"] = carve(1024)
    st["SINVF"] = carve(1024)
    WQ = carve(1024)
    WQT = carve(1024)
    WTB = carve(2 * 1024)
    OUTC = carve(NDT * TSC)
    BCOL = carve(NDT)
    assert off[0] <= GENW - 2 * TSC, off[0]

    wdt_hbm = nc.dram_tensor("l3f_wdt", [D, D], F32)

    tb_bias = b.dma("sy", BCOL, bias.ap(), stream="bias1",
                    deps=deps)

    xh_toks = []
    if _L3_PHASE in (20, 21, 30):
        tk = None
        for dt in range(NDT):
            tk = b.emit("ve", lambda h, dt=dt: h.tensor_copy(
                BIG[:, dt * T:(dt + 1) * T], XT[:, dt * T:(dt + 1) * T]),
                deps=deps if dt == 0 else (), inc=1)
        xh_toks.append(tk)
    for tc in range(NTC if _L3_PHASE not in (20, 21, 30) else 0):
        t0 = tc * TSC
        tok = preln_chunk(
            st, tc, lambda dt, t0=t0: BIG[:, dt * T + t0:dt * T + t0 + TSC],
            deps=deps if tc == 0 else ())
        xh_toks.append(tok)
    txh = xh_toks[-1]

    xq_toks = []
    if _L3_PHASE in (1, 21):
        return txh
    prev_q = None
    ntb = _L3_NTB if _L3_PHASE in (15, 16, 17, 18, 20, 25, 2) else 16
    for tb in range(ntb):
        tof = tb * 128
        tevs = []
        for dt in range(NDT):
            bk, war = b.take_bank()
            pt = b.psum[bk]
            d = war + ((txh,) if dt == 0 else ())
            if _L3_LOCKSTEP and dt >= 1:
                d = d + (tevs[dt - 1],)
            tt = b.emit("te", lambda h, dt=dt, pt=pt: h.transpose(
                pt[:, 0:128], BIG[:, dt * T + tof:dt * T + tof + 128],
                cn["ident"]), deps=d, inc=1)
            tev = b.emit("ve", lambda h, dt=dt, pt=pt: h.tensor_copy(
                XTOK[:, dt * 128:(dt + 1) * 128], pt[:, 0:128]),
                deps=(tt,) + ((prev_q,) if prev_q and dt == 0 else ()),
                inc=1, hard=True)
            tevs.append(tev)
            b.bank_done(bk, tev)
        tev = tevs[-1]
        if _L3_PHASE in (28, 29, 30) and tb == 0:
            st["qdump"] = lambda k: XT[:, (k + 1) * T:(k + 1) * T + 1024]
        if _L3_PHASE == 27 and tb == 0:
            return b.emit("ve", lambda h: h.tensor_copy(
                XT[:, 0:1024], XTOK[:, 0:1024]), deps=(tev,), inc=1)
        if _L3_PHASE in (15, 16, 17, 18, 20):
            tq = tev
        else:
            tq = emit_fp6_quant(st, XTOK, SCX, QSCR, 1024, deps=(tev,),
                                par=tb % 4)
        prev_q = tq
        if _L3_PHASE in (28, 29, 30) and tb == 0:
            st["qdump"] = None
            return b.emit("ve", lambda h: h.tensor_copy(
                XT[:, 0:1024], XTOK[:, 0:1024]), deps=(tq,), inc=1,
                hard=True)
        if _L3_PHASE == 26 and tb == 0:
            b.emit("ve", lambda h: h.tensor_copy(
                XT[:, 0:1024], XTOK[:, 0:1024]), deps=(tq,), hard=True)
            b.emit("ve", lambda h: h.tensor_copy(
                XT[:, 1024:1056], SCX[:, 0:32]))
            b.emit("ve", lambda h: h.tensor_copy(
                XT[:, 1056:2080], QSCR[:, 1024:2048]))
            return b.emit("ve", lambda h: h.tensor_copy(
                XT[:, 2080:2112], st["SINV"][:, 0:32]), inc=1)
        if _L3_PHASE in (16, 17, 18, 20):
            xq_toks.append(tq)
            continue
        tev2 = None
        for dt in range(NDT):
            bk2, war2 = b.take_bank()
            pt2 = b.psum[bk2]
            d = war2 + ((tq,) if dt == 0 else ())
            tt = b.emit("te", lambda h, dt=dt, pt2=pt2: h.transpose(
                pt2[:, 0:128], XTOK[:, dt * 128:(dt + 1) * 128],
                cn["ident"]), deps=d, inc=1)
            tev2 = b.emit("ve", lambda h, dt=dt, pt2=pt2: h.tensor_copy(
                BIG[:, dt * T + tof:dt * T + tof + 128], pt2[:, 0:128]),
                deps=(tt,), inc=1)
            b.bank_done(bk2, tev2)
        xq_toks.append(tev2)
    txd = xq_toks[-1]
    if _L3_PHASE == 25:
        cp = None
        for dt in range(NDT):
            cp = b.emit("ve", lambda h, dt=dt: h.tensor_copy(
                XT[:, dt * T:(dt + 1) * T], BIG[:, dt * T:(dt + 1) * T]),
                deps=(txd,) if dt == 0 else (), inc=1, hard=True)
        return cp
    if _L3_PHASE in (2, 15):
        return txd

    wq_tok = None
    wout_tok = None
    for ot in range(NDT):
        lw = b.dma("sy", WQ, w[ot * 128:(ot + 1) * 128, :],
                   deps=(wq_tok,) if wq_tok else deps, stream="w1_0")
        tqw = emit_fp6_quant(st, WQ, SCW, QSCR, 1024,
                             deps=(lw, txd), par=ot % 4)
        wq_tok = tqw
        tev = None
        for kt in range(NDT):
            bk, war = b.take_bank()
            pt = b.psum[bk]
            d = war + ((tqw,) if kt == 0 else ())
            tt = b.emit("te", lambda h, kt=kt, pt=pt: h.transpose(
                pt[:, 0:128], WQ[:, kt * 128:(kt + 1) * 128],
                cn["ident"]), deps=d, inc=1)
            tev = b.emit("ve", lambda h, kt=kt, pt=pt: h.tensor_copy(
                WQT[:, kt * 128:(kt + 1) * 128], pt[:, 0:128]),
                deps=(tt,) + ((wout_tok,) if wout_tok and kt == 0 else ()),
                inc=1)
            b.bank_done(bk, tev)
        tvf = ve_flush(st, deps=(tev,))
        wout_tok = b.dma("po", wdt_hbm.ap().rearrange(
            "(kt p) o -> p kt o", p=128)[:, :, ot * 128:(ot + 1) * 128],
            WQT.rearrange("p (kt o) -> p kt o", kt=NDT),
            deps=(tvf,), stream="bcout")

    if _L3_PHASE == 3:
        return wout_tok
    wdt3 = wdt_hbm.ap().rearrange("(kt p) o -> kt p o", p=128)
    wtok = [None, None]
    out_tok = None
    for tc in range(NTC):
        t0 = tc * TSC
        for dt in range(NDT):
            par = dt % 2
            wsl = WTB[:, par * 1024:(par + 1) * 1024]
            lw = b.dma("sy", wsl.rearrange("p (kt o) -> p kt o", kt=NDT),
                       wdt3[:, :, dt * 128:(dt + 1) * 128].rearrange(
                           "kt p o -> p kt o"),
                       deps=(wtok[par],) if wtok[par] else (wout_tok,),
                       stream=f"w2_{par}")
            bk, war = b.take_bank()
            pa = b.psum[bk]
            tmm = None
            for kt in range(NDT):
                d = war + (lw, txd) if kt == 0 else ()
                tmm = b.emit("te", lambda h, kt=kt, pa=pa, wsl=wsl, t0=t0:
                             h.matmul(pa[:, 0:TSC],
                                      wsl[:, kt * 128:(kt + 1) * 128],
                                      BIG[:, kt * T + t0:kt * T + t0 + TSC],
                                      start=(kt == 0), stop=(kt == NDT - 1)),
                             deps=d, inc=1 if kt == NDT - 1 else 0)
            wtok[par] = tmm
            tev = b.emit("ve", lambda h, pa=pa, dt=dt: h.tensor_scalar(
                OUTC[:, dt * TSC:(dt + 1) * TSC], pa[:, 0:TSC],
                BCOL[:, dt:dt + 1], None, ALU.add),
                deps=(tmm, tb_bias), inc=1)
            b.bank_done(bk, tev)
        tall = b.emit("ve", lambda h: h.engine_nop(), inc=1)
        out_tok = postln_residual_chunk(
            st, tc, lambda dt: OUTC[:, dt * TSC:(dt + 1) * TSC], (tall,))
    return out_tok


def emit_l2_ffn(st, deps):
    b = st["b"]
    nc = st["nc"]
    cn = st["cn"]
    hbm = st["hbm"]
    XT = st["XT"]
    BIG = st["BIG"]
    GEN = st["GEN"]

    off = [0]

    def carve(width):
        ap = GEN[:, off[0]:off[0] + width]
        off[0] += width
        return ap
    Y1 = carve(NDT * TSC)
    Y2 = carve(NDT * TSC)
    QT = carve(1024)
    SCT = carve(1024)
    WW2 = carve(2 * 1024)
    LAT = carve(1024)
    LBT = carve(1024)
    RR32 = carve(TSC)
    BCOL2 = carve(16)
    GCOL = carve(8)
    BECOL = carve(8)
    assert off[0] <= GENW - 2 * TSC, off[0]

    qw = hbm["L2_ffn_qT"]
    sct = hbm["L2_ffn_scT"]
    bt = hbm["L2_ffn_biasT"]
    lat = hbm["L2_ffn_laT"]
    lbt = hbm["L2_ffn_lbT"]
    lng = hbm["L2_ffn_ln_gT"]
    lnb = hbm["L2_ffn_ln_bT"]

    tk = None
    for tc in range(NTC):
        tk = preln_chunk(
            st, tc, lambda dt, tc=tc: BIG[:, dt * T + tc * TSC:
                                          dt * T + tc * TSC + TSC],
            deps=deps if tc == 0 else ())
    th = tk

    wtok = [None, None]
    ltok = ()
    btok = None

    def qlora_linear(li, src_fn, dst_fn, dst_bias_col, tin, extra_dep):
        nonlocal wtok, ltok, btok
        toks = []
        bk, war = b.take_bank()
        pr = b.psum[bk]
        tmm = None
        for kt in range(NDT):
            d = war + (tuple(ltok) + (tin,) + extra_dep
                       if kt == 0 else ())
            tmm = b.emit("te", lambda h, kt=kt, pr=pr: h.matmul(
                pr[:, 0:TSC], LAT[:, kt * 128:(kt + 1) * 128],
                src_fn(kt), start=(kt == 0), stop=(kt == NDT - 1)),
                deps=d, inc=1 if kt == NDT - 1 else 0)
        trr = b.emit("ve", lambda h, pr=pr: h.tensor_copy(
            RR32[:, 0:TSC], pr[:, 0:TSC]),
            deps=(tmm, st.get("l2_rrwar")))
        b.bank_done(bk, trr)
        for do in range(NDT):
            par = do % 2
            qsl = QT
            ssl = SCT
            lq = b.dma("sy", qsl.rearrange("p (kt o) -> p kt o", kt=NDT),
                       qw[li, :, do * 128:(do + 1) * 128].rearrange(
                           "(kt p) o -> p kt o", p=128),
                       deps=(st.get("l2_qwar"),), stream="l2q")
            ls = b.dma("sy", ssl.rearrange("p (kt o) -> p kt o", kt=NDT),
                       sct[li, :, do * 128:(do + 1) * 128].rearrange(
                           "(kt p) o -> p kt o", p=128),
                       deps=(st.get("l2_qwar"),), stream="l2s")
            WW = WW2[:, par * 1024:(par + 1) * 1024]
            b.emit("ve", lambda h, qsl=qsl, WW=WW: h.tensor_scalar(
                WW, qsl, float(1.0 / 7.5), -1.0, ALU.mult, ALU.add),
                deps=(lq, ls, wtok[par]))
            tw = b.emit("ve", lambda h, ssl=ssl, WW=WW: h.tensor_tensor(
                WW, WW, ssl, ALU.mult), inc=1)
            st["l2_qwar"] = tw
            bk2, war2 = b.take_bank()
            pa = b.psum[bk2]
            tmm = None
            for kt in range(NDT):
                d = war2 + (tw,) if kt == 0 else ()
                tmm = b.emit("te", lambda h, kt=kt, pa=pa, WW=WW: h.matmul(
                    pa[:, 0:TSC], WW[:, kt * 128:(kt + 1) * 128],
                    src_fn(kt), start=(kt == 0), stop=False),
                    deps=d, inc=1)
            wtok[par] = tmm
            tmm = b.emit("te", lambda h, do=do, pa=pa: h.matmul(
                pa[:, 0:TSC], LBT[0:32, do * 128:(do + 1) * 128],
                RR32[0:32, 0:TSC], start=False, stop=True),
                deps=(trr,), inc=1)
            st["l2_rrwar"] = tmm
            tvv = b.emit("ve", lambda h, do=do, pa=pa: h.tensor_scalar(
                dst_fn(do), pa[:, 0:TSC], dst_bias_col(do), None,
                ALU.add), deps=(tmm, btok), inc=1)
            b.bank_done(bk2, tvv)
            toks.append(tvv)
        return toks[-1]

    hprev = (th,)
    for blk in range(3):
        i0, i1 = 2 * blk, 2 * blk + 1
        btok0 = b.dma("sy", BCOL2[:, 0:8], bt[i0], stream="l2b",
                      deps=hprev)
        btok1 = b.dma("sy", BCOL2[:, 8:16], bt[i1], stream="l2b",
                      deps=(btok0,))
        gtok = b.dma("sy", GCOL, lng[blk], stream="l2g", deps=hprev)
        betok = b.dma("sy", BECOL, lnb[blk], stream="l2g", deps=(gtok,))
        lta = b.dma("sy", LAT.rearrange("p (kt m) -> p kt m", kt=NDT),
                    lat[i0].rearrange("(kt p) m -> p kt m", p=128),
                    stream="l2la", deps=hprev)
        ltb = b.dma("sy", LBT, lbt[i0], stream="l2lb", deps=hprev)
        ltok = (lta, ltb)
        btok = btok0
        out_tok = None
        for tc in range(NTC):
            t0 = tc * TSC
            t_y1 = qlora_linear(
                i0,
                lambda kt, t0=t0: BIG[:, kt * T + t0:kt * T + t0 + TSC],
                lambda do: Y1[:, do * TSC:(do + 1) * TSC],
                lambda do: BCOL2[:, do:do + 1],
                hprev[0] if tc == 0 else out_tok, ())
            bk_mu, rstd_ap, tstat = ln_stats_chunk(
                st, lambda dt: Y1[:, dt * TSC:(dt + 1) * TSC],
                deps=(t_y1,))
            pm = b.psum[bk_mu]
            tok = None
            for dt in range(NDT):
                ysl = Y1[:, dt * TSC:(dt + 1) * TSC]
                b.emit("ve", lambda h, ysl=ysl, pm=pm: h.tensor_tensor(
                    ysl, ysl, pm[:, 0:TSC], ALU.subtract),
                    deps=(tstat,) if dt == 0 else ())
                b.emit("ve", lambda h, ysl=ysl, rstd_ap=rstd_ap:
                       h.tensor_tensor(ysl, ysl, rstd_ap, ALU.mult))
                b.emit("ve", lambda h, ysl=ysl, dt=dt: h.tensor_scalar(
                    ysl, ysl, GCOL[:, dt:dt + 1], BECOL[:, dt:dt + 1],
                    ALU.mult, ALU.add), deps=(betok,))
                tok = b.emit("ve", lambda h, ysl=ysl: h.engine_nop(),
                             inc=1)
            b.bank_done(bk_mu, tok)
            st["sq_war"][0] = tok
            tvf = ve_flush(st, deps=(tok,))
            tge = None
            for dt in range(NDT):
                ysl = Y1[:, dt * TSC:(dt + 1) * TSC]
                tge = b.emit("se", lambda h, ysl=ysl: h.activation(
                    ysl, ysl, AF.Gelu), deps=(tvf,) if dt == 0 else (),
                    inc=1)
            tgf = se_flush(st, deps=(tge,))
            lta1 = b.dma("sy", LAT.rearrange("p (kt m) -> p kt m", kt=NDT),
                         lat[i1].rearrange("(kt p) m -> p kt m", p=128),
                         stream="l2la", deps=(t_y1,))
            ltb1 = b.dma("sy", LBT, lbt[i1], stream="l2lb", deps=(t_y1,))
            ltok = (lta1, ltb1)
            btok = btok1
            t_y2 = qlora_linear(
                i1,
                lambda kt: Y1[:, kt * TSC:(kt + 1) * TSC],
                lambda do: Y2[:, do * TSC:(do + 1) * TSC],
                lambda do: BCOL2[:, 8 + do:9 + do],
                tgf, ())
            cp = None
            for dt in range(NDT):
                cp = b.emit("ve", lambda h, dt=dt, t0=t0: h.tensor_tensor(
                    BIG[:, dt * T + t0:dt * T + t0 + TSC],
                    BIG[:, dt * T + t0:dt * T + t0 + TSC],
                    Y2[:, dt * TSC:(dt + 1) * TSC], ALU.add),
                    deps=(t_y2,) if dt == 0 else (), inc=1)
            out_tok = cp
            if tc < NTC - 1:
                lta0 = b.dma("sy",
                             LAT.rearrange("p (kt m) -> p kt m", kt=NDT),
                             lat[i0].rearrange("(kt p) m -> p kt m", p=128),
                             stream="l2la", deps=(t_y2,))
                ltb0 = b.dma("sy", LBT, lbt[i0], stream="l2lb",
                             deps=(t_y2,))
                ltok = (lta0, ltb0)
                btok = btok0
        hprev = (out_tok,)
    out_tok = None
    for tc in range(NTC):
        out_tok = postln_residual_chunk(
            st, tc, lambda dt, tc=tc: BIG[:, dt * T + tc * TSC:
                                          dt * T + tc * TSC + TSC],
            (hprev[0],) if tc == 0 else (out_tok,))
    return out_tok


def emit_l3_attn(st, deps):
    b = st["b"]
    nc = st["nc"]
    cn = st["cn"]
    hbm = st["hbm"]
    XT = st["XT"]
    BIG = st["BIG"]
    GEN = st["GEN"]
    NIT = 16
    NST = 64

    XD1 = GEN[:, 0:T]
    XD2 = GEN[:, T:2 * T]
    A1 = GEN[:, 2 * T:3 * T]
    A2 = GEN[:, 3 * T:4 * T]
    A3 = GEN[:, 4 * T:5 * T]
    A4 = GEN[:, 5 * T:6 * T]
    A5 = GEN[:, 6 * T:7 * T]
    mo = 7 * T
    ACOL = GEN[:, mo:mo + NIT * NST]
    BDT = GEN[:, mo + 1024:mo + 1040]
    DCOL = GEN[:, mo + 1040:mo + 1056]
    CVB = GEN[:, mo + 1056:mo + 1072]
    CVW = GEN[:, mo + 1072:mo + 1136]
    GCOL = GEN[:, mo + 1136:mo + 1144]
    BCOL = GEN[:, mo + 1144:mo + 1152]
    CVP = GEN[:, mo + 1152:mo + 1155]
    assert mo + 1155 <= GENW - 2 * TSC

    win = hbm["L3_attn_W_in"]
    wxp = hbm["L3A_WXP"]
    wdtp = hbm["L3A_WDTP"]
    wout = hbm["L3_attn_W_out"]

    xi_hbm = nc.dram_tensor("l3a_xi", [2048, T], F32)
    z_hbm = nc.dram_tensor("l3a_z", [2048, T], F32)
    u_hbm = nc.dram_tensor("l3a_u", [2048, T], F32)
    yf_hbm = nc.dram_tensor("l3a_yf", [2048, T], F32)

    tg = b.dma("sy", GCOL, hbm["L3A_lnw"].ap(), stream="l3c", deps=deps)
    tg = b.dma("sy", BCOL, hbm["L3A_lnb"].ap(), stream="l3c", deps=(tg,))
    tg = b.dma("sy", BDT, hbm["L3A_bdt"].ap(), stream="l3c", deps=(tg,))
    tg = b.dma("sy", DCOL, hbm["L3A_D"].ap(), stream="l3c", deps=(tg,))
    tg = b.dma("sy", CVW, hbm["L3A_convT"].ap(), stream="l3c", deps=(tg,))
    tg = b.dma("sy", CVB, hbm["L3A_convb"].ap(), stream="l3c", deps=(tg,))
    tga = b.dma("sy", ACOL, hbm["L3A_ALOG"].ap(), stream="l3c", deps=(tg,))
    tal = b.emit("se", lambda h: h.activation(
        ACOL, ACOL, AF.Exp), deps=(tga,), inc=1)
    tsfa = se_flush(st, deps=(tal,))
    tav = b.emit("ve", lambda h: h.tensor_scalar(
        ACOL, ACOL, -1.0, None, ALU.mult), deps=(tsfa,))
    tavf = ve_flush(st, deps=(tav,))

    xh_hbm = nc.dram_tensor("l3a_xh", [1024, T], F32)
    tk = None
    txs = None
    for tc in range(NTC):
        tk = preln_chunk(
            st, tc, lambda dt, tc=tc: BIG[:, dt * T + tc * TSC:
                                          dt * T + tc * TSC + TSC],
            deps=deps if tc == 0 else ())
    for dt in range(NDT):
        txs = b.dma("po", xh_hbm[dt * 128:(dt + 1) * 128, :],
                    BIG[:, dt * T:(dt + 1) * T],
                    deps=(tk,) if dt == 0 else (txs,), stream="l3xh")
    for tc in range(NTC):
        t0 = tc * TSC
        bk_mu, rstd_ap, tstat = ln_stats_chunk(
            st, lambda dt, t0=t0: BIG[:, dt * T + t0:dt * T + t0 + TSC],
            deps=(tk, txs) if tc == 0 else ())
        pm = b.psum[bk_mu]
        for dt in range(NDT):
            sl = BIG[:, dt * T + t0:dt * T + t0 + TSC]
            b.emit("ve", lambda h, sl=sl, pm=pm: h.tensor_tensor(
                sl, sl, pm[:, 0:TSC], ALU.subtract),
                deps=(tstat, txs) if dt == 0 else ())
            b.emit("ve", lambda h, sl=sl, rstd_ap=rstd_ap: h.tensor_tensor(
                sl, sl, rstd_ap, ALU.mult))
            b.emit("ve", lambda h, sl=sl, dt=dt: h.tensor_scalar(
                sl, sl, GCOL[:, dt:dt + 1], None, ALU.mult), deps=(tg,))
            tk = b.emit("ve", lambda h, sl=sl, dt=dt: h.tensor_scalar(
                sl, sl, BCOL[:, dt:dt + 1], None, ALU.add), inc=1)
        b.bank_done(bk_mu, tk)
        st["sq_war"][0] = tk
    txh = tk

    wpar = [None, None]
    tout = None
    for ot in range(32):
        par = ot % 2
        wsl = (A1 if par == 0 else A2)[:, 0:1024]
        lw = b.dma("sy", wsl.rearrange("p (kt m) -> p kt m", kt=NDT),
                   win[:, ot * 128:(ot + 1) * 128].rearrange(
                       "(kt p) m -> p kt m", p=128),
                   deps=(wpar[par],) if wpar[par] else (txh,),
                   stream=f"l3w{par}")
        for tc in range(NTC):
            t0 = tc * TSC
            bk, war = b.take_bank()
            pa = b.psum[bk]
            tmm = None
            for kt in range(NDT):
                d = war + ((lw, txh) if kt == 0 else ())
                tmm = b.emit("te", lambda h, kt=kt, pa=pa, wsl=wsl, t0=t0:
                             h.matmul(pa[:, 0:TSC],
                                      wsl[:, kt * 128:(kt + 1) * 128],
                                      BIG[:, kt * T + t0:kt * T + t0 + TSC],
                                      start=(kt == 0), stop=(kt == NDT - 1)),
                             deps=d, inc=1 if kt == NDT - 1 else 0)
            if tc == NTC - 1:
                wpar[par] = tmm
            tcp = b.emit("ve", lambda h, pa=pa, tc=tc: h.tensor_copy(
                A3[:, tc * TSC:(tc + 1) * TSC], pa[:, 0:TSC]),
                deps=(tmm, tout))
            b.bank_done(bk, tcp)
            dst = xi_hbm if ot < 16 else z_hbm
            ro = (ot % 16) * 128
            tout = b.dma("po", dst[ro:ro + 128, t0:t0 + TSC],
                         A3[:, tc * TSC:(tc + 1) * TSC],
                         deps=(tcp,), stream="l3o")
    t_p1 = tout

    tprev = t_p1
    tz = b.emit("ve", lambda h: h.memset(CVP, 0.0))
    for it in range(NIT):
        tld = b.dma("sy", A1[:, 0:T], xi_hbm[it * 128:(it + 1) * 128, :],
                    deps=(tprev,), stream="l3cv")
        tu = b.emit("ve", lambda h, it=it: h.tensor_scalar(
            A2[:, 0:T], A1[:, 0:T], CVW[:, it * 4 + 3:it * 4 + 4], None,
            ALU.mult), deps=(tld, tz, tg))
        for k in range(3):
            w_ = 3 - k
            tu = b.emit("ve", lambda h, it=it, k=k, w_=w_:
                        h.scalar_tensor_tensor(
                            A2[:, w_:T], A1[:, 0:T - w_],
                            CVW[:, it * 4 + k:it * 4 + k + 1],
                            A2[:, w_:T], ALU.mult, ALU.add))
        tu = b.emit("ve", lambda h, it=it: h.tensor_scalar(
            A2[:, 0:T], A2[:, 0:T], CVB[:, it:it + 1], None, ALU.add))
        tvf = ve_flush(st, deps=(tu,))
        tsg = b.emit("se", lambda h: h.activation(
            A5[:, 0:T], A2[:, 0:T], AF.Sigmoid), deps=(tvf,), inc=1)
        tsf = se_flush(st, deps=(tsg,))
        tu2 = b.emit("ve", lambda h: h.tensor_tensor(
            A2[:, 0:T], A2[:, 0:T], A5[:, 0:T], ALU.mult), deps=(tsf,))
        tprev = b.dma("po", u_hbm[it * 128:(it + 1) * 128, :], A2[:, 0:T],
                      deps=(tu2,), stream="l3o")
    t_p2 = tprev

    twx = b.dma("sy", A1[:, 0:T].rearrange("p (kt m) -> p kt m", kt=8),
                wxp.ap()[0:1024].rearrange("(kt p) m -> p kt m", p=128),
                deps=(t_p2,), stream="l3w0")
    twx2 = b.dma("sy", A2[:, 0:T].rearrange("p (kt m) -> p kt m", kt=8),
                 wxp.ap()[1024:2048].rearrange("(kt p) m -> p kt m", p=128),
                 deps=(t_p2,), stream="l3w1")

    def wxt(it, g):
        src = A1 if it < 8 else A2
        return src[:, (it % 8) * 256 + g * 128:(it % 8) * 256 + g * 128 + 128]
    upar = [None, None]
    tl_tok = [None, None]
    t_p3 = None
    for tc in range(NTC):
        t0 = tc * TSC
        for g in range(2):
            bk, war = b.take_bank()
            pa = b.psum[bk]
            tmm = None
            for it in range(NIT):
                p_ = it % 2
                usl = A3[:, p_ * TSC:(p_ + 1) * TSC]
                tl = b.dma("sy", usl,
                           u_hbm[it * 128:(it + 1) * 128, t0:t0 + TSC],
                           deps=(upar[p_],) if upar[p_] else (t_p2,),
                           stream=f"l3u{p_}")
                d = war + ((twx, twx2) if it == 0 else ())
                tmm = b.emit("te", lambda h, it=it, pa=pa, g=g, usl=usl,
                             tl=tl: h.matmul(
                                 pa[:, 0:TSC], wxt(it, g), usl,
                                 start=(it == 0), stop=(it == NIT - 1)),
                             deps=d + (tl,), inc=1)
                upar[p_] = tmm
            dstg = XD1 if g == 0 else XD2
            t_p3 = b.emit("ve", lambda h, pa=pa, dstg=dstg, t0=t0:
                          h.tensor_copy(dstg[:, t0:t0 + TSC], pa[:, 0:TSC]),
                          deps=(tmm,))
            b.bank_done(bk, t_p3)
    t_p3 = b.emit("ve", lambda h: h.engine_nop(), deps=(t_p3,), inc=1)

    twd = b.dma("sy", A3[:, 0:T], wdtp.ap(), deps=(upar[0], upar[1]),
                stream="l3w0")

    B1 = BIG[:, 0:T]
    B2 = BIG[:, T:2 * T]
    B3 = BIG[:, 2 * T:3 * T]
    B4 = BIG[:, 3 * T:4 * T]
    B5 = BIG[:, 4 * T:5 * T]
    B6 = BIG[:, 5 * T:6 * T]
    B7 = BIG[:, 6 * T:7 * T]
    B8 = BIG[:, 7 * T:8 * T]
    zcol = CVP[:, 0:1]

    tyout = None
    tscan_prev = None
    for it in range(NIT):
        tdel = None
        for tc in range(NTC):
            t0 = tc * TSC
            bk, war = b.take_bank()
            pa = b.psum[bk]
            tmm = b.emit("te", lambda h, it=it, pa=pa, t0=t0: h.matmul(
                pa[:, 0:TSC], A3[:, it * 128:(it + 1) * 128],
                XD1[:, t0:t0 + TSC], start=True, stop=True),
                deps=war + ((twd, t_p3) if tc == 0 and it == 0 else ()),
                inc=1)
            tdel = b.emit("ve", lambda h, pa=pa, t0=t0, it=it:
                          h.tensor_scalar(B4[:, t0:t0 + TSC], pa[:, 0:TSC],
                                          BDT[:, it:it + 1], None, ALU.add),
                          deps=(tmm,) + ((tyout,) if tyout and tc == 0
                                         else ()))
            b.bank_done(bk, tdel)
        tvf = ve_flush(st, deps=(tdel,))
        te1 = b.emit("se", lambda h: h.activation(
            B7, B4, AF.Exp), deps=(tvf,), inc=1)
        te2 = b.emit("se", lambda h: h.activation(
            B4, B7, AF.Ln, bias=1.0), deps=(te1,), inc=1)
        tsf = se_flush(st, deps=(te2,))
        tld = b.dma("sy", B5, u_hbm[it * 128:(it + 1) * 128, :],
                    deps=(tscan_prev,), stream="l3cv")
        tdu = b.emit("ve", lambda h: h.tensor_tensor(
            B5, B5, B4, ALU.mult), deps=(tld, tsf))
        tya = b.emit("ve", lambda h: h.memset(B6, 0.0), inc=1)
        for n in range(NST):
            tda = b.emit("se", lambda h, it=it, n=n: h.activation(
                B3, B4, AF.Exp,
                scale=ACOL[:, it * NST + n:it * NST + n + 1]),
                deps=(tavf, tdu, tya) + ((tscan_prev,)
                                         if tscan_prev else ()), inc=1)
            tdaf = se_flush(st, deps=(tda,))
            BR = st["RRT"][0:1, 128:128 + T]
            tbr = b.dma("sy", BR, XD1[64 + n:65 + n, 0:T],
                        deps=(st.get("l3brwar"), t_p3),
                        stream=f"l3br{n % 2}")
            tdbu = None
            bts = []
            for tc in range(NTC):
                t0 = tc * TSC
                bk, war = b.take_bank()
                pb = b.psum[bk]
                tb_ = b.emit("te", lambda h, pb=pb, t0=t0: h.matmul(
                    pb[:, 0:TSC], cn["ones_row"], BR[0:1, t0:t0 + TSC],
                    start=True, stop=True), deps=war + (tbr,), inc=1)
                bts.append(tb_)
                tdbu = b.emit("ve", lambda h, pb=pb, t0=t0: h.tensor_tensor(
                    B1[:, t0:t0 + TSC], B5[:, t0:t0 + TSC], pb[:, 0:TSC],
                    ALU.mult), deps=(tb_,), inc=1)
                b.bank_done(bk, tdbu)
            tcr = b.dma("sy", BR, XD2[n:n + 1, 0:T],
                        deps=(bts[-1],), stream=f"l3brc{n % 2}")
            tsc_ = b.emit("ve", lambda h: h.tensor_tensor_scan(
                B2, B3, B1, zcol, ALU.mult, ALU.add),
                deps=(tdbu, tdaf), inc=1)
            tyn = None
            for tc in range(NTC):
                t0 = tc * TSC
                bk, war = b.take_bank()
                pc = b.psum[bk]
                tb_ = b.emit("te", lambda h, pc=pc, t0=t0: h.matmul(
                    pc[:, 0:TSC], cn["ones_row"], BR[0:1, t0:t0 + TSC],
                    start=True, stop=True), deps=war + (tcr,), inc=1)
                st["l3brwar"] = tb_
                b.emit("ve", lambda h, pc=pc, t0=t0: h.tensor_tensor(
                    B8[:, t0:t0 + TSC], B2[:, t0:t0 + TSC], pc[:, 0:TSC],
                    ALU.mult), deps=(tb_, tsc_))
                tyn = b.emit("ve", lambda h, t0=t0: h.tensor_tensor(
                    B6[:, t0:t0 + TSC], B6[:, t0:t0 + TSC],
                    B8[:, t0:t0 + TSC], ALU.add), inc=1)
                b.bank_done(bk, tyn)
            tscan_prev = tyn
        tlu = b.dma("sy", B7, u_hbm[it * 128:(it + 1) * 128, :],
                    deps=(tdu,), stream="l3cv")
        tyd = b.emit("ve", lambda h, it=it: h.scalar_tensor_tensor(
            B6, B7, DCOL[:, it:it + 1], B6, ALU.mult, ALU.add),
            deps=(tlu, tscan_prev))
        tlz = b.dma("sy", B7, z_hbm[it * 128:(it + 1) * 128, :],
                    deps=(tyd,), stream="l3cv")
        tvf2 = ve_flush(st, deps=(tyd,))
        tsg = b.emit("se", lambda h: h.activation(
            B8, B7, AF.Sigmoid), deps=(tlz, tvf2), inc=1)
        tsf2 = se_flush(st, deps=(tsg,))
        b.emit("ve", lambda h: h.tensor_tensor(B7, B7, B8, ALU.mult),
               deps=(tsf2, tlz))
        tyf = b.emit("ve", lambda h: h.tensor_tensor(B6, B6, B7, ALU.mult))
        tyout = b.dma("po", yf_hbm[it * 128:(it + 1) * 128, :], B6,
                      deps=(tyf,), stream="l3o")
        tscan_prev = tyout

    wpar2 = [None, None]
    ypar = [None, None]
    out_tok = None
    for tc in range(NTC):
        t0 = tc * TSC
        for dto in range(NDT):
            par = dto % 2
            wsl = (A1 if par == 0 else A2)[:, 0:T]
            lw = b.dma("sy", wsl.rearrange("p (kt m) -> p kt m", kt=16),
                       wout[:, dto * 128:(dto + 1) * 128].rearrange(
                           "(kt p) m -> p kt m", p=128),
                       deps=(wpar2[par],) if wpar2[par] else
                       (tyout, t_p3), stream=f"l3w{par}")
            bk, war = b.take_bank()
            pa = b.psum[bk]
            tmm = None
            for kt in range(16):
                p_ = kt % 2
                ysl = B7[:, p_ * TSC:(p_ + 1) * TSC]
                tl = b.dma("sy", ysl,
                           yf_hbm[kt * 128:(kt + 1) * 128, t0:t0 + TSC],
                           deps=(ypar[p_],) if ypar[p_] else (tyout,),
                           stream=f"l3u{p_}")
                tmm = b.emit("te", lambda h, kt=kt, pa=pa, wsl=wsl, tl=tl,
                             lw=lw: h.matmul(
                                 pa[:, 0:TSC],
                                 wsl[:, kt * 128:(kt + 1) * 128],
                                 B7[:, (kt % 2) * TSC:(kt % 2 + 1) * TSC],
                                 start=(kt == 0), stop=(kt == 15)),
                             deps=(tl,) + (war + (lw,) if kt == 0 else ()),
                             inc=1)
                ypar[kt % 2] = tmm
            wpar2[par] = tmm
            xsl = A3[:, (dto % 4) * TSC:(dto % 4 + 1) * TSC]
            lxh = b.dma("sy", xsl,
                        xh_hbm[dto * 128:(dto + 1) * 128, t0:t0 + TSC],
                        deps=(st.get("l3xhwar"), tyout), stream="l3xh")
            tcp = b.emit("ve", lambda h, pa=pa, dto=dto, xsl=xsl:
                         h.tensor_tensor(
                             BIG[:, dto * TSC:(dto + 1) * TSC], xsl,
                             pa[:, 0:TSC], ALU.add), deps=(tmm, lxh),
                         inc=1)
            st["l3xhwar"] = tcp
            b.bank_done(bk, tcp)
        out_tok = postln_residual_chunk(
            st, tc, lambda dto: BIG[:, dto * TSC:(dto + 1) * TSC],
            (tcp,))
    return out_tok





def emit_l2_attn(st, deps):
    b = st["b"]
    nc = st["nc"]
    cn = st["cn"]
    hbm = st["hbm"]
    XT = st["XT"]
    BIG = st["BIG"]
    GEN = st["GEN"]

    WPAR = GEN[:, 0:2048]
    SCR = GEN[:, 2048:2560]
    LAMT = GEN[:, 2560:3072]
    INPT = GEN[:, 3072:3584]
    COLS = GEN[:, 3584:3648]
    IJ = GEN[:, 3648:5696]
    LJ = GEN[:, 5696:7744]
    HH = GEN[:, 7744:9792]
    GT = GEN[:, 3648:7744]
    OUTC = GEN[:, 7744:11840]

    cc_in = nc.dram_tensor("l2a_in", [2048, T], F32)
    cc_out = nc.dram_tensor("l2a_out", [8 * 2048, T], F32)
    gate_hbm = nc.dram_tensor("l2a_gate", [1024, T], F32)

    tg = b.dma("sy", COLS[:, 0:8], hbm["L2A_binT"].ap(), stream="l2ac",
               deps=deps)
    tg = b.dma("sy", COLS[:, 8:16], hbm["L2A_blamT"].ap(), stream="l2ac",
               deps=(tg,))
    tg = b.dma("sy", COLS[:, 16:24], hbm["L2A_bgateT"].ap(), stream="l2ac",
               deps=(tg,))
    tg = b.dma("sy", COLS[:, 24:32], hbm["L2A_boutT"].ap(), stream="l2ac",
               deps=(tg,))
    tg = b.dma("sy", COLS[:, 32:40], hbm["L2A_gamT"].ap(), stream="l2ac",
               deps=(tg,))
    tg = b.dma("sy", COLS[:, 40:48], hbm["L2A_betT"].ap(), stream="l2ac",
               deps=(tg,))
    tg = b.dma("sy", COLS[:, 48:56], hbm["bsel"].ap(), stream="l2ac",
               deps=(tg,))

    tk = None
    for tc in range(NTC):
        tk = preln_chunk(
            st, tc, lambda dt, tc=tc: BIG[:, dt * T + tc * TSC:
                                          dt * T + tc * TSC + TSC],
            deps=deps if tc == 0 else ())
    txh = tk

    wts = {"in": hbm["L2A_WinT"], "lam": hbm["L2A_WlamT"],
           "gate": hbm["L2A_WgateT"]}
    wpar_tok = {}
    dout = None
    for ot in range(NDT):
        lws = {}
        for wi, name in enumerate(("in", "lam", "gate")):
            wsl = WPAR[:, (wi % 2) * 1024:(wi % 2) * 1024 + 1024]
            lws[name] = (wsl, None)
        for tc in range(NTC):
            t0 = tc * TSC

            def mk(name, wi, extra=()):
                wsl = WPAR[:, (wi % 2) * 1024:(wi % 2) * 1024 + 1024]
                lw = b.dma("sy",
                           wsl.rearrange("p (kt m) -> p kt m", kt=NDT),
                           wts[name][:, ot * 128:(ot + 1) * 128].rearrange(
                               "(kt p) m -> p kt m", p=128),
                           deps=(wpar_tok.get(wi % 2), txh),
                           stream=f"l2aw{wi % 2}")
                bk, war = b.take_bank()
                pa = b.psum[bk]
                tmm = None
                for kt in range(NDT):
                    d = war + ((lw,) if kt == 0 else ())
                    tmm = b.emit("te", lambda h, kt=kt, pa=pa, wsl=wsl,
                                 t0=t0: h.matmul(
                                     pa[:, 0:TSC],
                                     wsl[:, kt * 128:(kt + 1) * 128],
                                     BIG[:, kt * T + t0:kt * T + t0 + TSC],
                                     start=(kt == 0), stop=(kt == NDT - 1)),
                                 deps=d, inc=1)
                wpar_tok[wi % 2] = tmm
                return pa, tmm, bk
            pa, tmm, bk = mk("lam", 0)
            b.emit("ve", lambda h, pa=pa, ot=ot: h.tensor_scalar(
                LAMT, pa[:, 0:TSC], COLS[:, 8 + ot:9 + ot], None, ALU.add),
                deps=(tmm, tg, dout))
            tvf = ve_flush(st)
            tsg = b.emit("se", lambda h: h.activation(
                SCR, LAMT, AF.Sigmoid), deps=(tvf,), inc=1)
            tsf = se_flush(st, deps=(tsg,))
            tlam = b.emit("ve", lambda h: h.tensor_copy(LAMT, SCR),
                          deps=(tsf,), inc=1)
            b.bank_done(bk, tlam)
            pa2, tmm2, bk2 = mk("in", 1)
            b.emit("ve", lambda h, pa2=pa2, ot=ot: h.tensor_scalar(
                INPT, pa2[:, 0:TSC], COLS[:, ot:ot + 1], None, ALU.add),
                deps=(tmm2,))
            tvf = ve_flush(st)
            tsg = b.emit("se", lambda h: h.activation(
                SCR, INPT, AF.Sigmoid), deps=(tvf,), inc=1)
            tsf = se_flush(st, deps=(tsg,))
            b.emit("ve", lambda h: h.tensor_tensor(
                INPT, INPT, SCR, ALU.mult), deps=(tsf,))
            b.emit("ve", lambda h: h.tensor_tensor(
                SCR, LAMT, INPT, ALU.mult))
            tinp = b.emit("ve", lambda h: h.tensor_tensor(
                INPT, INPT, SCR, ALU.subtract), inc=1)
            b.bank_done(bk2, tinp)
            d1 = b.dma("po", cc_in[ot * 128:(ot + 1) * 128, t0:t0 + TSC],
                       INPT, deps=(tinp, dout), stream="l2o")
            d2 = b.dma("po",
                       cc_in[1024 + ot * 128:1024 + (ot + 1) * 128,
                             t0:t0 + TSC],
                       LAMT, deps=(tinp,), stream="l2o")
            pa3, tmm3, bk3 = mk("gate", 0)
            b.emit("ve", lambda h, pa3=pa3, ot=ot: h.tensor_scalar(
                INPT, pa3[:, 0:TSC], COLS[:, 16 + ot:17 + ot], None,
                ALU.add), deps=(tmm3, d1, d2))
            tvf = ve_flush(st)
            tsg = b.emit("se", lambda h: h.activation(
                SCR, INPT, AF.Sigmoid), deps=(tvf,), inc=1)
            tsf = se_flush(st, deps=(tsg,))
            tgate = b.emit("ve", lambda h: h.tensor_tensor(
                INPT, INPT, SCR, ALU.mult), deps=(tsf,), inc=1)
            b.bank_done(bk3, tgate)
            dout = b.dma("po", gate_hbm[ot * 128:(ot + 1) * 128,
                                        t0:t0 + TSC],
                         INPT, deps=(tgate,), stream="l2o")
    tcc = b.emit("po", lambda h: h.collective_compute(
        "AllGather", ALU.bypass, replica_groups=[list(range(8))],
        ins=[cc_in.ap().opt()], outs=[cc_out.ap().opt()]),
        deps=(dout,), inc=1, sem="s_cc")
    tsel = None
    for dt in range(NDT):
        for j in range(8):
            li = b.dma("sy", IJ,
                       cc_out[j * 2048 + dt * 128:
                              j * 2048 + dt * 128 + 128, :],
                       deps=(tcc, tsel), stream="l2sc")
            ll = b.dma("sy", LJ,
                       cc_out[j * 2048 + 1024 + dt * 128:
                              j * 2048 + 1024 + dt * 128 + 128, :],
                       deps=(li,), stream="l2sc")
            if j == 0:
                b.emit("ve", lambda h: h.tensor_copy(HH, IJ),
                       deps=(li, ll))
                tsel = b.emit("ve", lambda h, dt=dt: h.tensor_scalar(
                    BIG[:, dt * T:(dt + 1) * T], HH, COLS[:, 48:49],
                    None, ALU.mult), inc=1)
            else:
                b.emit("ve", lambda h: h.tensor_tensor(
                    HH, HH, LJ, ALU.mult), deps=(li, ll))
                b.emit("ve", lambda h: h.tensor_tensor(
                    HH, HH, IJ, ALU.add))
                tsel = b.emit("ve", lambda h, dt=dt, j=j:
                              h.scalar_tensor_tensor(
                                  BIG[:, dt * T:(dt + 1) * T], HH,
                                  COLS[:, 48 + j:49 + j],
                                  BIG[:, dt * T:(dt + 1) * T],
                                  ALU.mult, ALU.add), inc=1)
    tln = tsel
    for tc in range(NTC):
        t0 = tc * TSC
        bk_mu, rstd_ap, tstat = ln_stats_chunk(
            st, lambda dt, t0=t0: BIG[:, dt * T + t0:dt * T + t0 + TSC],
            deps=(tln,))
        pm = b.psum[bk_mu]
        for dt in range(NDT):
            sl = BIG[:, dt * T + t0:dt * T + t0 + TSC]
            lgt = b.dma("sy", GT[:, dt * TSC:(dt + 1) * TSC],
                        gate_hbm[dt * 128:(dt + 1) * 128, t0:t0 + TSC],
                        deps=(tln,), stream="l2sc")
            b.emit("ve", lambda h, sl=sl, pm=pm: h.tensor_tensor(
                sl, sl, pm[:, 0:TSC], ALU.subtract),
                deps=(tstat,) if dt == 0 else ())
            b.emit("ve", lambda h, sl=sl, rstd_ap=rstd_ap: h.tensor_tensor(
                sl, sl, rstd_ap, ALU.mult))
            b.emit("ve", lambda h, sl=sl, dt=dt: h.tensor_scalar(
                sl, sl, COLS[:, 32 + dt:33 + dt], COLS[:, 40 + dt:41 + dt],
                ALU.mult, ALU.add))
            tln = b.emit("ve", lambda h, sl=sl, dt=dt: h.tensor_tensor(
                sl, sl, GT[:, dt * TSC:(dt + 1) * TSC], ALU.mult),
                deps=(lgt,), inc=1)
        b.bank_done(bk_mu, tln)
        st["sq_war"][0] = tln
    out_tok = None
    wpt = None
    for tc in range(NTC):
        t0 = tc * TSC
        for ot in range(NDT):
            par = ot % 2
            wsl = WPAR[:, par * 1024:(par + 1) * 1024]
            lw = b.dma("sy", wsl.rearrange("p (kt m) -> p kt m", kt=NDT),
                       hbm["L2A_WoutT"][:, ot * 128:(ot + 1) * 128]
                       .rearrange("(kt p) m -> p kt m", p=128),
                       deps=(wpar_tok.get(par), tln), stream=f"l2aw{par}")
            bk, war = b.take_bank()
            pa = b.psum[bk]
            tmm = None
            for kt in range(NDT):
                d = war + ((lw, tln) if kt == 0 else ())
                tmm = b.emit("te", lambda h, kt=kt, pa=pa, wsl=wsl, t0=t0:
                             h.matmul(pa[:, 0:TSC],
                                      wsl[:, kt * 128:(kt + 1) * 128],
                                      BIG[:, kt * T + t0:kt * T + t0 + TSC],
                                      start=(kt == 0), stop=(kt == NDT - 1)),
                             deps=d, inc=1)
            wpar_tok[par] = tmm
            tcp = b.emit("ve", lambda h, pa=pa, ot=ot: h.tensor_scalar(
                OUTC[:, ot * TSC:(ot + 1) * TSC], pa[:, 0:TSC],
                COLS[:, 24 + ot:25 + ot], None, ALU.add),
                deps=(tmm,), inc=1)
            b.bank_done(bk, tcp)
        out_tok = postln_residual_chunk(
            st, tc, lambda ot: OUTC[:, ot * TSC:(ot + 1) * TSC], (tcp,))
    return out_tok


def emit_l0_attn(st, deps):
    b = st["b"]
    nc = st["nc"]
    cn = st["cn"]
    hbm = st["hbm"]
    XT = st["XT"]
    BIG = st["BIG"]
    GEN = st["GEN"]
    C = 32
    NCH = T // C

    QH = GEN[:, 0:4096]
    KH = GEN[:, 4096:8192]
    VH = GEN[:, 8192:12288]
    BB = GEN[:, 12288:14336]
    SS = GEN[:, 14336:14848]
    RM = GEN[:, 14848:15040]
    TMV = GEN[:, 15040:16320]
    WFM = GEN[:, 16320:16384]
    KBC = GEN[:, 16384:16448]
    CV = GEN[:, 16448:16544]
    TRI = GEN[0:32, 16544:16640]
    NW = st["CONSTS_T"][:, 417:425]
    EYE32 = cn["ident"][0:32, 0:32]
    assert GENW - 2 * TSC >= 16640

    q_hbm = nc.dram_tensor("l0_q", [1024, T], F32)
    k_hbm = nc.dram_tensor("l0_k", [1024, T], F32)
    v_hbm = nc.dram_tensor("l0_v", [1024, T], F32)
    bm_hbm = nc.dram_tensor("l0_bm", [8, T], F32)
    o_hbm = nc.dram_tensor("l0_o", [T, 1024], F32)

    tg = b.dma("sy", NW, hbm["L0_normwT"].ap(), stream="l0c", deps=deps)
    tg2 = b.dma("sy", CV, hbm["L0_convT"].ap(), stream="l0c", deps=(tg,))
    tg3 = b.dma("sy", TRI, hbm["L0_TRIL"].ap(), stream="l0c", deps=(tg2,))

    tk = None
    for tc in range(NTC):
        tk = preln_chunk(
            st, tc, lambda dt, tc=tc: BIG[:, dt * T + tc * TSC:
                                          dt * T + tc * TSC + TSC],
            deps=deps if tc == 0 else ())
    txh = tk

    wts = (("q", hbm["L0_WqT"], q_hbm), ("k", hbm["L0_WkT"], k_hbm),
           ("v", hbm["L0_WvT"], v_hbm))
    wpt = {}
    dout = None
    for wi, (nm, wt, dst) in enumerate(wts):
        for ot in range(NDT):
            par = ot % 2
            wsl = (QH if par == 0 else KH)[:, 0:1024]
            lw = b.dma("sy", wsl.rearrange("p (kt m) -> p kt m", kt=NDT),
                       wt[:, ot * 128:(ot + 1) * 128].rearrange(
                           "(kt p) m -> p kt m", p=128),
                       deps=(wpt.get(par), txh), stream=f"l0w{par}")
            for tc in range(NTC):
                t0 = tc * TSC
                bk, war = b.take_bank()
                pa = b.psum[bk]
                tmm = None
                for kt in range(NDT):
                    d = war + ((lw,) if kt == 0 else ())
                    tmm = b.emit("te", lambda h, kt=kt, pa=pa, wsl=wsl,
                                 t0=t0: h.matmul(
                                     pa[:, 0:TSC],
                                     wsl[:, kt * 128:(kt + 1) * 128],
                                     BIG[:, kt * T + t0:kt * T + t0 + TSC],
                                     start=(kt == 0), stop=(kt == NDT - 1)),
                                 deps=d, inc=1)
                wpt[par] = tmm
                tcp = b.emit("ve", lambda h, pa=pa: h.tensor_copy(
                    VH[:, 0:TSC], pa[:, 0:TSC]), deps=(tmm, dout))
                b.bank_done(bk, tcp)
                dout = b.dma("po", dst[ot * 128:(ot + 1) * 128,
                                       t0:t0 + TSC], VH[:, 0:TSC],
                             deps=(tcp,), stream="l0o")
    lwb = b.dma("sy", QH[:, 0:1024].rearrange("p (kt m) -> p kt m", kt=NDT),
                hbm["L0_WbmT"].ap().rearrange("(kt p) m -> p kt m", p=128),
                deps=(wpt.get(0), txh), stream="l0w0")
    for tc in range(NTC):
        t0 = tc * TSC
        bk, war = b.take_bank()
        pa = b.psum[bk]
        tmm = None
        for kt in range(NDT):
            d = war + ((lwb,) if kt == 0 else ())
            tmm = b.emit("te", lambda h, kt=kt, pa=pa, t0=t0: h.matmul(
                pa[:, 0:TSC], QH[:, kt * 128:(kt + 1) * 128],
                BIG[:, kt * T + t0:kt * T + t0 + TSC],
                start=(kt == 0), stop=(kt == NDT - 1)), deps=d, inc=1)
        tcp = b.emit("ve", lambda h, pa=pa: h.tensor_copy(
            VH[:, 0:TSC], pa[:, 0:TSC]), deps=(tmm, dout))
        tvf = ve_flush(st)
        tsg = b.emit("se", lambda h: h.activation(
            VH[:, 0:TSC], VH[:, 0:TSC], AF.Sigmoid), deps=(tvf,), inc=1)
        tsf = se_flush(st, deps=(tsg,))
        b.bank_done(bk, tsf)
        dout = b.dma("po", bm_hbm[0:8, t0:t0 + TSC], VH[0:8, 0:TSC],
                     deps=(tsf,), stream="l0o")
    t_mm = dout

    tprev = t_mm
    for wi, (nm, wt, dst) in enumerate(wts):
        for dt in range(NDT):
            tld = b.dma("sy", QH[:, 0:T], dst[dt * 128:(dt + 1) * 128, :],
                        deps=(tprev,), stream="l0cv")
            cb = 32 * wi + dt * 4
            tu = b.emit("ve", lambda h, cb=cb: h.tensor_scalar(
                KH[:, 0:T], QH[:, 0:T], CV[:, cb + 3:cb + 4], None,
                ALU.mult), deps=(tld, tg2))
            for k in range(3):
                w_ = 3 - k
                tu = b.emit("ve", lambda h, cb=cb, k=k, w_=w_:
                            h.scalar_tensor_tensor(
                                KH[:, w_:T], QH[:, 0:T - w_],
                                CV[:, cb + k:cb + k + 1],
                                KH[:, w_:T], ALU.mult, ALU.add))
            tvf = ve_flush(st, deps=(tu,))
            tsg = b.emit("se", lambda h: h.activation(
                VH[:, 0:T], KH[:, 0:T], AF.Sigmoid), deps=(tvf,), inc=1)
            tsf = se_flush(st, deps=(tsg,))
            tu2 = b.emit("ve", lambda h: h.tensor_tensor(
                KH[:, 0:T], KH[:, 0:T], VH[:, 0:T], ALU.mult), deps=(tsf,))
            tprev = b.dma("po", dst[dt * 128:(dt + 1) * 128, :], KH[:, 0:T],
                          deps=(tu2,), stream="l0o")
    t_cv = tprev

    def l2norm_pair(buf, eps, dep0):
        tok = dep0
        for tc in range(NTC):
            t0 = tc * TSC
            bk1, war1 = b.take_bank()
            p1 = b.psum[bk1]
            tqs = None
            for sub in range(2):
                tq_ = b.emit("se", lambda h, sub=sub, t0=t0: h.activation(
                    st["SQ"][:, sub * TSC:(sub + 1) * TSC],
                    buf[:, sub * T + t0:sub * T + t0 + TSC], AF.Square),
                    deps=(tok, st["sq_war"][sub]) if sub == 0 else
                    (st["sq_war"][sub],), inc=1)
                tqs = se_flush(st, deps=(tq_,))
            for sub in range(2):
                tmm = b.emit("te", lambda h, sub=sub, p1=p1: h.matmul(
                    p1[:, 0:TSC], cn["ones128"],
                    st["SQ"][:, sub * TSC:(sub + 1) * TSC],
                    start=(sub == 0), stop=(sub == 1)),
                    deps=(tqs,) + (war1 if sub == 0 else ()), inc=1)
            varb = st["SQ"][:, TSC:2 * TSC]
            tv = b.emit("ve", lambda h, p1=p1: h.tensor_scalar(
                varb, p1[:, 0:TSC], 1.0, float(eps), ALU.mult, ALU.add),
                deps=(tmm,))
            trc = b.emit("ve", lambda h: h.reciprocal(varb, varb), inc=1)
            tsq = b.emit("se", lambda h: h.activation(
                st["SQ"][:, 0:TSC], varb, AF.Sqrt), deps=(trc,), inc=1)
            tsf = se_flush(st, deps=(tsq,))
            st["sq_war"][1] = tsf
            for sub in range(2):
                tok = b.emit("ve", lambda h, sub=sub, t0=t0: h.tensor_tensor(
                    buf[:, sub * T + t0:sub * T + t0 + TSC],
                    buf[:, sub * T + t0:sub * T + t0 + TSC],
                    st["SQ"][:, 0:TSC], ALU.mult), deps=(tsf,), inc=1)
            st["sq_war"][0] = tok
            b.bank_done(bk1, tok)
        return tok

    BR = st["RRT"][0:1, 128:128 + T]
    tdone = t_cv
    for hd in range(4):
        tq1 = b.dma("sy", QH[:, 0:T], q_hbm[hd * 256:hd * 256 + 128, :],
                    deps=(tdone,), stream="l0cv")
        tq2 = b.dma("sy", QH[:, T:2 * T],
                    q_hbm[hd * 256 + 128:hd * 256 + 256, :],
                    deps=(tq1,), stream="l0cv")
        tk1 = b.dma("sy", KH[:, 0:T], k_hbm[hd * 256:hd * 256 + 128, :],
                    deps=(tq2,), stream="l0cv")
        tk2 = b.dma("sy", KH[:, T:2 * T],
                    k_hbm[hd * 256 + 128:hd * 256 + 256, :],
                    deps=(tk1,), stream="l0cv")
        tv1 = b.dma("sy", VH[:, 0:T], v_hbm[hd * 256:hd * 256 + 128, :],
                    deps=(tk2,), stream="l0cv")
        tv2 = b.dma("sy", VH[:, T:2 * T],
                    v_hbm[hd * 256 + 128:hd * 256 + 256, :],
                    deps=(tv1,), stream="l0cv")
        tqn = l2norm_pair(QH, 1e-06, tq2)
        tkn = l2norm_pair(KH, 1e-06, tk2)
        tbr = b.dma("sy", BR, bm_hbm[hd:hd + 1, :], deps=(t_mm,),
                    stream="l0br")
        tbb = None
        for tc in range(NTC):
            t0 = tc * TSC
            bk, war = b.take_bank()
            pb = b.psum[bk]
            tb_ = b.emit("te", lambda h, pb=pb, t0=t0: h.matmul(
                pb[:, 0:TSC], cn["ones_row"], BR[0:1, t0:t0 + TSC],
                start=True, stop=True), deps=war + (tbr,), inc=1)
            tbb = b.emit("ve", lambda h, pb=pb, t0=t0: h.tensor_copy(
                BB[:, t0:t0 + TSC], pb[:, 0:TSC]), deps=(tb_,), inc=1)
            b.bank_done(bk, tbb)
        tvb = None
        for sub in range(2):
            tvb = b.emit("ve", lambda h, sub=sub: h.tensor_tensor(
                VH[:, sub * T:(sub + 1) * T], VH[:, sub * T:(sub + 1) * T],
                BB[:, 0:T], ALU.mult), deps=(tbb, tv2), inc=1)
        ts0 = b.emit("ve", lambda h: h.memset(SS, 0.0), inc=1)
        AT = RM[0:32, 0:32]
        ANT = RM[0:32, 32:64]
        R = RM[0:32, 64:96]
        RT = RM[0:32, 96:128]
        N2 = RM[0:32, 128:160]
        NT2 = RM[0:32, 160:192]
        VTM = TMV[0:32, 0:256]
        KBTM = TMV[0:32, 256:512]
        KTM = TMV[0:32, 512:768]
        UNEW = TMV[0:32, 768:1024]
        OTM = TMV[0:32, 1024:1280]
        tlast = (tqn, tkn, tvb, ts0, tg3)
        for ci in range(NCH):
            c0 = ci * C
            tkb = None
            for sub in range(2):
                tkb = b.emit("ve", lambda h, sub=sub, c0=c0:
                             h.tensor_tensor(
                                 KBC[:, sub * 32:(sub + 1) * 32],
                                 KH[:, sub * T + c0:sub * T + c0 + C],
                                 BB[:, c0:c0 + C], ALU.mult),
                             deps=tlast if ci == 0 else (), inc=1)
            trs = []
            for sub in range(2):
                for src, dst_ in ((VH[:, sub * T + c0:sub * T + c0 + C],
                                   VTM[0:32, sub * 128:(sub + 1) * 128]),
                                  (KBC[:, sub * 32:(sub + 1) * 32],
                                   KBTM[0:32, sub * 128:(sub + 1) * 128]),
                                  (KH[:, sub * T + c0:sub * T + c0 + C],
                                   KTM[0:32, sub * 128:(sub + 1) * 128])):
                    bk, war = b.take_bank()
                    pt = b.psum[bk]
                    tt = b.emit("te", lambda h, src=src, pt=pt: h.transpose(
                        pt[0:32, 0:128], src, cn["ident"]),
                        deps=war + ((tkb,) if not trs else ()), inc=1)
                    tcpy = b.emit("ve", lambda h, pt=pt, dst_=dst_:
                                  h.tensor_copy(dst_, pt[0:32, 0:128]),
                                  deps=(tt,), inc=1)
                    b.bank_done(bk, tcpy)
                    trs.append(tcpy)
            ttr = trs[-1]
            bk, war = b.take_bank()
            pa = b.psum[bk]
            for sub in range(2):
                tmm = b.emit("te", lambda h, sub=sub, pa=pa, c0=c0:
                             h.matmul(pa[0:32, 0:32],
                                      KBC[:, sub * 32:(sub + 1) * 32],
                                      KH[:, sub * T + c0:sub * T + c0 + C],
                                      start=(sub == 0), stop=(sub == 1)),
                             deps=war + ((tkb,) if sub == 0 else ()), inc=1)
            tat = b.emit("ve", lambda h, pa=pa: h.tensor_tensor(
                AT, pa[0:32, 0:32], TRI[0:32, 0:32], ALU.mult), deps=(tmm,))
            tat = b.emit("ve", lambda h: h.tensor_scalar(
                AT, AT, -1.0, None, ALU.mult), inc=1)
            b.bank_done(bk, tat)
            bk, war = b.take_bank()
            pa2 = b.psum[bk]
            for sub in range(2):
                tmm = b.emit("te", lambda h, sub=sub, pa2=pa2, c0=c0:
                             h.matmul(pa2[0:32, 0:32],
                                      KH[:, sub * T + c0:sub * T + c0 + C],
                                      KBC[:, sub * 32:(sub + 1) * 32],
                                      start=(sub == 0), stop=(sub == 1)),
                             deps=war + ((tkb,) if sub == 0 else ()), inc=1)
            tant = b.emit("ve", lambda h, pa2=pa2: h.tensor_tensor(
                ANT, pa2[0:32, 0:32], TRI[0:32, 64:96], ALU.mult),
                deps=(tmm,))
            tant = b.emit("ve", lambda h: h.tensor_scalar(
                ANT, ANT, -1.0, None, ALU.mult), inc=1)
            b.bank_done(bk, tant)
            b.emit("ve", lambda h: h.tensor_tensor(
                R, AT, EYE32, ALU.add), deps=(tat, tant))
            tr_ = b.emit("ve", lambda h: h.tensor_tensor(
                RT, ANT, EYE32, ALU.add), inc=1)
            cur, curT = AT, ANT
            for lv in range(4):
                bk, war = b.take_bank()
                p1 = b.psum[bk]
                t1 = b.emit("te", lambda h, p1=p1, cur=cur, curT=curT:
                            h.matmul(p1[0:32, 0:32], curT, cur,
                                     start=True, stop=True),
                            deps=war + (tr_,), inc=1)
                bk2, war2 = b.take_bank()
                p2 = b.psum[bk2]
                t2_ = b.emit("te", lambda h, p2=p2, cur=cur, curT=curT:
                             h.matmul(p2[0:32, 0:32], cur, curT,
                                      start=True, stop=True),
                             deps=war2, inc=1)
                tn2 = b.emit("ve", lambda h, p1=p1: h.tensor_copy(
                    N2, p1[0:32, 0:32]), deps=(t1,))
                tnt2 = b.emit("ve", lambda h, p2=p2: h.tensor_copy(
                    NT2, p2[0:32, 0:32]), deps=(t2_,), inc=1)
                b.bank_done(bk, tnt2)
                b.bank_done(bk2, tnt2)
                bk, war = b.take_bank()
                p1 = b.psum[bk]
                t1 = b.emit("te", lambda h, p1=p1: h.matmul(
                    p1[0:32, 0:32], NT2, R, start=True, stop=True),
                    deps=war + (tnt2,), inc=1)
                bk2, war2 = b.take_bank()
                p2 = b.psum[bk2]
                t2_ = b.emit("te", lambda h, p2=p2: h.matmul(
                    p2[0:32, 0:32], R, NT2, start=True, stop=True),
                    deps=war2, inc=1)
                b.emit("ve", lambda h, p1=p1: h.tensor_tensor(
                    R, R, p1[0:32, 0:32], ALU.add), deps=(t1, t2_))
                tr_ = b.emit("ve", lambda h, p2=p2: h.tensor_tensor(
                    RT, RT, p2[0:32, 0:32], ALU.add), inc=1)
                b.bank_done(bk, tr_)
                b.bank_done(bk2, tr_)
                cur, curT = N2, NT2
            bk, war = b.take_bank()
            pw = b.psum[bk]
            tw1 = b.emit("te", lambda h, pw=pw: h.matmul(
                pw[:, 0:32], KBTM[0:32, 0:128], RT, start=True, stop=True),
                deps=war + (tr_, ttr), inc=1)
            tw2 = b.emit("te", lambda h, pw=pw: h.matmul(
                pw[:, 32:64], KBTM[0:32, 128:256], RT, start=True,
                stop=True, skip_group_check=True), deps=(), inc=1)
            twf = b.emit("ve", lambda h, pw=pw: h.tensor_scalar(
                WFM, pw[:, 0:64], -1.0, None, ALU.mult),
                deps=(tw1, tw2), inc=1)
            b.bank_done(bk, twf)
            bk, war = b.take_bank()
            pu = b.psum[bk]
            tmm = b.emit("te", lambda h, pu=pu: h.matmul(
                pu[0:32, 0:256], RT, VTM, start=True, stop=False),
                deps=war + (twf,), inc=1)
            for sub in range(2):
                tmm = b.emit("te", lambda h, pu=pu, sub=sub: h.matmul(
                    pu[0:32, 0:256], WFM[:, sub * 32:(sub + 1) * 32],
                    SS[:, sub * 256:(sub + 1) * 256],
                    start=False, stop=(sub == 1)), deps=(), inc=1)
            tun = b.emit("ve", lambda h, pu=pu: h.tensor_copy(
                UNEW, pu[0:32, 0:256]), deps=(tmm,), inc=1)
            b.bank_done(bk, tun)
            bk, war = b.take_bank()
            pq = b.psum[bk]
            for sub in range(2):
                tmm = b.emit("te", lambda h, pq=pq, sub=sub, c0=c0:
                             h.matmul(pq[0:32, 0:32],
                                      KH[:, sub * T + c0:sub * T + c0 + C],
                                      QH[:, sub * T + c0:sub * T + c0 + C],
                                      start=(sub == 0), stop=(sub == 1)),
                             deps=war, inc=1)
            taq = b.emit("ve", lambda h, pq=pq: h.tensor_tensor(
                N2, pq[0:32, 0:32], TRI[0:32, 64:96], ALU.mult),
                deps=(tmm,))
            taq = b.emit("ve", lambda h, pq=pq: h.tensor_tensor(
                NT2, pq[0:32, 0:32], EYE32, ALU.mult))
            taq = b.emit("ve", lambda h: h.tensor_tensor(
                N2, N2, NT2, ALU.add), inc=1)
            b.bank_done(bk, taq)
            bk, war = b.take_bank()
            po = b.psum[bk]
            tmm = None
            for sub in range(2):
                tmm = b.emit("te", lambda h, po=po, sub=sub, c0=c0:
                             h.matmul(po[0:32, 0:256],
                                      QH[:, sub * T + c0:sub * T + c0 + C],
                                      SS[:, sub * 256:(sub + 1) * 256],
                                      start=(sub == 0), stop=False),
                             deps=war + ((tun, taq) if sub == 0 else ()),
                             inc=1)
            tmm = b.emit("te", lambda h, po=po: h.matmul(
                po[0:32, 0:256], N2, UNEW, start=False, stop=True),
                deps=(), inc=1)
            tot = b.emit("ve", lambda h, po=po: h.tensor_copy(
                OTM, po[0:32, 0:256]), deps=(tmm,), inc=1)
            b.bank_done(bk, tot)
            tdo = b.dma("po", o_hbm[c0:c0 + C, hd * 256:hd * 256 + 256],
                        OTM, deps=(tot,), stream="l0o")
            tsu = None
            for sub in range(2):
                bk, war = b.take_bank()
                ps = b.psum[bk]
                tmm = b.emit("te", lambda h, ps=ps, sub=sub: h.matmul(
                    ps[:, 0:256], KTM[0:32, sub * 128:(sub + 1) * 128],
                    UNEW, start=True, stop=True),
                    deps=war + (tun, tot), inc=1)
                tsu = b.emit("ve", lambda h, ps=ps, sub=sub: h.tensor_tensor(
                    SS[:, sub * 256:(sub + 1) * 256],
                    SS[:, sub * 256:(sub + 1) * 256], ps[:, 0:256],
                    ALU.add), deps=(tmm,), inc=1)
                b.bank_done(bk, tsu)
            tlast = (tsu, tdo)
        tdone = tsu

    trl = tdone
    for ft in range(NDT):
        for tt16 in range(16):
            r0 = tt16 * 128
            tld = b.dma("sy", QH[:, 0:128],
                        o_hbm[r0:r0 + 128, ft * 128:(ft + 1) * 128],
                        deps=(trl,), stream="l0cv")
            bk, war = b.take_bank()
            pt = b.psum[bk]
            ttp = b.emit("te", lambda h, pt=pt: h.transpose(
                pt[:, 0:128], QH[:, 0:128], cn["ident"]),
                deps=war + (tld,), inc=1)
            trl = b.emit("ve", lambda h, pt=pt, ft=ft, r0=r0: h.tensor_copy(
                BIG[:, ft * T + r0:ft * T + r0 + 128], pt[:, 0:128]),
                deps=(ttp,), inc=1)
            b.bank_done(bk, trl)
    t_ofm = trl

    tmixdone = t_ofm
    for hd in range(4):
        tbr = b.dma("sy", BR, bm_hbm[4 + hd:5 + hd, :], deps=(tmixdone,),
                    stream="l0br")
        tbb = None
        for tc in range(NTC):
            t0 = tc * TSC
            bk, war = b.take_bank()
            pb = b.psum[bk]
            tb_ = b.emit("te", lambda h, pb=pb, t0=t0: h.matmul(
                pb[:, 0:TSC], cn["ones_row"], BR[0:1, t0:t0 + TSC],
                start=True, stop=True), deps=war + (tbr,), inc=1)
            tbb = b.emit("ve", lambda h, pb=pb, t0=t0: h.tensor_copy(
                BB[:, t0:t0 + TSC], pb[:, 0:TSC]), deps=(tb_,), inc=1)
            b.bank_done(bk, tbb)
        for sub in range(2):
            dt = 2 * hd + sub
            tld = b.dma("sy", QH[:, 0:T], v_hbm[dt * 128:(dt + 1) * 128, :],
                        deps=(tmixdone,), stream="l0cv")
            sl = BIG[:, dt * T:(dt + 1) * T]
            b.emit("ve", lambda h, sl=sl: h.tensor_tensor(
                sl, sl, QH[:, 0:T], ALU.subtract), deps=(tld, tbb))
            b.emit("ve", lambda h, sl=sl: h.tensor_tensor(
                sl, sl, BB[:, 0:T], ALU.mult))
            tmixdone = b.emit("ve", lambda h, sl=sl: h.tensor_tensor(
                sl, sl, QH[:, 0:T], ALU.add), inc=1)
        for tc in range(NTC):
            t0 = tc * TSC
            bk1, war1 = b.take_bank()
            p1 = b.psum[bk1]
            tqs = None
            for sub in range(2):
                dt = 2 * hd + sub
                tq_ = b.emit("se", lambda h, sub=sub, dt=dt, t0=t0:
                             h.activation(
                                 st["SQ"][:, sub * TSC:(sub + 1) * TSC],
                                 BIG[:, dt * T + t0:dt * T + t0 + TSC],
                                 AF.Square),
                             deps=(tmixdone, st["sq_war"][sub]) if sub == 0
                             else (st["sq_war"][sub],), inc=1)
                tqs = se_flush(st, deps=(tq_,))
            for sub in range(2):
                tmm = b.emit("te", lambda h, sub=sub, p1=p1: h.matmul(
                    p1[:, 0:TSC], cn["ones128"],
                    st["SQ"][:, sub * TSC:(sub + 1) * TSC],
                    start=(sub == 0), stop=(sub == 1)),
                    deps=(tqs,) + (war1 if sub == 0 else ()), inc=1)
            varb = st["SQ"][:, TSC:2 * TSC]
            tv = b.emit("ve", lambda h, p1=p1: h.tensor_scalar(
                varb, p1[:, 0:TSC], 1.0 / 256.0, 1e-05, ALU.mult, ALU.add),
                deps=(tmm,))
            trc = b.emit("ve", lambda h: h.reciprocal(varb, varb), inc=1)
            tsq = b.emit("se", lambda h: h.activation(
                st["SQ"][:, 0:TSC], varb, AF.Sqrt), deps=(trc,), inc=1)
            tsf = se_flush(st, deps=(tsq,))
            st["sq_war"][1] = tsf
            for sub in range(2):
                dt = 2 * hd + sub
                b.emit("ve", lambda h, dt=dt, t0=t0: h.tensor_tensor(
                    BIG[:, dt * T + t0:dt * T + t0 + TSC],
                    BIG[:, dt * T + t0:dt * T + t0 + TSC],
                    st["SQ"][:, 0:TSC], ALU.mult), deps=(tsf,))
                tmixdone = b.emit("ve", lambda h, dt=dt, t0=t0:
                                  h.tensor_scalar(
                                      BIG[:, dt * T + t0:dt * T + t0 + TSC],
                                      BIG[:, dt * T + t0:dt * T + t0 + TSC],
                                      NW[:, dt:dt + 1], None, ALU.mult),
                                  deps=(tg,), inc=1)
            st["sq_war"][0] = tmixdone
            b.bank_done(bk1, tmixdone)

    out_tok = None
    wpt2 = {}
    for tc in range(NTC):
        t0 = tc * TSC
        for ot in range(NDT):
            par = ot % 2
            wsl = (QH if par == 0 else KH)[:, 0:1024]
            lw = b.dma("sy", wsl.rearrange("p (kt m) -> p kt m", kt=NDT),
                       hbm["L0_WoT"][:, ot * 128:(ot + 1) * 128].rearrange(
                           "(kt p) m -> p kt m", p=128),
                       deps=(wpt2.get(par), tmixdone), stream=f"l0w{par}")
            bk, war = b.take_bank()
            pa = b.psum[bk]
            tmm = None
            for kt in range(NDT):
                d = war + ((lw, tmixdone) if kt == 0 else ())
                tmm = b.emit("te", lambda h, kt=kt, pa=pa, wsl=wsl, t0=t0:
                             h.matmul(pa[:, 0:TSC],
                                      wsl[:, kt * 128:(kt + 1) * 128],
                                      BIG[:, kt * T + t0:kt * T + t0 + TSC],
                                      start=(kt == 0), stop=(kt == NDT - 1)),
                             deps=d, inc=1)
            wpt2[par] = tmm
            tcp = b.emit("ve", lambda h, pa=pa, ot=ot: h.tensor_copy(
                VH[:, ot * TSC:(ot + 1) * TSC], pa[:, 0:TSC]),
                deps=(tmm,), inc=1)
            b.bank_done(bk, tcp)
        out_tok = postln_residual_chunk(
            st, tc, lambda ot: VH[:, ot * TSC:(ot + 1) * TSC], (tcp,))
    return out_tok



def build_program(wshapes, stages):
    nc = bass.Bass("TRN2", target_bir_lowering=False)
    es = ExitStack()
    es.__enter__()
    b = B(nc, es)

    hbm = {}
    for name, (shape, dt) in wshapes.items():
        hbm[name] = nc.dram_tensor(name, list(shape), dt,
                                   kind="ExternalInput")
    out_x = nc.dram_tensor("out_x", [D, T], F32, kind="ExternalOutput")

    block = es.enter_context(nc.Block())

    XT = b.sb("XT", [128, NDT * T])
    BIG = b.sb("BIG", [128, NDT * T])
    GEN = b.sb("GEN", [128, GENW])
    CONSTS = b.sb("CONSTS", [128, 512])
    RR = b.sb("RR", [1, 2176])
    SEDUMMY = RR[0:1, 0:64]
    VEDUMMY = RR[0:1, 64:128]

    cn = {
        "ident": CONSTS[:, 0:128],
        "ones128": CONSTS[:, 128:256],
        "ones_col": CONSTS[:, 128:129],
        "chi64": CONSTS[:, 256:257],
        "chi32": CONSTS[:, 320:321],
        "ones_row": CONSTS[0:1, 128:256],
    }
    chi64w = CONSTS[:, 256:320]
    chi32w = CONSTS[:, 320:384]

    tok_x = b.dma("sy", XT[:, :].rearrange("p (dt t) -> p dt t", dt=NDT),
                  hbm["x_fm"][:, :].rearrange("(dt p) t -> p dt t", p=128),
                  stream="xload")
    tok_c = b.dma("sy", CONSTS[:, 0:512], hbm["consts"][:, :],
                  stream="consts")
    b.emit("ve", lambda h: h.tensor_scalar(chi64w, chi64w, C2, C2,
                                           ALU.add, ALU.subtract),
           deps=(tok_c,))
    tok_chi = b.emit("ve", lambda h: h.tensor_scalar(
        chi32w, chi64w, -1.0, 1.0, ALU.mult, ALU.add), inc=1)

    st = {"b": b, "nc": nc, "hbm": hbm, "cn": cn, "CONSTS_T": CONSTS,
          "XT": XT, "BIG": BIG, "GEN": GEN,
          "SEDUMMY": SEDUMMY, "VEDUMMY": VEDUMMY, "RRT": RR,
          "MU": RR[0:1, TSC:2 * TSC], "RS": RR[0:1, 2 * TSC:3 * TSC],
          "ROWSCR": RR[0:1, 3 * TSC:3 * TSC + 256],
          "ROWSCR2": RR[0:1, 3 * TSC + 256:3 * TSC + 512],
          "SQ": GEN[:, GENW - 2 * TSC:GENW],
          "sq_war": [None, None], "rows_war": None, "rs_war": None,
          "mu_war": None}

    tok_last = (tok_x, tok_chi)

    for stg in stages:
        if stg == "stats_dbg":
            st["dbg_sq2"] = GEN[:, 0:TSC]
            st["dbg_var"] = GEN[:, TSC:2 * TSC]
            st["dbg_musq"] = GEN[:, 2 * TSC:3 * TSC]
            tk = None
            for tc in range(NTC):
                t0 = tc * TSC
                bk_mu, rstd_ap, tstat = ln_stats_chunk(
                    st, lambda dt, t0=t0: XT[:, dt * T + t0:dt * T + t0 + TSC],
                    deps=tok_last if tc == 0 else ())
                pm = b.psum[bk_mu]
                for dt in range(NDT):
                    if dt % 2 == 0:
                        tk = b.emit("ve", lambda h, dt=dt, t0=t0, pm=pm:
                                    h.tensor_copy(
                                        st["BIG"][:, dt * T + t0:
                                                  dt * T + t0 + TSC],
                                        pm[:, 0:TSC]),
                                    deps=(tstat,) if dt == 0 else (), inc=1)
                    elif dt == 5:
                        tk = b.emit("ve", lambda h, dt=dt, t0=t0:
                                    h.tensor_copy(
                                        st["BIG"][:, dt * T + t0:
                                                  dt * T + t0 + TSC],
                                        st["dbg_var"]),
                                    deps=(), inc=1)
                    elif dt == 7:
                        tk = b.emit("ve", lambda h, dt=dt, t0=t0:
                                    h.tensor_copy(
                                        st["BIG"][:, dt * T + t0:
                                                  dt * T + t0 + TSC],
                                        st["dbg_musq"]),
                                    deps=(), inc=1)
                    elif dt == 3:
                        tk = b.emit("ve", lambda h, dt=dt, t0=t0:
                                    h.tensor_copy(
                                        st["BIG"][:, dt * T + t0:
                                                  dt * T + t0 + TSC],
                                        st["dbg_sq2"]),
                                    deps=(), inc=1)
                    else:
                        tk = b.emit("ve", lambda h, dt=dt, t0=t0,
                                    rstd_ap=rstd_ap:
                                    h.tensor_copy(
                                        st["BIG"][:, dt * T + t0:
                                                  dt * T + t0 + TSC],
                                        rstd_ap),
                                    deps=(), inc=1)
                b.bank_done(bk_mu, tk)
                st["sq_war"][1] = tk
            cp = None
            for dt in range(NDT):
                cp = b.emit("ve", lambda h, dt=dt: h.tensor_copy(
                    XT[:, dt * T:(dt + 1) * T],
                    st["BIG"][:, dt * T:(dt + 1) * T]),
                    deps=(tk,) if dt == 0 else (), inc=1)
            tok_last = (cp,)
        elif stg == "preln_only":
            tk = None
            for tc in range(NTC):
                tk = preln_chunk(
                    st, tc,
                    lambda dt, tc=tc: st["BIG"][:, dt * T + tc * TSC:
                                                dt * T + tc * TSC + TSC],
                    deps=tok_last if tc == 0 else ())
            cp = None
            for dt in range(NDT):
                cp = b.emit("ve", lambda h, dt=dt: h.tensor_copy(
                    XT[:, dt * T:(dt + 1) * T],
                    st["BIG"][:, dt * T:(dt + 1) * T]),
                    deps=(tk,) if dt == 0 else (), inc=1)
            tok_last = (cp,)
        elif stg == "l0ffn":
            tok_last = (sandwich_pwl_ffn(st, "L0_ffn", tok_last),)
        elif stg == "l1ffn":
            tok_last = (sandwich_pwl_ffn(st, "L1_ffn", tok_last),)
        elif stg == "l1attn":
            tok_last = (emit_l1_attn(st, tok_last),)
        elif stg == "l0attn":
            tok_last = (emit_l0_attn(st, tok_last),)
        elif stg == "l2attn":
            tok_last = (emit_l2_attn(st, tok_last),)
        elif stg == "l3attn":
            tok_last = (emit_l3_attn(st, tok_last),)
        elif stg == "l2ffn":
            tok_last = (emit_l2_ffn(st, tok_last),)
        elif stg == "l3ffn":
            tok_last = (emit_l3_ffn(st, tok_last),)
        else:
            raise ValueError(stg)

    stok = b.dma("po", out_x[:, :].rearrange("(dt p) t -> p dt t", p=128),
                 XT[:, :].rearrange("p (dt t) -> p dt t", dt=NDT),
                 deps=tok_last, stream="out")
    b.emit("po", lambda h: h.engine_nop(), deps=(stok,))

    b.finalize(block)
    return nc



def _consts_np():
    c = np.zeros((128, 512), np.float32)
    c[:, 0:128] = np.eye(128, dtype=np.float32)
    c[:, 128:256] = 1.0
    c[:, 256:320] = 0.75
    assert len(_CONST_VALS) <= 33
    for i, v in enumerate(_CONST_VALS):
        c[:, CONST_BASE + i] = v
    c[:, 480:512] = np.float32(1.0 / 7.5)
    return c


def _default_stages():
    return ["l0attn", "l0ffn", "l1attn", "l1ffn",
            "l2attn", "l2ffn", "l3attn", "l3ffn"]


def kernel(**inputs):
    global _CONST_VALS
    _CONST_VALS = []
    stages = _STAGES if _STAGES is not None else _default_stages()
    x = np.ascontiguousarray(inputs["L0_attn_hidden_states"])

    wshapes = {"x_fm": ((D, T), F32), "consts": ((128, 512), F32)}
    shared = {}

    def addw(name, arr):
        arr = np.ascontiguousarray(np.asarray(arr, np.float32))
        wshapes[name] = (arr.shape, F32)
        shared[name] = arr

    if "l0ffn" in stages:
        for sfx in ("w1", "b1", "w2", "b2"):
            addw(f"L0_ffn_{sfx}", inputs[f"L0_ffn_{sfx}"])
    if "l1ffn" in stages:
        for sfx in ("w1", "b1", "w2", "b2"):
            addw(f"L1_ffn_{sfx}", inputs[f"L1_ffn_{sfx}"])
    if "l0attn" in stages:
        for nm in ("Wq", "Wk", "Wv"):
            addw(f"L0_{nm}T",
                 np.asarray(inputs[f"L0_attn_{nm}"], np.float32).T)
        addw("L0_WoT", np.asarray(inputs["L0_attn_Wo"], np.float32).T)
        wbm = np.zeros((1024, 128), np.float32)
        wbm[:, 0:4] = np.asarray(inputs["L0_attn_Wb"], np.float32).T
        wbm[:, 4:8] = np.asarray(inputs["L0_attn_Wmix"], np.float32).T
        addw("L0_WbmT", wbm)
        cv = np.zeros((128, 96), np.float32)
        for wi, nm in enumerate(("conv_q", "conv_k", "conv_v")):
            cw = np.asarray(inputs[f"L0_attn_{nm}"],
                            np.float32).reshape(8, 128, 4)
            cv[:, 32 * wi:32 * wi + 32] = cw.transpose(1, 0, 2).reshape(
                128, 32)
        addw("L0_convT", cv)
        tri = np.zeros((32, 96), np.float32)
        tri[:, 0:32] = np.tril(np.ones((32, 32)), -1)
        tri[:, 32:64] = np.tril(np.ones((32, 32)), 0)
        tri[:, 64:96] = np.triu(np.ones((32, 32)), 1)
        addw("L0_TRIL", tri)
        nw = np.asarray(inputs["L0_attn_norm_w"], np.float32)
        nwt = np.zeros((128, 8), np.float32)
        for dt in range(8):
            nwt[:, dt] = nw[(dt % 2) * 128:(dt % 2) * 128 + 128]
        addw("L0_normwT", nwt)
    if "l2attn" in stages:
        for nm, key in (("WinT", "W_in"), ("WlamT", "W_lam"),
                        ("WgateT", "W_gate"), ("WoutT", "W_out")):
            addw(f"L2A_{nm}",
                 np.asarray(inputs[f"L2_attn_{key}"], np.float32).T)
        for nm, key in (("binT", "b_in"), ("blamT", "b_lam"),
                        ("bgateT", "b_gate"), ("boutT", "b_out"),
                        ("gamT", "gamma"), ("betT", "beta")):
            addw(f"L2A_{nm}",
                 np.asarray(inputs[f"L2_attn_{key}"],
                            np.float32).reshape(8, 128).T)
        wshapes["bsel"] = ((128, 8), F32)
    if "l3attn" in stages:
        addw("L3_attn_W_in", inputs["L3_attn_W_in"])
        wxpP = np.zeros((2048, 256), np.float32)
        wxpP[:, 0:192] = np.asarray(inputs["L3_attn_W_xproj"], np.float32)
        addw("L3A_WXP", wxpP)
        wdtP = np.zeros((128, 2048), np.float32)
        wdtP[0:64, :] = np.asarray(inputs["L3_attn_W_dt"], np.float32)
        addw("L3A_WDTP", wdtP)
        addw("L3_attn_W_out", inputs["L3_attn_W_out"])
        addw("L3A_ALOG",
             np.asarray(inputs["L3_attn_A_log"],
                        np.float32).reshape(16, 128, 64).transpose(
                            1, 0, 2).reshape(128, 1024))
        addw("L3A_bdt", np.asarray(inputs["L3_attn_b_dt"],
                                   np.float32).reshape(16, 128).T)
        addw("L3A_D", np.asarray(inputs["L3_attn_D"],
                                 np.float32).reshape(16, 128).T)
        addw("L3A_convT",
             np.asarray(inputs["L3_attn_conv_w"],
                        np.float32).reshape(16, 128, 4).transpose(
                            1, 0, 2).reshape(128, 64))
        addw("L3A_convb", np.asarray(inputs["L3_attn_conv_b"],
                                     np.float32).reshape(16, 128).T)
        addw("L3A_lnw", np.asarray(inputs["L3_attn_ln_w"],
                                   np.float32).reshape(8, 128).T)
        addw("L3A_lnb", np.asarray(inputs["L3_attn_ln_b"],
                                   np.float32).reshape(8, 128).T)
    if "l2ffn" in stages:
        addw("L2_ffn_qT",
             np.transpose(np.asarray(inputs["L2_ffn_qweights"],
                                     np.float32), (0, 2, 1)))
        scr = np.repeat(np.asarray(inputs["L2_ffn_scales"], np.float32),
                        16, axis=2)
        addw("L2_ffn_scT", np.transpose(scr, (0, 2, 1)))
        addw("L2_ffn_biasT",
             np.asarray(inputs["L2_ffn_biases"],
                        np.float32).reshape(6, 8, 128).transpose(0, 2, 1))
        la = np.asarray(inputs["L2_ffn_lora_a"], np.float32)
        lap = np.zeros((6, 1024, 128), np.float32)
        lap[:, :, 0:32] = np.transpose(la, (0, 2, 1))
        addw("L2_ffn_laT", lap)
        lb = np.asarray(inputs["L2_ffn_lora_b"], np.float32)
        lbp = np.zeros((6, 128, 1024), np.float32)
        lbp[:, 0:32, :] = np.transpose(lb, (0, 2, 1))
        addw("L2_ffn_lbT", lbp)
        addw("L2_ffn_ln_gT",
             np.asarray(inputs["L2_ffn_ln_gamma"],
                        np.float32).reshape(3, 8, 128).transpose(0, 2, 1))
        addw("L2_ffn_ln_bT",
             np.asarray(inputs["L2_ffn_ln_beta"],
                        np.float32).reshape(3, 8, 128).transpose(0, 2, 1))
    if "l3ffn" in stages:
        addw("L3_ffn_weight", inputs["L3_ffn_weight"])
        addw("L3_ffn_biasT",
             np.asarray(inputs["L3_ffn_bias"],
                        np.float32).reshape(8, 128).T)
    if "l1attn" in stages:
        wxp_t = np.asarray(inputs["L1_attn_W_xproj"], np.float32).T
        wxp_pad = np.zeros((1024, 128), np.float32)
        wxp_pad[:, 0:96] = wxp_t
        addw("L1_attn_W_xprojT", wxp_pad)
        addw("L1_attn_W_dtT", inputs["L1_attn_W_dt"].T)
        addw("L1_attn_b_dt", inputs["L1_attn_b_dt"])
        addw("L1_attn_A_log", inputs["L1_attn_A_log"])
        addw("L1_attn_D_param", inputs["L1_attn_D_param"])

    with allow_ici_all_to_all():
        nc = build_program(wshapes, stages)
        shared["consts"] = _consts_np()
        in_maps = []
        for c in range(NCORES):
            m = {"x_fm": np.ascontiguousarray(x[c].T)}
            m.update(shared)
            if "bsel" in wshapes:
                bs = np.zeros((128, 8), np.float32)
                bs[:, c] = 1.0
                m["bsel"] = bs
            in_maps.append(m)
        res = run_bass_kernel_spmd(nc, in_maps, core_ids=list(range(NCORES)))

    out = np.stack([np.ascontiguousarray(res.results[c]["out_x"].T)
                    for c in range(NCORES)])
    return out


# version 09 (final)
# speedup vs baseline: 1.0537x; 1.0537x over previous
import numpy as np
from contextlib import ExitStack

import concourse.bass as bass
import concourse.mybir as mybir
from concourse.bass_utils import run_bass_kernel_spmd
from concourse.aot_env import allow_ici_all_to_all

F32 = mybir.dt.float32
I32 = mybir.dt.int32
AF = mybir.ActivationFunctionType
ALU = mybir.AluOpType
AX = mybir.AxisListType

T = 2048
D = 1024
NDT = D // 128
NCORES = 8
TSC = 512
NTC = T // TSC
GENW = 17664

C1 = 1.5 * 2.0 ** 23
C2 = 1.5 * 2.0 ** 52

_STAGES = None

CONST_BASE = 384
_CONST_VALS = []

ENGS = ("sy", "po", "ve", "se", "te")


class B:
    def __init__(self, nc, es):
        self.nc = nc
        self.es = es
        self.prog = {e: [] for e in ENGS}
        self.sem = {}
        for name in ("s_sy", "s_po", "s_ve", "s_ve2", "s_ve3",
                     "s_se", "s_te", "s_te2", "s_cc"):
            self.sem[name] = es.enter_context(nc.semaphore(name))
        self.cnt = {k: 0 for k in self.sem}
        self.engsem = {"sy": "s_sy", "po": "s_po", "ve": "s_ve",
                       "se": "s_se", "te": "s_te"}
        self.streams = {}
        self.psum = [es.enter_context(nc.psum_tensor(f"PS{i}", [128, 512],
                                                     F32))
                     for i in range(8)]
        self.psum_tok = [None] * 8
        self.psum_rr = 0
        self.names = set()
        self.ve_prev = None
        self.ve_flip = 0
        self.te_flip = 0

    def emit(self, eng, fn, deps=(), inc=0, sem=None, hard=False):
        if eng == "ve":
            prev = self.ve_prev
            if inc == 0 and sem is None:
                inc = 1
            if sem is None:
                sem = ("s_ve", "s_ve2", "s_ve3")[self.ve_flip % 3]
                self.ve_flip += 1
            if prev is not None:
                deps = tuple(deps) + (prev,)
            hard = True
        elif eng == "te" and inc and sem is None:
            sem = ("s_te", "s_te2")[self.te_flip % 2]
            self.te_flip += 1
        semname = sem or self.engsem[eng]
        tok = None
        if inc:
            self.cnt[semname] += inc
            tok = (semname, self.cnt[semname])
        if eng == "ve" and semname.startswith("s_ve"):
            self.ve_prev = tok
        self.prog[eng].append((tuple(d for d in deps if d is not None),
                               fn, semname if inc else None, inc, hard))
        return tok

    def stream_sem(self, name):
        if name not in self.streams:
            semname = "sd_" + name
            self.sem[semname] = self.es.enter_context(
                self.nc.semaphore(semname))
            self.cnt[semname] = 0
            self.streams[name] = semname
        return self.streams[name]

    def dma(self, eng, out, in_, deps=(), slow=False, stream=None):
        semname = self.stream_sem(stream if stream else "q_" + eng)
        if slow:
            return self.emit(eng, lambda h: h.dma_start(
                out=out, in_=in_, allow_slow_non_contiguous=True),
                deps=deps, inc=16, sem=semname)
        return self.emit(eng, lambda h: h.dma_start(out=out, in_=in_),
                         deps=deps, inc=16, sem=semname)

    def take_bank(self):
        i = self.psum_rr % 8
        self.psum_rr += 1
        dep = self.psum_tok[i]
        return i, ((dep,) if dep else ())

    def bank_done(self, i, tok):
        self.psum_tok[i] = tok

    def sb(self, name, shape, dtype=F32):
        assert name not in self.names, name
        self.names.add(name)
        return self.es.enter_context(self.nc.sbuf_tensor(name, shape, dtype))

    def finalize(self, block):
        def run_list(h, eng):
            waited = {}
            own = {"ve": ("s_ve", "s_ve2", "s_ve3"),
                   "te": ("s_te", "s_te2")}.get(eng, (self.engsem[eng],))
            for deps, fn, semname, inc, hard in self.prog[eng]:
                for (sname, val) in deps:
                    if sname in own and not hard:
                        continue
                    if waited.get(sname, -1) >= val:
                        continue
                    h.wait_ge(self.sem[sname], val)
                    waited[sname] = val
                inst = fn(h)
                if semname is not None:
                    inst.then_inc(self.sem[semname], inc)

        @block.sync
        def _(h):
            run_list(h, "sy")

        @block.gpsimd
        def _(h):
            run_list(h, "po")

        @block.vector
        def _(h):
            run_list(h, "ve")

        @block.scalar
        def _(h):
            run_list(h, "se")

        @block.tensor
        def _(h):
            run_list(h, "te")



def ccol(st, val):
    val = float(val)
    if val in (0.0, 1.0):
        return val
    if val not in _CONST_VALS:
        _CONST_VALS.append(val)
        i = CONST_BASE + _CONST_VALS.index(val)
        st["b"].nc.const_aps.register(F32, val,
                                      st["CONSTS_T"][:, i:i + 1])
    return val


def ve_flush(st, deps=(), hard=False):
    b = st["b"]
    return b.emit("ve", lambda h: h.tensor_scalar(
        st["VEDUMMY"], st["cn"]["ones_row"][0:1, 0:64], 1.0, None,
        ALU.mult), deps=deps, inc=1, hard=hard)


def se_flush(st, deps=()):
    b = st["b"]
    return b.emit("se", lambda h: h.activation(
        st["SEDUMMY"], st["cn"]["ones_row"][0:1, 0:64], AF.Copy,
        bias=0.0, scale=1.0), deps=deps, inc=1)


def ccol_ap(st, val):
    val = float(val)
    if val not in _CONST_VALS:
        _CONST_VALS.append(val)
        i = CONST_BASE + _CONST_VALS.index(val)
        st["b"].nc.const_aps.register(F32, val,
                                      st["CONSTS_T"][:, i:i + 1])
    i = CONST_BASE + _CONST_VALS.index(val)
    return st["CONSTS_T"][:, i:i + 1]


def rne(st, out_ap, src_ap, u1, u2, deps=(), post_scale=None):
    b = st["b"]
    cn = st["cn"]
    b.emit("ve", lambda h: h.tensor_scalar(u2, src_ap, C2, C2, ALU.add,
                                           ALU.subtract), deps=deps)
    tchi = b.emit("ve", lambda h: h.tensor_scalar(u2, u2, cn["chi64"], None,
                                                  ALU.mult), inc=1)
    b.emit("ve", lambda h: h.tensor_scalar(u1, src_ap, C1, C1, ALU.add,
                                           ALU.subtract))
    if post_scale is None:
        return b.emit("ve", lambda h: h.scalar_tensor_tensor(
            out_ap, u1, cn["chi32"], u2, ALU.mult, ALU.add),
            deps=(tchi,), hard=True, inc=1)
    tst = b.emit("ve", lambda h: h.scalar_tensor_tensor(
        out_ap, u1, cn["chi32"], u2, ALU.mult, ALU.add),
        deps=(tchi,), hard=True, inc=1)
    return b.emit("ve", lambda h: h.tensor_scalar(
        out_ap, out_ap, float(post_scale), None, ALU.mult),
        deps=(tst,), hard=True, inc=1)


def ln_stats_chunk(st, get_tile, deps=(), eps=1e-5):
    b = st["b"]
    cn = st["cn"]
    SQ = st["SQ"]
    bk1, war1 = b.take_bank()
    bk2, war2 = b.take_bank()
    p1, p2 = b.psum[bk1], b.psum[bk2]
    tsums = None
    for dt in range(NDT):
        d = tuple(deps) + war1 if dt == 0 else ()
        tsums = b.emit("te", lambda h, dt=dt, p1=p1: h.matmul(
            p1[:, 0:TSC], cn["ones128"], get_tile(dt),
            start=(dt == 0), stop=(dt == NDT - 1)), deps=d,
            inc=1 if dt == NDT - 1 else 0)
    def sq_ap(dt):
        return SQ[:, (dt % 2) * TSC:(dt % 2 + 1) * TSC]

    def emit_sq(dt, extra=()):
        dd = tuple(extra)
        if st["sq_war"][dt % 2]:
            dd = dd + (st["sq_war"][dt % 2],)
        return b.emit("se", lambda h, dt=dt: h.activation(
            sq_ap(dt), get_tile(dt), AF.Square), deps=dd, inc=1)

    tsq = [None] * (NDT + 1)
    tsq[0] = emit_sq(0, extra=deps)
    tsq[1] = emit_sq(1, extra=deps)
    tsq2 = None
    for dt in range(NDT):
        gate = tsq[dt + 1] if dt < NDT - 1 else tsq[NDT]
        d2 = (gate,) + (war2 if dt == 0 else ())
        tsq2 = b.emit("te", lambda h, p2=p2, dt=dt: h.matmul(
            p2[:, 0:TSC], cn["ones128"], sq_ap(dt),
            start=(dt == 0), stop=(dt == NDT - 1)), deps=d2, inc=1)
        st["sq_war"][dt % 2] = tsq2
        if dt + 2 < NDT:
            tsq[dt + 2] = emit_sq(dt + 2)
        elif dt + 2 == NDT:
            tsq[NDT] = se_flush(st)
    tmp = SQ[:, 0:TSC]
    varb = SQ[:, TSC:2 * TSC]
    if st.get("dbg_sq2") is not None:
        b.emit("ve", lambda h, p2=p2: h.tensor_copy(
            st["dbg_sq2"], p2[:, 0:TSC]), deps=(tsq2,))
    b.emit("ve", lambda h, p1=p1: h.tensor_scalar(
        p1[:, 0:TSC], p1[:, 0:TSC], 1.0 / D, None, ALU.mult),
        deps=(tsums,), inc=1)
    tdiv = ve_flush(st)
    tvar = b.emit("ve", lambda h, p2=p2, varb=varb: h.tensor_scalar(
        varb, p2[:, 0:TSC], 1.0 / D, float(eps), ALU.mult, ALU.add),
        deps=(tsq2, st["sq_war"][1]), inc=1)
    b.bank_done(bk2, tvar)
    b.emit("se", lambda h, p1=p1, tmp=tmp: h.activation(
        tmp, p1[:, 0:TSC], AF.Square),
        deps=(st["sq_war"][0], tdiv), inc=1)
    tmusq = se_flush(st)
    if st.get("dbg_musq") is not None:
        b.emit("ve", lambda h, tmp=tmp: h.tensor_copy(
            st["dbg_musq"], tmp), deps=(tmusq,))
    b.emit("ve", lambda h, varb=varb, tmp=tmp: h.tensor_tensor(
        varb, varb, tmp, ALU.subtract), deps=(tmusq,))
    if st.get("dbg_var") is not None:
        b.emit("ve", lambda h, varb=varb: h.tensor_copy(
            st["dbg_var"], varb))
    trcp = b.emit("ve", lambda h, varb=varb: h.reciprocal(varb, varb),
                  inc=1)
    b.emit("se", lambda h, varb=varb, tmp=tmp: h.activation(
        tmp, varb, AF.Sqrt), deps=(trcp,), inc=1)
    tok = se_flush(st)
    st["sq_war"][1] = tok
    return bk1, tmp, tok


def bcast_row(st, row_ap, pbank, width, deps=()):
    b = st["b"]
    cn = st["cn"]
    assert width <= 512
    return b.emit("te", lambda h: h.matmul(
        pbank[:, 0:width], cn["ones_row"], row_ap, start=True, stop=True),
        deps=deps, inc=1)


def apply_ln_chunk(st, get_src, get_dst, bk_mu, rstd_ap, deps=()):
    b = st["b"]
    pm = b.psum[bk_mu]
    tok = None
    for dt in range(NDT):
        dst = get_dst(dt)
        b.emit("ve", lambda h, dt=dt, dst=dst: h.tensor_tensor(
            dst, get_src(dt), pm[:, 0:TSC], ALU.subtract),
            deps=deps if dt == 0 else ())
        tok = b.emit("ve", lambda h, dst=dst, rstd_ap=rstd_ap:
                     h.tensor_tensor(dst, dst, rstd_ap, ALU.mult), inc=1)
    b.bank_done(bk_mu, tok)
    st["sq_war"][0] = tok
    return tok


def postln_residual_chunk(st, tc, get_tile, deps):
    b = st["b"]
    XT = st["XT"]
    t0 = tc * TSC
    bk_mu, rstd_ap, tstat = ln_stats_chunk(st, get_tile, deps=deps)
    pm = b.psum[bk_mu]
    tok = None
    for dt in range(NDT):
        src_ap = get_tile(dt)
        b.emit("ve", lambda h, src_ap=src_ap: h.tensor_tensor(
            src_ap, src_ap, pm[:, 0:TSC], ALU.subtract),
            deps=(tstat,) if dt == 0 else ())
        b.emit("ve", lambda h, src_ap=src_ap, rstd_ap=rstd_ap:
               h.tensor_tensor(src_ap, src_ap, rstd_ap, ALU.mult))
        tok = b.emit("ve", lambda h, dt=dt, src_ap=src_ap: h.tensor_tensor(
            XT[:, dt * T + t0:dt * T + t0 + TSC],
            XT[:, dt * T + t0:dt * T + t0 + TSC], src_ap, ALU.add), inc=1)
    b.bank_done(bk_mu, tok)
    st["sq_war"][0] = tok
    return tok


def preln_chunk(st, tc, get_dst, deps=()):
    XT = st["XT"]
    t0 = tc * TSC
    bk_mu, rstd_ap, tstat = ln_stats_chunk(
        st, lambda dt: XT[:, dt * T + t0:dt * T + t0 + TSC], deps=deps)
    return apply_ln_chunk(
        st, lambda dt: XT[:, dt * T + t0:dt * T + t0 + TSC],
        get_dst, bk_mu, rstd_ap, deps=(tstat,))



def emit_pwl_ffn(st, pfx, src_fn, dest_fn, stage_deps=()):
    b = st["b"]
    cn = st["cn"]
    hbm = st["hbm"]
    FF = 4096
    NOT = FF // 128
    GEN = st["GEN"]
    HAR = st["BIG"][:, 0:NOT * TSC]
    XHC = GEN[:, 0:NDT * TSC]
    OUTC = GEN[:, NDT * TSC:2 * NDT * TSC]
    W1B = GEN[:, 2 * NDT * TSC:2 * NDT * TSC + 2048]
    W2B = GEN[:, 2 * NDT * TSC + 2048:2 * NDT * TSC + 4096]
    SCR = GEN[:, 2 * NDT * TSC + 4096:2 * NDT * TSC + 4096 + 5 * TSC]
    BCOL = GEN[:, 2 * NDT * TSC + 4096 + 5 * TSC:
               2 * NDT * TSC + 4096 + 5 * TSC + 40]

    w1 = hbm[pfx + "_w1"]
    b1 = hbm[pfx + "_b1"]
    w2 = hbm[pfx + "_w2"]
    b2 = hbm[pfx + "_b2"]
    B1C = BCOL[:, 0:NOT]
    B2C = BCOL[:, NOT:NOT + NDT]

    tb1 = b.dma("sy", B1C, b1.ap().rearrange("(o p) -> p o", p=128),
                slow=True, stream="bias1", deps=stage_deps)
    tb2 = b.dma("sy", B2C, b2.ap().rearrange("(o p) -> p o", p=128),
                slow=True, stream="bias2", deps=stage_deps)

    w1t = w1.ap().rearrange("(kt p) o -> kt p o", p=128)
    w2t = w2.ap().rearrange("(kt p) o -> kt p o", p=128)

    wtokA = [None, None]
    wtokB = [None, None]
    har_tok = [None] * NOT
    arena_war = None
    out_tok = None
    xh_war = None
    for tc in range(NTC):
        xhc, txh = src_fn(tc, XHC, xh_war)
        for ot in range(NOT):
            par = ot % 2
            w1sl = W1B[:, par * NDT * 128:(par + 1) * NDT * 128]
            lw1 = b.dma("sy",
                        w1sl.rearrange("p (kt o) -> p kt o", kt=NDT),
                        w1t[:, :, ot * 128:(ot + 1) * 128].rearrange(
                            "kt p o -> p kt o"),
                        deps=(wtokA[par],) if wtokA[par] else stage_deps,
                        stream=f"w1_{par}")
            bk, war = b.take_bank()
            pa = b.psum[bk]
            tmm = None
            for kt in range(NDT):
                d = war + (lw1, txh) if kt == 0 else ()
                tmm = b.emit("te", lambda h, kt=kt, pa=pa, w1sl=w1sl, xhc=xhc:
                             h.matmul(pa[:, 0:TSC],
                                      w1sl[:, kt * 128:(kt + 1) * 128],
                                      xhc[:, kt * TSC:(kt + 1) * TSC],
                                      start=(kt == 0), stop=(kt == NDT - 1)),
                             deps=d, inc=1 if kt == NDT - 1 else 0)
            wtokA[par] = tmm
            z = SCR[:, 0:TSC]
            u1 = SCR[:, TSC:2 * TSC]
            u2 = SCR[:, 2 * TSC:3 * TSC]
            xq = SCR[:, 3 * TSC:4 * TSC]
            gl = SCR[:, 0:TSC]
            gh = SCR[:, 4 * TSC:5 * TSC]
            b.emit("ve", lambda h, pa=pa, ot=ot, z=z: h.tensor_scalar(
                z, pa[:, 0:TSC], B1C[:, ot:ot + 1], 256.0, ALU.add, ALU.mult),
                deps=(tmm, tb1))
            tz = b.emit("ve", lambda h: h.engine_nop(), inc=1)
            b.bank_done(bk, tz)
            rne(st, xq, z, u1, u2)
            b.emit("ve", lambda h, xq=xq, u1=u1: h.tensor_scalar(
                u1, xq, -1024.0, 1024.0, ALU.max, ALU.min))
            b.emit("ve", lambda h, u1=u1, u2=u2: h.tensor_scalar(
                u2, u1, 1.0 / 64.0, 15.5, ALU.mult, ALU.add))
            b.emit("ve", lambda h, u2=u2, gl=gl: h.tensor_scalar(
                gl, u2, C2, C2, ALU.add, ALU.subtract))
            b.emit("ve", lambda h, gl=gl: h.tensor_scalar(
                gl, gl, cn["chi64"], None, ALU.mult))
            b.emit("ve", lambda h, u2=u2, gh=gh: h.tensor_scalar(
                gh, u2, C1, C1, ALU.add, ALU.subtract))
            b.emit("ve", lambda h, gh=gh, gl=gl, u1=u1:
                   h.scalar_tensor_tensor(u1, gh, cn["chi32"], gl,
                                          ALU.mult, ALU.add))
            b.emit("ve", lambda h, u1=u1, u2=u2: h.scalar_tensor_tensor(
                u2, u1, -1.0, u2, ALU.mult, ALU.add))
            ti = b.emit("ve", lambda h: h.engine_nop(), inc=1)
            b.emit("se", lambda h, u1=u1, gl=gl: h.activation(
                gl, u1, AF.Gelu, bias=ccol(st, -4.0), scale=ccol(st, 0.25)),
                deps=(ti,))
            b.emit("se", lambda h, u1=u1, gh=gh: h.activation(
                gh, u1, AF.Gelu, bias=ccol(st, -3.75),
                scale=ccol(st, 0.25)), inc=1)
            tg = se_flush(st)
            b.emit("ve", lambda h, gh=gh, gl=gl: h.tensor_tensor(
                gh, gh, gl, ALU.subtract), deps=(tg,))
            b.emit("ve", lambda h, u2=u2, gh=gh: h.scalar_tensor_tensor(
                gh, u2, 0.5, gh, ALU.add, ALU.mult))
            b.emit("ve", lambda h, gl=gl, gh=gh: h.tensor_tensor(
                gl, gl, gh, ALU.add))
            b.emit("ve", lambda h, xq=xq, gh=gh: h.scalar_tensor_tensor(
                gh, xq, 1024.0, xq, ALU.is_gt, ALU.mult))
            b.emit("ve", lambda h, xq=xq, gl=gl: h.scalar_tensor_tensor(
                gl, xq, 1024.0, gl, ALU.is_le, ALU.mult))
            b.emit("ve", lambda h, xq=xq, gl=gl: h.scalar_tensor_tensor(
                gl, xq, -1024.0, gl, ALU.is_ge, ALU.mult))
            b.emit("ve", lambda h, u1=u1, gl=gl, gh=gh:
                   h.scalar_tensor_tensor(u1, gl, 256.0, gh, ALU.mult,
                                          ALU.add))
            hdst = HAR[:, ot * TSC:(ot + 1) * TSC]
            dw = (arena_war,) if (tc > 0 and ot == 0) else ()
            b.emit("ve", lambda h, u1=u1, u2=u2: h.tensor_scalar(
                u2, u1, C2, C2, ALU.add, ALU.subtract), deps=dw)
            b.emit("ve", lambda h, u2=u2: h.tensor_scalar(
                u2, u2, cn["chi64"], None, ALU.mult))
            b.emit("ve", lambda h, u1=u1, gh=gh: h.tensor_scalar(
                gh, u1, C1, C1, ALU.add, ALU.subtract))
            b.emit("ve", lambda h, gh=gh, u2=u2, hdst=hdst:
                   h.scalar_tensor_tensor(hdst, gh, cn["chi32"], u2,
                                          ALU.mult, ALU.add))
            har_tok[ot] = b.emit("ve", lambda h, hdst=hdst: h.tensor_scalar(
                hdst, hdst, 1.0 / 256.0, None, ALU.mult), inc=1)
        xh_war = wtokA[(NOT - 1) % 2]
        tmm2 = None
        for dt in range(NDT):
            bk, war = b.take_bank()
            pa = b.psum[bk]
            for kg in range(NOT // 8):
                par = (dt * (NOT // 8) + kg) % 2
                w2b = W2B[:, par * 8 * 128:(par + 1) * 8 * 128]
                lw2 = b.dma("sy",
                            w2b.rearrange("p (kt o) -> p kt o", kt=8),
                            w2t[kg * 8:(kg + 1) * 8, :,
                                dt * 128:(dt + 1) * 128].rearrange(
                                "kt p o -> p kt o"),
                            deps=(wtokB[par],) if wtokB[par] else stage_deps,
                            stream=f"w2_{par}")
                for kk in range(8):
                    kt = kg * 8 + kk
                    d = [lw2, har_tok[kt]]
                    if kt == 0:
                        d += list(war)
                    tmm2 = b.emit(
                        "te", lambda h, kt=kt, kk=kk, pa=pa, w2b=w2b:
                        h.matmul(pa[:, 0:TSC],
                                 w2b[:, kk * 128:(kk + 1) * 128],
                                 HAR[:, kt * TSC:(kt + 1) * TSC],
                                 start=(kt == 0), stop=(kt == NOT - 1)),
                        deps=tuple(d),
                        inc=1 if (kt == NOT - 1 or kt % 8 == 7) else 0)
                wtokB[par] = tmm2
            z = SCR[:, 0:TSC]
            u1 = SCR[:, TSC:2 * TSC]
            u2 = SCR[:, 2 * TSC:3 * TSC]
            b.emit("ve", lambda h, pa=pa, dt=dt, z=z: h.tensor_scalar(
                z, pa[:, 0:TSC], B2C[:, dt:dt + 1], 256.0, ALU.add,
                ALU.mult), deps=(tmm2, tb2))
            tz = b.emit("ve", lambda h: h.engine_nop(), inc=1)
            b.bank_done(bk, tz)
            od = OUTC[:, dt * TSC:(dt + 1) * TSC]
            rne(st, od, z, u1, u2, post_scale=1.0 / 256.0)
        arena_war = tmm2
        tall = b.emit("ve", lambda h: h.engine_nop(), inc=1)
        out_tok = dest_fn(tc, OUTC, tall)
    return out_tok


def sandwich_pwl_ffn(st, pfx, deps):
    first = [True]

    def src_fn(tc, XHC, war):
        d = tuple(deps) if first[0] else ()
        first[0] = False
        if war:
            d = d + (war,)
        return XHC, preln_chunk(
            st, tc, lambda dt: XHC[:, dt * TSC:(dt + 1) * TSC], deps=d)

    def dest_fn(tc, outc, tok):
        return postln_residual_chunk(
            st, tc, lambda dt: outc[:, dt * TSC:(dt + 1) * TSC], (tok,))

    return emit_pwl_ffn(st, pfx, src_fn, dest_fn, stage_deps=tuple(deps))



def emit_l1_attn(st, deps):
    b = st["b"]
    nc = st["nc"]
    cn = st["cn"]
    hbm = st["hbm"]
    XT = st["XT"]
    BIG = st["BIG"]
    GEN = st["GEN"]
    N_ST = 16

    off = [0]

    def carve(w):
        ap = GEN[:, off[0]:off[0] + w]
        off[0] += w
        return ap

    XDBL = carve(T)
    DELTPRE = XDBL[0:64, :]
    BCS = XDBL[64:96, :]
    DELT = carve(T)
    DU = carve(T)
    DA = carve(T)
    DBU = carve(T)
    HT = carve(T)
    YACC = carve(T)
    BROW = st["RRT"][0:1, 128:128 + T]
    WXPF = carve(NDT * 128)
    WDTF = carve(NDT * 128)
    ACOLS = carve(N_ST * NDT)
    BDTC = carve(NDT)
    DCOL = carve(NDT)
    assert off[0] <= GENW - 2 * TSC, off[0]

    bc_hbm = nc.dram_tensor("l1_bc_scratch", [32, T], F32)

    wxp = hbm["L1_attn_W_xprojT"]
    wdt = hbm["L1_attn_W_dtT"]
    bdt = hbm["L1_attn_b_dt"]
    alog = hbm["L1_attn_A_log"]
    dpar = hbm["L1_attn_D_param"]

    xh_toks = []
    for tc in range(NTC):
        t0 = tc * TSC
        tok = preln_chunk(
            st, tc, lambda dt, t0=t0: BIG[:, dt * T + t0:dt * T + t0 + TSC],
            deps=deps if tc == 0 else ())
        xh_toks.append(tok)
    txh = xh_toks[-1]

    tbdt = b.dma("sy", BDTC, bdt.ap().rearrange("(o p) -> p o", p=128),
                 slow=True, stream="bias1", deps=(txh,))
    tdp = b.dma("sy", DCOL, dpar.ap().rearrange("(o p) -> p o", p=128),
                slow=True, stream="bias2", deps=(txh,))
    tal = b.dma("sy",
                ACOLS.rearrange("p (dt n) -> p dt n", dt=NDT),
                alog.ap().rearrange("(dt p) n -> p dt n", p=128),
                stream="acols", deps=deps)
    b.emit("se", lambda h: h.activation(ACOLS, ACOLS, AF.Exp), deps=(tal,))
    tac = b.emit("se", lambda h: h.activation(
        ACOLS, ACOLS, AF.Copy, bias=0.0, scale=ccol(st, -1.0)), inc=1)

    wxpt = wxp.ap().rearrange("(kt p) o -> kt p o", p=128)
    tdpre = None
    tbc = None
    twxpf = b.dma("sy", WXPF.rearrange("p (kt o) -> p kt o", kt=NDT),
                  wxp.ap().rearrange("(kt p) o -> p kt o", p=128),
                  stream="w1_0", deps=deps)
    twdtf = b.dma("sy", WDTF[0:64, :], wdt[:, :], stream="w2_0", deps=deps)
    for tc in range(NTC):
        t0 = tc * TSC
        bk1, war1 = b.take_bank()
        bk2, war2 = b.take_bank()
        pd, pb = b.psum[bk1], b.psum[bk2]
        tmm = None
        for kt in range(NDT):
            wsl = WXPF[:, kt * 128:(kt + 1) * 128]
            d1 = (twxpf, txh) + (war1 if kt == 0 else ())
            tmm = b.emit("te", lambda h, kt=kt, t0=t0, pd=pd, wsl=wsl:
                         h.matmul(pd[:, 0:TSC], wsl,
                                  BIG[:, kt * T + t0:kt * T + t0 + TSC],
                                  start=(kt == 0), stop=(kt == NDT - 1)),
                         deps=d1, inc=1 if kt == NDT - 1 else 0)
        tdpre = b.emit("ve", lambda h, pd=pd, t0=t0: h.tensor_copy(
            XDBL[0:96, t0:t0 + TSC], pd[0:96, 0:TSC]), deps=(tmm,),
            inc=1)
        b.bank_done(bk1, tdpre)
        tbc = tdpre
        b.bank_done(bk2, tbc)
    tbcf = ve_flush(st, deps=(tbc,))
    tbc_out = b.dma("po", bc_hbm[:, :], BCS[:, :], deps=(tbcf,),
                    stream="bcout")

    ytoks = []
    brow_war = None
    tfin = None
    for dt in range(NDT):
        wsl = WDTF[0:64, dt * 128:(dt + 1) * 128]
        tdel = None
        for tc in range(NTC):
            t0 = tc * TSC
            bk, war = b.take_bank()
            pa = b.psum[bk]
            tm = b.emit("te", lambda h, dt=dt, t0=t0, pa=pa, wsl=wsl:
                        h.matmul(pa[:, 0:TSC], wsl,
                                 DELTPRE[0:64, t0:t0 + TSC],
                                 start=True, stop=True),
                        deps=(twdtf, tdpre) + war, inc=1)
            te_ = b.emit("se", lambda h, pa=pa, dt=dt, t0=t0: h.activation(
                DELT[:, t0:t0 + TSC], pa[:, 0:TSC], AF.Exp,
                bias=BDTC[:, dt:dt + 1]), deps=(tm, tbdt), inc=1)
            b.bank_done(bk, te_)
            b.emit("se", lambda h, dt=dt, t0=t0: h.activation(
                DELT[:, t0:t0 + TSC], DELT[:, t0:t0 + TSC], AF.Ln,
                bias=1.0), inc=1)
            tdel = se_flush(st)
        tdu = b.emit("ve", lambda h, dt=dt: h.tensor_tensor(
            DU[:, :], DELT[:, :], BIG[:, dt * T:(dt + 1) * T], ALU.mult),
            deps=(tdel, tfin), inc=1)
        for n in range(N_ST):
            b.emit("se", lambda h, dt=dt, n=n: h.activation(
                DA[:, :], DELT[:, :], AF.Exp,
                scale=ACOLS[:, dt * N_ST + n:dt * N_ST + n + 1]),
                deps=(tac, tdel), inc=1)
            ta = se_flush(st)
            trow = b.dma("se", BROW, bc_hbm[n:n + 1, :],
                         deps=(tbc_out, brow_war), stream="brow")
            tdbu = None
            for tc in range(NTC):
                t0 = tc * TSC
                bk, war = b.take_bank()
                pa = b.psum[bk]
                tbb = bcast_row(st, BROW[0:1, t0:t0 + TSC], pa, TSC,
                                deps=(trow,) + war)
                tdbu = b.emit("ve", lambda h, t0=t0, pa=pa: h.tensor_tensor(
                    DBU[:, t0:t0 + TSC], DU[:, t0:t0 + TSC], pa[:, 0:TSC],
                    ALU.mult), deps=(tbb, tdu), inc=1)
                b.bank_done(bk, tdbu)
            tsc_ = b.emit("ve", lambda h: h.tensor_tensor_scan(
                HT[:, :], DA[:, :], DBU[:, :], 0.0, ALU.mult, ALU.add),
                deps=(ta,), inc=1)
            trowc = b.dma("se", BROW, bc_hbm[16 + n:17 + n, :],
                          deps=(tsc_,), stream="brow")
            thc = None
            tcb = None
            for tc in range(NTC):
                t0 = tc * TSC
                bk, war = b.take_bank()
                pa = b.psum[bk]
                tcb = bcast_row(st, BROW[0:1, t0:t0 + TSC], pa, TSC,
                                deps=(trowc,) + war)
                thc = b.emit("ve", lambda h, t0=t0, pa=pa: h.tensor_tensor(
                    DBU[:, t0:t0 + TSC], HT[:, t0:t0 + TSC], pa[:, 0:TSC],
                    ALU.mult), deps=(tcb,), inc=1)
                b.bank_done(bk, thc)
            brow_war = tcb
            if n == 0:
                tyn = b.emit("ve", lambda h: h.tensor_copy(
                    YACC[:, :], DBU[:, :]), inc=1)
            else:
                tyn = b.emit("ve", lambda h: h.tensor_tensor(
                    YACC[:, :], YACC[:, :], DBU[:, :], ALU.add), inc=1)
        if st.get("l1_dbg") and dt == 0:
            XTd = st["XT"]
            for j, src_ap in enumerate([DELT[:, :], DU[:, :], DA[:, :],
                                        DBU[:, :], HT[:, :], YACC[:, :]]):
                b.emit("ve", lambda h, j=j, src_ap=src_ap: h.tensor_copy(
                    XTd[:, j * T:(j + 1) * T], src_ap), deps=(tyn,))
            b.emit("ve", lambda h: h.tensor_copy(
                XTd[0:64, 6 * T:7 * T], DELTPRE[0:64, :]))
            tdbg = b.emit("ve", lambda h: h.engine_nop(), inc=1)
            return tdbg
        tfin = b.emit("ve", lambda h, dt=dt: h.scalar_tensor_tensor(
            BIG[:, dt * T:(dt + 1) * T], BIG[:, dt * T:(dt + 1) * T],
            DCOL[:, dt:dt + 1], YACC[:, :], ALU.mult, ALU.add),
            deps=(tyn, tdp), inc=1)
        ytoks.append(tfin)

    tok = None
    for tc in range(NTC):
        t0 = tc * TSC
        tok = postln_residual_chunk(
            st, tc, lambda dt, t0=t0: BIG[:, dt * T + t0:dt * T + t0 + TSC],
            (ytoks[-1],) if tc == 0 else ())
    return tok




_L3_PHASE = 4
_L3_NTB = 16
_L3_LOCKSTEP = False


def emit_fp6_quant(st, xap, scap, scr, width, deps=(), par=0):
    b = st["b"]
    cn = st["cn"]
    nb = width // 32
    ay = scr[:, 0:width]
    t2 = scr[:, width:2 * width]
    t3 = scr[:, 2 * width:3 * width]
    am = st["AM"][:, par * 32:par * 32 + nb]
    aml = st["AML"][:, 0:nb]
    b.emit("ve", lambda h: h.tensor_scalar(ay, xap, -1.0, None, ALU.mult),
           deps=deps)
    b.emit("ve", lambda h: h.tensor_tensor(ay, ay, xap, ALU.max))
    trds = ()
    for c0 in range(0, width, 512):
        nbc = min(512, width - c0) // 32
        amc = am[:, c0 // 32:c0 // 32 + nbc]
        x3 = bass.AP(xap.tensor, xap.offset + c0,
                     [xap.ap[0], [32, nbc], [1, 32]])
        trds = trds + (b.emit("ve", lambda h, amc=amc, x3=x3:
                              h.tensor_reduce(amc, x3, AX.X, ALU.max,
                                              apply_absolute_value=True),
                              deps=deps if c0 == 0 else (), inc=1),)
    tse = b.emit("se", lambda h: h.activation(
        aml, am, AF.Copy, bias=0.0, scale=1.0), deps=trds, inc=1)
    if st.get("amdump") is not None:
        b.emit("ve", lambda h: h.tensor_copy(st["amdump"], aml),
               deps=(se_flush(st, deps=(tse,)),))
    tfa = se_flush(st, deps=(tse,))
    r0blk = st["CONSTS_T"][:, 480:512]
    rf = st["R75F"]
    e32 = st["E32"]
    if not st.get("r75_done"):
        t_e = b.emit("ve", lambda h: h.tensor_scalar(
            e32, r0blk, -7.5, 2.0, ALU.mult, ALU.add), inc=1)
        st["r75_done"] = b.emit("ve", lambda h: h.tensor_tensor(
            rf, e32, r0blk, ALU.mult), deps=(t_e,), hard=True, inc=1)
    tr75 = st["r75_done"]
    b.emit("ve", lambda h: h.tensor_tensor(scap, aml, rf[:, 0:nb],
                                           ALU.mult), deps=(tr75, tfa))
    if st.get("amdump") is not None:
        b.emit("ve", lambda h: h.tensor_copy(st["amdump2"], rf))
        b.emit("ve", lambda h: h.tensor_copy(st["amdump3"], scap))
    tcl = b.emit("ve", lambda h: h.tensor_scalar(
        scap, scap, 1e-30, None, ALU.max), inc=1)

    def replicate(dst, seed, seed_tok):
        t = b.emit("ve", lambda h: h.tensor_copy(
            bass.AP(dst.tensor, dst.offset, [dst.ap[0], [32, nb]]),
            seed), deps=seed_tok)
        for j in range(1, 32):
            t = b.emit("ve", lambda h, j=j: h.tensor_copy(
                bass.AP(dst.tensor, dst.offset + j, [dst.ap[0], [32, nb]]),
                bass.AP(dst.tensor, dst.offset, [dst.ap[0], [32, nb]])),
                inc=(1 if j == 31 else 0))
        return t

    scf = st["SCF"][:, 0:width]
    replicate(scf, scap, ())
    sraw = st["SINV"][:, par * 32:par * 32 + nb]
    sinvl = st["SINVL"][:, 0:nb]
    tsv0 = b.emit("ve", lambda h: h.reciprocal(sraw, scap), deps=(tcl,),
                  inc=1)
    tsv = b.emit("se", lambda h: h.activation(
        sinvl, sraw, AF.Copy, bias=0.0, scale=1.0), deps=(tsv0,), inc=1)
    tsf0 = se_flush(st, deps=(tsv,))
    sif = st["SINVF"][:, 0:width]
    tsf = replicate(sif, sinvl, (tsf0,))
    b.emit("ve", lambda h: h.tensor_tensor(xap, xap, sif, ALU.mult),
           deps=(tsf,))
    b.emit("ve", lambda h: h.tensor_tensor(ay, ay, sif, ALU.mult))
    b.emit("ve", lambda h: h.tensor_scalar(t2, ay, 2.0, None, ALU.is_ge))
    b.emit("ve", lambda h: h.tensor_scalar(t3, ay, 4.0, None, ALU.is_ge))
    b.emit("ve", lambda h: h.tensor_scalar(ay, t2, -4.0, 8.0, ALU.mult,
                                           ALU.add))
    b.emit("ve", lambda h: h.scalar_tensor_tensor(ay, t3, -2.0, ay,
                                                  ALU.mult, ALU.add))
    b.emit("ve", lambda h: h.tensor_tensor(xap, xap, ay, ALU.mult))
    b.emit("ve", lambda h: h.tensor_scalar(t2, t2, 0.125, 0.125, ALU.mult,
                                           ALU.add))
    b.emit("ve", lambda h: h.scalar_tensor_tensor(t2, t3, 0.25, t2,
                                                  ALU.mult, ALU.add))
    trne = rne(st, xap, xap, ay, t3)
    b.emit("ve", lambda h: h.tensor_tensor(xap, xap, t2, ALU.mult),
           deps=(trne,), hard=True)
    return b.emit("ve", lambda h: h.tensor_tensor(xap, xap, scf,
                                                  ALU.mult), inc=1)


def emit_l3_ffn(st, deps):
    b = st["b"]
    nc = st["nc"]
    cn = st["cn"]
    hbm = st["hbm"]
    XT = st["XT"]
    BIG = st["BIG"]
    GEN = st["GEN"]
    w = hbm["L3_ffn_weight"]
    bias = hbm["L3_ffn_biasT"]

    off = [0]

    def carve(width):
        ap = GEN[:, off[0]:off[0] + width]
        off[0] += width
        return ap
    XTOK = carve(1024)
    QSCR = carve(3 * 1024)
    SCX = carve(32)
    SCW = carve(32)
    st["R75"] = carve(2)
    st["R75F"] = carve(32)
    st["E32"] = carve(32)
    st["SINV"] = carve(128)
    st["SINVL"] = carve(32)
    st["AM"] = carve(128)
    st["AML"] = carve(32)
    st["SCF"] = carve(1024)
    st["SINVF"] = carve(1024)
    WQ = carve(1024)
    WQT = carve(1024)
    WTB = carve(2 * 1024)
    OUTC = carve(NDT * TSC)
    BCOL = carve(NDT)
    assert off[0] <= GENW - 2 * TSC, off[0]

    wdt_hbm = nc.dram_tensor("l3f_wdt", [D, D], F32)

    tb_bias = b.dma("sy", BCOL, bias.ap(), stream="bias1",
                    deps=deps)

    xh_toks = []
    if _L3_PHASE in (20, 21, 30):
        tk = None
        for dt in range(NDT):
            tk = b.emit("ve", lambda h, dt=dt: h.tensor_copy(
                BIG[:, dt * T:(dt + 1) * T], XT[:, dt * T:(dt + 1) * T]),
                deps=deps if dt == 0 else (), inc=1)
        xh_toks.append(tk)
    for tc in range(NTC if _L3_PHASE not in (20, 21, 30) else 0):
        t0 = tc * TSC
        tok = preln_chunk(
            st, tc, lambda dt, t0=t0: BIG[:, dt * T + t0:dt * T + t0 + TSC],
            deps=deps if tc == 0 else ())
        xh_toks.append(tok)
    txh = xh_toks[-1]

    xq_toks = []
    if _L3_PHASE in (1, 21):
        return txh
    prev_q = None
    ntb = _L3_NTB if _L3_PHASE in (15, 16, 17, 18, 20, 25, 2) else 16
    for tb in range(ntb):
        tof = tb * 128
        tevs = []
        for dt in range(NDT):
            bk, war = b.take_bank()
            pt = b.psum[bk]
            d = war + ((txh,) if dt == 0 else ())
            if _L3_LOCKSTEP and dt >= 1:
                d = d + (tevs[dt - 1],)
            tt = b.emit("te", lambda h, dt=dt, pt=pt: h.transpose(
                pt[:, 0:128], BIG[:, dt * T + tof:dt * T + tof + 128],
                cn["ident"]), deps=d, inc=1)
            tev = b.emit("ve", lambda h, dt=dt, pt=pt: h.tensor_copy(
                XTOK[:, dt * 128:(dt + 1) * 128], pt[:, 0:128]),
                deps=(tt,) + ((prev_q,) if prev_q and dt == 0 else ()),
                inc=1, hard=True)
            tevs.append(tev)
            b.bank_done(bk, tev)
        tev = tevs[-1]
        if _L3_PHASE in (28, 29, 30) and tb == 0:
            st["qdump"] = lambda k: XT[:, (k + 1) * T:(k + 1) * T + 1024]
        if _L3_PHASE == 27 and tb == 0:
            return b.emit("ve", lambda h: h.tensor_copy(
                XT[:, 0:1024], XTOK[:, 0:1024]), deps=(tev,), inc=1)
        if _L3_PHASE in (15, 16, 17, 18, 20):
            tq = tev
        else:
            tq = emit_fp6_quant(st, XTOK, SCX, QSCR, 1024, deps=(tev,),
                                par=tb % 4)
        prev_q = tq
        if _L3_PHASE in (28, 29, 30) and tb == 0:
            st["qdump"] = None
            return b.emit("ve", lambda h: h.tensor_copy(
                XT[:, 0:1024], XTOK[:, 0:1024]), deps=(tq,), inc=1,
                hard=True)
        if _L3_PHASE == 26 and tb == 0:
            b.emit("ve", lambda h: h.tensor_copy(
                XT[:, 0:1024], XTOK[:, 0:1024]), deps=(tq,), hard=True)
            b.emit("ve", lambda h: h.tensor_copy(
                XT[:, 1024:1056], SCX[:, 0:32]))
            b.emit("ve", lambda h: h.tensor_copy(
                XT[:, 1056:2080], QSCR[:, 1024:2048]))
            return b.emit("ve", lambda h: h.tensor_copy(
                XT[:, 2080:2112], st["SINV"][:, 0:32]), inc=1)
        if _L3_PHASE in (16, 17, 18, 20):
            xq_toks.append(tq)
            continue
        tev2 = None
        for dt in range(NDT):
            bk2, war2 = b.take_bank()
            pt2 = b.psum[bk2]
            d = war2 + ((tq,) if dt == 0 else ())
            tt = b.emit("te", lambda h, dt=dt, pt2=pt2: h.transpose(
                pt2[:, 0:128], XTOK[:, dt * 128:(dt + 1) * 128],
                cn["ident"]), deps=d, inc=1)
            tev2 = b.emit("ve", lambda h, dt=dt, pt2=pt2: h.tensor_copy(
                BIG[:, dt * T + tof:dt * T + tof + 128], pt2[:, 0:128]),
                deps=(tt,), inc=1)
            b.bank_done(bk2, tev2)
        xq_toks.append(tev2)
    txd = xq_toks[-1]
    if _L3_PHASE == 25:
        cp = None
        for dt in range(NDT):
            cp = b.emit("ve", lambda h, dt=dt: h.tensor_copy(
                XT[:, dt * T:(dt + 1) * T], BIG[:, dt * T:(dt + 1) * T]),
                deps=(txd,) if dt == 0 else (), inc=1, hard=True)
        return cp
    if _L3_PHASE in (2, 15):
        return txd

    wq_tok = None
    wout_tok = None
    for ot in range(NDT):
        lw = b.dma("sy", WQ, w[ot * 128:(ot + 1) * 128, :],
                   deps=(wq_tok,) if wq_tok else deps, stream="w1_0")
        tqw = emit_fp6_quant(st, WQ, SCW, QSCR, 1024,
                             deps=(lw, txd), par=ot % 4)
        wq_tok = tqw
        tev = None
        for kt in range(NDT):
            bk, war = b.take_bank()
            pt = b.psum[bk]
            d = war + ((tqw,) if kt == 0 else ())
            tt = b.emit("te", lambda h, kt=kt, pt=pt: h.transpose(
                pt[:, 0:128], WQ[:, kt * 128:(kt + 1) * 128],
                cn["ident"]), deps=d, inc=1)
            tev = b.emit("ve", lambda h, kt=kt, pt=pt: h.tensor_copy(
                WQT[:, kt * 128:(kt + 1) * 128], pt[:, 0:128]),
                deps=(tt,) + ((wout_tok,) if wout_tok and kt == 0 else ()),
                inc=1)
            b.bank_done(bk, tev)
        tvf = ve_flush(st, deps=(tev,))
        wout_tok = b.dma("po", wdt_hbm.ap().rearrange(
            "(kt p) o -> p kt o", p=128)[:, :, ot * 128:(ot + 1) * 128],
            WQT.rearrange("p (kt o) -> p kt o", kt=NDT),
            deps=(tvf,), stream="bcout")

    if _L3_PHASE == 3:
        return wout_tok
    wdt3 = wdt_hbm.ap().rearrange("(kt p) o -> kt p o", p=128)
    wtok = [None, None]
    out_tok = None
    for tc in range(NTC):
        t0 = tc * TSC
        for dt in range(NDT):
            par = dt % 2
            wsl = WTB[:, par * 1024:(par + 1) * 1024]
            lw = b.dma("sy", wsl.rearrange("p (kt o) -> p kt o", kt=NDT),
                       wdt3[:, :, dt * 128:(dt + 1) * 128].rearrange(
                           "kt p o -> p kt o"),
                       deps=(wtok[par],) if wtok[par] else (wout_tok,),
                       stream=f"w2_{par}")
            bk, war = b.take_bank()
            pa = b.psum[bk]
            tmm = None
            for kt in range(NDT):
                d = war + (lw, txd) if kt == 0 else ()
                tmm = b.emit("te", lambda h, kt=kt, pa=pa, wsl=wsl, t0=t0:
                             h.matmul(pa[:, 0:TSC],
                                      wsl[:, kt * 128:(kt + 1) * 128],
                                      BIG[:, kt * T + t0:kt * T + t0 + TSC],
                                      start=(kt == 0), stop=(kt == NDT - 1)),
                             deps=d, inc=1 if kt == NDT - 1 else 0)
            wtok[par] = tmm
            tev = b.emit("ve", lambda h, pa=pa, dt=dt: h.tensor_scalar(
                OUTC[:, dt * TSC:(dt + 1) * TSC], pa[:, 0:TSC],
                BCOL[:, dt:dt + 1], None, ALU.add),
                deps=(tmm, tb_bias), inc=1)
            b.bank_done(bk, tev)
        tall = b.emit("ve", lambda h: h.engine_nop(), inc=1)
        out_tok = postln_residual_chunk(
            st, tc, lambda dt: OUTC[:, dt * TSC:(dt + 1) * TSC], (tall,))
    return out_tok


def emit_l2_ffn(st, deps):
    b = st["b"]
    nc = st["nc"]
    cn = st["cn"]
    hbm = st["hbm"]
    XT = st["XT"]
    BIG = st["BIG"]
    GEN = st["GEN"]

    off = [0]

    def carve(width):
        ap = GEN[:, off[0]:off[0] + width]
        off[0] += width
        return ap
    Y1 = carve(NDT * TSC)
    Y2 = carve(NDT * TSC)
    QT = carve(1024)
    SCT = carve(1024)
    WW2 = carve(2 * 1024)
    LAT = carve(1024)
    LBT = carve(1024)
    RR32 = carve(TSC)
    BCOL2 = carve(16)
    GCOL = carve(8)
    BECOL = carve(8)
    assert off[0] <= GENW - 2 * TSC, off[0]

    qw = hbm["L2_ffn_qT"]
    sct = hbm["L2_ffn_scT"]
    bt = hbm["L2_ffn_biasT"]
    lat = hbm["L2_ffn_laT"]
    lbt = hbm["L2_ffn_lbT"]
    lng = hbm["L2_ffn_ln_gT"]
    lnb = hbm["L2_ffn_ln_bT"]

    tk = None
    for tc in range(NTC):
        tk = preln_chunk(
            st, tc, lambda dt, tc=tc: BIG[:, dt * T + tc * TSC:
                                          dt * T + tc * TSC + TSC],
            deps=deps if tc == 0 else ())
    th = tk

    wtok = [None, None]
    ltok = ()
    btok = None

    def qlora_linear(li, src_fn, dst_fn, dst_bias_col, tin, extra_dep):
        nonlocal wtok, ltok, btok
        toks = []
        bk, war = b.take_bank()
        pr = b.psum[bk]
        tmm = None
        for kt in range(NDT):
            d = war + (tuple(ltok) + (tin,) + extra_dep
                       if kt == 0 else ())
            tmm = b.emit("te", lambda h, kt=kt, pr=pr: h.matmul(
                pr[:, 0:TSC], LAT[:, kt * 128:(kt + 1) * 128],
                src_fn(kt), start=(kt == 0), stop=(kt == NDT - 1)),
                deps=d, inc=1 if kt == NDT - 1 else 0)
        trr = b.emit("ve", lambda h, pr=pr: h.tensor_copy(
            RR32[:, 0:TSC], pr[:, 0:TSC]),
            deps=(tmm, st.get("l2_rrwar")))
        b.bank_done(bk, trr)
        for do in range(NDT):
            par = do % 2
            qsl = QT
            ssl = SCT
            lq = b.dma("sy", qsl.rearrange("p (kt o) -> p kt o", kt=NDT),
                       qw[li, :, do * 128:(do + 1) * 128].rearrange(
                           "(kt p) o -> p kt o", p=128),
                       deps=(st.get("l2_qwar"),), stream="l2q")
            ls = b.dma("sy", ssl.rearrange("p (kt o) -> p kt o", kt=NDT),
                       sct[li, :, do * 128:(do + 1) * 128].rearrange(
                           "(kt p) o -> p kt o", p=128),
                       deps=(st.get("l2_qwar"),), stream="l2s")
            WW = WW2[:, par * 1024:(par + 1) * 1024]
            b.emit("ve", lambda h, qsl=qsl, WW=WW: h.tensor_scalar(
                WW, qsl, float(1.0 / 7.5), -1.0, ALU.mult, ALU.add),
                deps=(lq, ls, wtok[par]))
            tw = b.emit("ve", lambda h, ssl=ssl, WW=WW: h.tensor_tensor(
                WW, WW, ssl, ALU.mult), inc=1)
            st["l2_qwar"] = tw
            bk2, war2 = b.take_bank()
            pa = b.psum[bk2]
            tmm = None
            for kt in range(NDT):
                d = war2 + (tw,) if kt == 0 else ()
                tmm = b.emit("te", lambda h, kt=kt, pa=pa, WW=WW: h.matmul(
                    pa[:, 0:TSC], WW[:, kt * 128:(kt + 1) * 128],
                    src_fn(kt), start=(kt == 0), stop=False),
                    deps=d, inc=1)
            wtok[par] = tmm
            tmm = b.emit("te", lambda h, do=do, pa=pa: h.matmul(
                pa[:, 0:TSC], LBT[0:32, do * 128:(do + 1) * 128],
                RR32[0:32, 0:TSC], start=False, stop=True),
                deps=(trr,), inc=1)
            st["l2_rrwar"] = tmm
            tvv = b.emit("ve", lambda h, do=do, pa=pa: h.tensor_scalar(
                dst_fn(do), pa[:, 0:TSC], dst_bias_col(do), None,
                ALU.add), deps=(tmm, btok), inc=1)
            b.bank_done(bk2, tvv)
            toks.append(tvv)
        return toks[-1]

    hprev = (th,)
    for blk in range(3):
        i0, i1 = 2 * blk, 2 * blk + 1
        btok0 = b.dma("sy", BCOL2[:, 0:8], bt[i0], stream="l2b",
                      deps=hprev)
        btok1 = b.dma("sy", BCOL2[:, 8:16], bt[i1], stream="l2b",
                      deps=(btok0,))
        gtok = b.dma("sy", GCOL, lng[blk], stream="l2g", deps=hprev)
        betok = b.dma("sy", BECOL, lnb[blk], stream="l2g", deps=(gtok,))
        lta = b.dma("sy", LAT.rearrange("p (kt m) -> p kt m", kt=NDT),
                    lat[i0].rearrange("(kt p) m -> p kt m", p=128),
                    stream="l2la", deps=hprev)
        ltb = b.dma("sy", LBT, lbt[i0], stream="l2lb", deps=hprev)
        ltok = (lta, ltb)
        btok = btok0
        out_tok = None
        for tc in range(NTC):
            t0 = tc * TSC
            t_y1 = qlora_linear(
                i0,
                lambda kt, t0=t0: BIG[:, kt * T + t0:kt * T + t0 + TSC],
                lambda do: Y1[:, do * TSC:(do + 1) * TSC],
                lambda do: BCOL2[:, do:do + 1],
                hprev[0] if tc == 0 else out_tok, ())
            bk_mu, rstd_ap, tstat = ln_stats_chunk(
                st, lambda dt: Y1[:, dt * TSC:(dt + 1) * TSC],
                deps=(t_y1,))
            pm = b.psum[bk_mu]
            tok = None
            for dt in range(NDT):
                ysl = Y1[:, dt * TSC:(dt + 1) * TSC]
                b.emit("ve", lambda h, ysl=ysl, pm=pm: h.tensor_tensor(
                    ysl, ysl, pm[:, 0:TSC], ALU.subtract),
                    deps=(tstat,) if dt == 0 else ())
                b.emit("ve", lambda h, ysl=ysl, rstd_ap=rstd_ap:
                       h.tensor_tensor(ysl, ysl, rstd_ap, ALU.mult))
                b.emit("ve", lambda h, ysl=ysl, dt=dt: h.tensor_scalar(
                    ysl, ysl, GCOL[:, dt:dt + 1], BECOL[:, dt:dt + 1],
                    ALU.mult, ALU.add), deps=(betok,))
                tok = b.emit("ve", lambda h, ysl=ysl: h.engine_nop(),
                             inc=1)
            b.bank_done(bk_mu, tok)
            st["sq_war"][0] = tok
            tvf = ve_flush(st, deps=(tok,))
            tge = None
            for dt in range(NDT):
                ysl = Y1[:, dt * TSC:(dt + 1) * TSC]
                tge = b.emit("se", lambda h, ysl=ysl: h.activation(
                    ysl, ysl, AF.Gelu), deps=(tvf,) if dt == 0 else (),
                    inc=1)
            tgf = se_flush(st, deps=(tge,))
            lta1 = b.dma("sy", LAT.rearrange("p (kt m) -> p kt m", kt=NDT),
                         lat[i1].rearrange("(kt p) m -> p kt m", p=128),
                         stream="l2la", deps=(t_y1,))
            ltb1 = b.dma("sy", LBT, lbt[i1], stream="l2lb", deps=(t_y1,))
            ltok = (lta1, ltb1)
            btok = btok1
            t_y2 = qlora_linear(
                i1,
                lambda kt: Y1[:, kt * TSC:(kt + 1) * TSC],
                lambda do: Y2[:, do * TSC:(do + 1) * TSC],
                lambda do: BCOL2[:, 8 + do:9 + do],
                tgf, ())
            cp = None
            for dt in range(NDT):
                cp = b.emit("ve", lambda h, dt=dt, t0=t0: h.tensor_tensor(
                    BIG[:, dt * T + t0:dt * T + t0 + TSC],
                    BIG[:, dt * T + t0:dt * T + t0 + TSC],
                    Y2[:, dt * TSC:(dt + 1) * TSC], ALU.add),
                    deps=(t_y2,) if dt == 0 else (), inc=1)
            out_tok = cp
            if tc < NTC - 1:
                lta0 = b.dma("sy",
                             LAT.rearrange("p (kt m) -> p kt m", kt=NDT),
                             lat[i0].rearrange("(kt p) m -> p kt m", p=128),
                             stream="l2la", deps=(t_y2,))
                ltb0 = b.dma("sy", LBT, lbt[i0], stream="l2lb",
                             deps=(t_y2,))
                ltok = (lta0, ltb0)
                btok = btok0
        hprev = (out_tok,)
    out_tok = None
    for tc in range(NTC):
        out_tok = postln_residual_chunk(
            st, tc, lambda dt, tc=tc: BIG[:, dt * T + tc * TSC:
                                          dt * T + tc * TSC + TSC],
            (hprev[0],) if tc == 0 else (out_tok,))
    return out_tok


def emit_l3_attn(st, deps):
    b = st["b"]
    nc = st["nc"]
    cn = st["cn"]
    hbm = st["hbm"]
    XT = st["XT"]
    BIG = st["BIG"]
    GEN = st["GEN"]
    NIT = 16
    NST = 64

    XD1 = GEN[:, 0:T]
    XD2 = GEN[:, T:2 * T]
    A1 = GEN[:, 2 * T:3 * T]
    A2 = GEN[:, 3 * T:4 * T]
    A3 = GEN[:, 4 * T:5 * T]
    A4 = GEN[:, 5 * T:6 * T]
    A5 = GEN[:, 6 * T:7 * T]
    mo = 7 * T
    ACOL = GEN[:, mo:mo + NIT * NST]
    BDT = GEN[:, mo + 1024:mo + 1040]
    DCOL = GEN[:, mo + 1040:mo + 1056]
    CVB = GEN[:, mo + 1056:mo + 1072]
    CVW = GEN[:, mo + 1072:mo + 1136]
    GCOL = GEN[:, mo + 1136:mo + 1144]
    BCOL = GEN[:, mo + 1144:mo + 1152]
    CVP = GEN[:, mo + 1152:mo + 1155]
    assert mo + 1155 <= GENW - 2 * TSC

    win = hbm["L3_attn_W_in"]
    wxp = hbm["L3A_WXP"]
    wdtp = hbm["L3A_WDTP"]
    wout = hbm["L3_attn_W_out"]

    xi_hbm = nc.dram_tensor("l3a_xi", [2048, T], F32)
    z_hbm = nc.dram_tensor("l3a_z", [2048, T], F32)
    u_hbm = nc.dram_tensor("l3a_u", [2048, T], F32)
    yf_hbm = nc.dram_tensor("l3a_yf", [2048, T], F32)

    tg = b.dma("sy", GCOL, hbm["L3A_lnw"].ap(), stream="l3c", deps=deps)
    tg = b.dma("sy", BCOL, hbm["L3A_lnb"].ap(), stream="l3c", deps=(tg,))
    tg = b.dma("sy", BDT, hbm["L3A_bdt"].ap(), stream="l3c", deps=(tg,))
    tg = b.dma("sy", DCOL, hbm["L3A_D"].ap(), stream="l3c", deps=(tg,))
    tg = b.dma("sy", CVW, hbm["L3A_convT"].ap(), stream="l3c", deps=(tg,))
    tg = b.dma("sy", CVB, hbm["L3A_convb"].ap(), stream="l3c", deps=(tg,))
    tga = b.dma("sy", ACOL, hbm["L3A_ALOG"].ap(), stream="l3c", deps=(tg,))
    tal = b.emit("se", lambda h: h.activation(
        ACOL, ACOL, AF.Exp), deps=(tga,), inc=1)
    tsfa = se_flush(st, deps=(tal,))
    tav = b.emit("ve", lambda h: h.tensor_scalar(
        ACOL, ACOL, -1.0, None, ALU.mult), deps=(tsfa,))
    tavf = ve_flush(st, deps=(tav,))

    xh_hbm = nc.dram_tensor("l3a_xh", [1024, T], F32)
    tk = None
    txs = None
    for tc in range(NTC):
        tk = preln_chunk(
            st, tc, lambda dt, tc=tc: BIG[:, dt * T + tc * TSC:
                                          dt * T + tc * TSC + TSC],
            deps=deps if tc == 0 else ())
    for dt in range(NDT):
        txs = b.dma("po", xh_hbm[dt * 128:(dt + 1) * 128, :],
                    BIG[:, dt * T:(dt + 1) * T],
                    deps=(tk,) if dt == 0 else (txs,), stream="l3xh")
    for tc in range(NTC):
        t0 = tc * TSC
        bk_mu, rstd_ap, tstat = ln_stats_chunk(
            st, lambda dt, t0=t0: BIG[:, dt * T + t0:dt * T + t0 + TSC],
            deps=(tk, txs) if tc == 0 else ())
        pm = b.psum[bk_mu]
        for dt in range(NDT):
            sl = BIG[:, dt * T + t0:dt * T + t0 + TSC]
            b.emit("ve", lambda h, sl=sl, pm=pm: h.tensor_tensor(
                sl, sl, pm[:, 0:TSC], ALU.subtract),
                deps=(tstat, txs) if dt == 0 else ())
            b.emit("ve", lambda h, sl=sl, rstd_ap=rstd_ap: h.tensor_tensor(
                sl, sl, rstd_ap, ALU.mult))
            b.emit("ve", lambda h, sl=sl, dt=dt: h.tensor_scalar(
                sl, sl, GCOL[:, dt:dt + 1], None, ALU.mult), deps=(tg,))
            tk = b.emit("ve", lambda h, sl=sl, dt=dt: h.tensor_scalar(
                sl, sl, BCOL[:, dt:dt + 1], None, ALU.add), inc=1)
        b.bank_done(bk_mu, tk)
        st["sq_war"][0] = tk
    txh = tk

    wpar = [None, None]
    tout = None
    for ot in range(32):
        par = ot % 2
        wsl = (A1 if par == 0 else A2)[:, 0:1024]
        lw = b.dma("sy", wsl.rearrange("p (kt m) -> p kt m", kt=NDT),
                   win[:, ot * 128:(ot + 1) * 128].rearrange(
                       "(kt p) m -> p kt m", p=128),
                   deps=(wpar[par],) if wpar[par] else (txh,),
                   stream=f"l3w{par}")
        for tc in range(NTC):
            t0 = tc * TSC
            bk, war = b.take_bank()
            pa = b.psum[bk]
            tmm = None
            for kt in range(NDT):
                d = war + ((lw, txh) if kt == 0 else ())
                tmm = b.emit("te", lambda h, kt=kt, pa=pa, wsl=wsl, t0=t0:
                             h.matmul(pa[:, 0:TSC],
                                      wsl[:, kt * 128:(kt + 1) * 128],
                                      BIG[:, kt * T + t0:kt * T + t0 + TSC],
                                      start=(kt == 0), stop=(kt == NDT - 1)),
                             deps=d, inc=1 if kt == NDT - 1 else 0)
            if tc == NTC - 1:
                wpar[par] = tmm
            tcp = b.emit("ve", lambda h, pa=pa, tc=tc: h.tensor_copy(
                A3[:, tc * TSC:(tc + 1) * TSC], pa[:, 0:TSC]),
                deps=(tmm, tout))
            b.bank_done(bk, tcp)
            dst = xi_hbm if ot < 16 else z_hbm
            ro = (ot % 16) * 128
            tout = b.dma("po", dst[ro:ro + 128, t0:t0 + TSC],
                         A3[:, tc * TSC:(tc + 1) * TSC],
                         deps=(tcp,), stream="l3o")
    t_p1 = tout

    tprev = t_p1
    tz = b.emit("ve", lambda h: h.memset(CVP, 0.0))
    for it in range(NIT):
        tld = b.dma("sy", A1[:, 0:T], xi_hbm[it * 128:(it + 1) * 128, :],
                    deps=(tprev,), stream="l3cv")
        tu = b.emit("ve", lambda h, it=it: h.tensor_scalar(
            A2[:, 0:T], A1[:, 0:T], CVW[:, it * 4 + 3:it * 4 + 4], None,
            ALU.mult), deps=(tld, tz, tg))
        for k in range(3):
            w_ = 3 - k
            tu = b.emit("ve", lambda h, it=it, k=k, w_=w_:
                        h.scalar_tensor_tensor(
                            A2[:, w_:T], A1[:, 0:T - w_],
                            CVW[:, it * 4 + k:it * 4 + k + 1],
                            A2[:, w_:T], ALU.mult, ALU.add))
        tu = b.emit("ve", lambda h, it=it: h.tensor_scalar(
            A2[:, 0:T], A2[:, 0:T], CVB[:, it:it + 1], None, ALU.add))
        tvf = ve_flush(st, deps=(tu,))
        tsg = b.emit("se", lambda h: h.activation(
            A5[:, 0:T], A2[:, 0:T], AF.Sigmoid), deps=(tvf,), inc=1)
        tsf = se_flush(st, deps=(tsg,))
        tu2 = b.emit("ve", lambda h: h.tensor_tensor(
            A2[:, 0:T], A2[:, 0:T], A5[:, 0:T], ALU.mult), deps=(tsf,))
        tprev = b.dma("po", u_hbm[it * 128:(it + 1) * 128, :], A2[:, 0:T],
                      deps=(tu2,), stream="l3o")
    t_p2 = tprev

    twx = b.dma("sy", A1[:, 0:T].rearrange("p (kt m) -> p kt m", kt=8),
                wxp.ap()[0:1024].rearrange("(kt p) m -> p kt m", p=128),
                deps=(t_p2,), stream="l3w0")
    twx2 = b.dma("sy", A2[:, 0:T].rearrange("p (kt m) -> p kt m", kt=8),
                 wxp.ap()[1024:2048].rearrange("(kt p) m -> p kt m", p=128),
                 deps=(t_p2,), stream="l3w1")

    def wxt(it, g):
        src = A1 if it < 8 else A2
        return src[:, (it % 8) * 256 + g * 128:(it % 8) * 256 + g * 128 + 128]
    upar = [None, None]
    tl_tok = [None, None]
    t_p3 = None
    for tc in range(NTC):
        t0 = tc * TSC
        for g in range(2):
            bk, war = b.take_bank()
            pa = b.psum[bk]
            tmm = None
            for it in range(NIT):
                p_ = it % 2
                usl = A3[:, p_ * TSC:(p_ + 1) * TSC]
                tl = b.dma("sy", usl,
                           u_hbm[it * 128:(it + 1) * 128, t0:t0 + TSC],
                           deps=(upar[p_],) if upar[p_] else (t_p2,),
                           stream=f"l3u{p_}")
                d = war + ((twx, twx2) if it == 0 else ())
                tmm = b.emit("te", lambda h, it=it, pa=pa, g=g, usl=usl,
                             tl=tl: h.matmul(
                                 pa[:, 0:TSC], wxt(it, g), usl,
                                 start=(it == 0), stop=(it == NIT - 1)),
                             deps=d + (tl,), inc=1)
                upar[p_] = tmm
            dstg = XD1 if g == 0 else XD2
            t_p3 = b.emit("ve", lambda h, pa=pa, dstg=dstg, t0=t0:
                          h.tensor_copy(dstg[:, t0:t0 + TSC], pa[:, 0:TSC]),
                          deps=(tmm,))
            b.bank_done(bk, t_p3)
    t_p3 = b.emit("ve", lambda h: h.engine_nop(), deps=(t_p3,), inc=1)

    twd = b.dma("sy", A3[:, 0:T], wdtp.ap(), deps=(upar[0], upar[1]),
                stream="l3w0")

    B1 = BIG[:, 0:T]
    B2 = BIG[:, T:2 * T]
    B3 = BIG[:, 2 * T:3 * T]
    B4 = BIG[:, 3 * T:4 * T]
    B5 = BIG[:, 4 * T:5 * T]
    B6 = BIG[:, 5 * T:6 * T]
    B7 = BIG[:, 6 * T:7 * T]
    B8 = BIG[:, 7 * T:8 * T]
    zcol = CVP[:, 0:1]

    tyout = None
    tscan_prev = None
    for itp in range(NIT // 2):
        its = (2 * itp, 2 * itp + 1)
        DELTs = {its[0]: A1, its[1]: A2}
        DUs = {its[0]: B4, its[1]: B5}
        YACCs = {its[0]: B6, its[1]: B3}
        tdu = {}
        tya = None
        for it in its:
            DELT = DELTs[it]
            tdel = None
            for tc in range(NTC):
                t0 = tc * TSC
                bk, war = b.take_bank()
                pa = b.psum[bk]
                tmm = b.emit("te", lambda h, it=it, pa=pa, t0=t0: h.matmul(
                    pa[:, 0:TSC], A3[:, it * 128:(it + 1) * 128],
                    XD1[:, t0:t0 + TSC], start=True, stop=True),
                    deps=war + ((twd, t_p3) if tc == 0 and it == 0
                                else ()), inc=1)
                tdel = b.emit("ve", lambda h, pa=pa, t0=t0, it=it,
                              DELT=DELT: h.tensor_scalar(
                                  DELT[:, t0:t0 + TSC], pa[:, 0:TSC],
                                  BDT[:, it:it + 1], None, ALU.add),
                              deps=(tmm,) + ((tyout,)
                                             if tyout and tc == 0 else ()))
                b.bank_done(bk, tdel)
            tvf = ve_flush(st, deps=(tdel,))
            te1 = b.emit("se", lambda h, DELT=DELT: h.activation(
                B7, DELT, AF.Exp), deps=(tvf,), inc=1)
            te2 = b.emit("se", lambda h, DELT=DELT: h.activation(
                DELT, B7, AF.Ln, bias=1.0), deps=(te1,), inc=1)
            tsf = se_flush(st, deps=(te2,))
            tld = b.dma("sy", DUs[it], u_hbm[it * 128:(it + 1) * 128, :],
                        deps=(tscan_prev, te1), stream="l3cv")
            tdu[it] = b.emit("ve", lambda h, du=DUs[it], DELT=DELT:
                             h.tensor_tensor(du, du, DELT,
                                             ALU.mult), deps=(tld, tsf))
            tya = b.emit("ve", lambda h, ya=YACCs[it]: h.memset(
                ya, 0.0), inc=1)
        for n in range(NST):
            BR = st["RRT"][0:1, 128:128 + T]
            tbr = b.dma("sy", BR, XD1[64 + n:65 + n, 0:T],
                        deps=(st.get("l3brwar"), t_p3),
                        stream=f"l3br{n % 2}")
            bbanks = []
            for tc in range(NTC):
                t0 = tc * TSC
                bk, war = b.take_bank()
                pb = b.psum[bk]
                tb_ = b.emit("te", lambda h, pb=pb, t0=t0: h.matmul(
                    pb[:, 0:TSC], cn["ones_row"], BR[0:1, t0:t0 + TSC],
                    start=True, stop=True), deps=war + (tbr,), inc=1)
                bbanks.append((bk, pb, tb_))
            tyn = None
            cbanks = None
            for it in its:
                dprev = ()
                if it == its[1]:
                    dprev = (st["_scan_tok_%d" % its[0]],)
                tda = b.emit("se", lambda h, it=it, de=DELTs[it]:
                             h.activation(
                    B7, de, AF.Exp,
                    scale=ACOL[:, it * NST + n:it * NST + n + 1]),
                    deps=(tavf, tdu[it], tya) + dprev +
                    ((tscan_prev,) if tscan_prev else ()), inc=1)
                tdaf = se_flush(st, deps=(tda,))
                tdbu = None
                for tc in range(NTC):
                    t0 = tc * TSC
                    bk, pb, tb_ = bbanks[tc]
                    tdbu = b.emit("ve", lambda h, pb=pb, t0=t0,
                                  du=DUs[it]: h.tensor_tensor(
                                      B1[:, t0:t0 + TSC],
                                      du[:, t0:t0 + TSC],
                                      pb[:, 0:TSC], ALU.mult),
                                  deps=(tb_,), inc=1)
                if it == its[1]:
                    for bk, pb, tb_ in bbanks:
                        b.bank_done(bk, tdbu)
                    tcr = b.dma("sy", BR, XD2[n:n + 1, 0:T],
                                deps=(tdbu,), stream=f"l3brc{n % 2}")
                tsc_ = b.emit("ve", lambda h: h.tensor_tensor_scan(
                    B2, B7, B1, zcol, ALU.mult, ALU.add),
                    deps=(tdbu, tdaf), inc=1)
                if it == its[1]:
                    cbanks = []
                    for tc in range(NTC):
                        t0 = tc * TSC
                        bk, war = b.take_bank()
                        pc = b.psum[bk]
                        tb_ = b.emit("te", lambda h, pc=pc, t0=t0:
                                     h.matmul(pc[:, 0:TSC],
                                              cn["ones_row"],
                                              BR[0:1, t0:t0 + TSC],
                                              start=True, stop=True),
                                     deps=war + (tcr,), inc=1)
                        st["l3brwar"] = tb_
                        cbanks.append((bk, pc, tb_))
                if it == its[0]:
                    pass
                if it == its[1]:
                    pass
                st["_scan_tok_%d" % it] = tsc_
                if it == its[0]:
                    tmv = b.emit("ve", lambda h: h.tensor_copy(
                        B8, B2), deps=(tsc_,), inc=1)
                    st["_h0"] = tmv
            for it in its:
                hsrc = B8 if it == its[0] else B2
                gate = st["_h0"] if it == its[0] else \
                    st["_scan_tok_%d" % it]
                for tc in range(NTC):
                    t0 = tc * TSC
                    bk, pc, tb_ = cbanks[tc]
                    b.emit("ve", lambda h, pc=pc, t0=t0, hsrc=hsrc:
                           h.tensor_tensor(
                               B1[:, t0:t0 + TSC], hsrc[:, t0:t0 + TSC],
                               pc[:, 0:TSC], ALU.mult),
                           deps=(tb_, gate))
                    tyn = b.emit("ve", lambda h, t0=t0, ya=YACCs[it]:
                                 h.tensor_tensor(
                                     ya[:, t0:t0 + TSC],
                                     ya[:, t0:t0 + TSC],
                                     B1[:, t0:t0 + TSC], ALU.add), inc=1)
            for bk, pc, tb_ in cbanks:
                b.bank_done(bk, tyn)
            tscan_prev = tyn
        for it in its:
            tlu = b.dma("sy", B7, u_hbm[it * 128:(it + 1) * 128, :],
                        deps=(tscan_prev, tyout), stream="l3cv")
            tyd = b.emit("ve", lambda h, it=it, ya=YACCs[it]:
                         h.scalar_tensor_tensor(
                             ya, B7, DCOL[:, it:it + 1], ya,
                             ALU.mult, ALU.add), deps=(tlu, tscan_prev))
            tlz = b.dma("sy", B7, z_hbm[it * 128:(it + 1) * 128, :],
                        deps=(tyd,), stream="l3cv")
            tvf2 = ve_flush(st, deps=(tyd,))
            tsg = b.emit("se", lambda h: h.activation(
                B1, B7, AF.Sigmoid), deps=(tlz, tvf2), inc=1)
            tsf2 = se_flush(st, deps=(tsg,))
            b.emit("ve", lambda h: h.tensor_tensor(B7, B7, B1, ALU.mult),
                   deps=(tsf2, tlz))
            tyf = b.emit("ve", lambda h, ya=YACCs[it]: h.tensor_tensor(
                ya, ya, B7, ALU.mult))
            tyout = b.dma("po", yf_hbm[it * 128:(it + 1) * 128, :],
                          YACCs[it], deps=(tyf,), stream="l3o")
        tscan_prev = tyout

    wpar2 = [None, None]
    ypar = [None, None]
    out_tok = None
    for tc in range(NTC):
        t0 = tc * TSC
        for dto in range(NDT):
            par = dto % 2
            wsl = (A1 if par == 0 else A2)[:, 0:T]
            lw = b.dma("sy", wsl.rearrange("p (kt m) -> p kt m", kt=16),
                       wout[:, dto * 128:(dto + 1) * 128].rearrange(
                           "(kt p) m -> p kt m", p=128),
                       deps=(wpar2[par],) if wpar2[par] else
                       (tyout, t_p3), stream=f"l3w{par}")
            bk, war = b.take_bank()
            pa = b.psum[bk]
            tmm = None
            for kt in range(16):
                p_ = kt % 2
                ysl = B7[:, p_ * TSC:(p_ + 1) * TSC]
                tl = b.dma("sy", ysl,
                           yf_hbm[kt * 128:(kt + 1) * 128, t0:t0 + TSC],
                           deps=(ypar[p_],) if ypar[p_] else (tyout,),
                           stream=f"l3u{p_}")
                tmm = b.emit("te", lambda h, kt=kt, pa=pa, wsl=wsl, tl=tl,
                             lw=lw: h.matmul(
                                 pa[:, 0:TSC],
                                 wsl[:, kt * 128:(kt + 1) * 128],
                                 B7[:, (kt % 2) * TSC:(kt % 2 + 1) * TSC],
                                 start=(kt == 0), stop=(kt == 15)),
                             deps=(tl,) + (war + (lw,) if kt == 0 else ()),
                             inc=1)
                ypar[kt % 2] = tmm
            wpar2[par] = tmm
            xsl = A3[:, (dto % 4) * TSC:(dto % 4 + 1) * TSC]
            lxh = b.dma("sy", xsl,
                        xh_hbm[dto * 128:(dto + 1) * 128, t0:t0 + TSC],
                        deps=(st.get("l3xhwar"), tyout), stream="l3xh")
            tcp = b.emit("ve", lambda h, pa=pa, dto=dto, xsl=xsl:
                         h.tensor_tensor(
                             BIG[:, dto * TSC:(dto + 1) * TSC], xsl,
                             pa[:, 0:TSC], ALU.add), deps=(tmm, lxh),
                         inc=1)
            st["l3xhwar"] = tcp
            b.bank_done(bk, tcp)
        out_tok = postln_residual_chunk(
            st, tc, lambda dto: BIG[:, dto * TSC:(dto + 1) * TSC],
            (tcp,))
    return out_tok





def emit_l2_attn(st, deps):
    b = st["b"]
    nc = st["nc"]
    cn = st["cn"]
    hbm = st["hbm"]
    XT = st["XT"]
    BIG = st["BIG"]
    GEN = st["GEN"]

    WPAR = GEN[:, 0:2048]
    SCR = GEN[:, 2048:2560]
    LAMT = GEN[:, 2560:3072]
    INPT = GEN[:, 3072:3584]
    COLS = GEN[:, 3584:3648]
    IJ = GEN[:, 3648:5696]
    LJ = GEN[:, 5696:7744]
    HH = GEN[:, 7744:9792]
    GT = GEN[:, 3648:7744]
    OUTC = GEN[:, 7744:11840]

    cc_in = nc.dram_tensor("l2a_in", [2048, T], F32)
    cc_out = nc.dram_tensor("l2a_out", [8 * 2048, T], F32)
    gate_hbm = nc.dram_tensor("l2a_gate", [1024, T], F32)

    tg = b.dma("sy", COLS[:, 0:8], hbm["L2A_binT"].ap(), stream="l2ac",
               deps=deps)
    tg = b.dma("sy", COLS[:, 8:16], hbm["L2A_blamT"].ap(), stream="l2ac",
               deps=(tg,))
    tg = b.dma("sy", COLS[:, 16:24], hbm["L2A_bgateT"].ap(), stream="l2ac",
               deps=(tg,))
    tg = b.dma("sy", COLS[:, 24:32], hbm["L2A_boutT"].ap(), stream="l2ac",
               deps=(tg,))
    tg = b.dma("sy", COLS[:, 32:40], hbm["L2A_gamT"].ap(), stream="l2ac",
               deps=(tg,))
    tg = b.dma("sy", COLS[:, 40:48], hbm["L2A_betT"].ap(), stream="l2ac",
               deps=(tg,))
    tg = b.dma("sy", COLS[:, 48:56], hbm["bsel"].ap(), stream="l2ac",
               deps=(tg,))

    tk = None
    for tc in range(NTC):
        tk = preln_chunk(
            st, tc, lambda dt, tc=tc: BIG[:, dt * T + tc * TSC:
                                          dt * T + tc * TSC + TSC],
            deps=deps if tc == 0 else ())
    txh = tk

    wts = {"in": hbm["L2A_WinT"], "lam": hbm["L2A_WlamT"],
           "gate": hbm["L2A_WgateT"]}
    wpar_tok = {}
    dout = None
    for ot in range(NDT):
        lws = {}
        for wi, name in enumerate(("in", "lam", "gate")):
            wsl = WPAR[:, (wi % 2) * 1024:(wi % 2) * 1024 + 1024]
            lws[name] = (wsl, None)
        for tc in range(NTC):
            t0 = tc * TSC

            def mk(name, wi, extra=()):
                wsl = WPAR[:, (wi % 2) * 1024:(wi % 2) * 1024 + 1024]
                lw = b.dma("sy",
                           wsl.rearrange("p (kt m) -> p kt m", kt=NDT),
                           wts[name][:, ot * 128:(ot + 1) * 128].rearrange(
                               "(kt p) m -> p kt m", p=128),
                           deps=(wpar_tok.get(wi % 2), txh),
                           stream=f"l2aw{wi % 2}")
                bk, war = b.take_bank()
                pa = b.psum[bk]
                tmm = None
                for kt in range(NDT):
                    d = war + ((lw,) if kt == 0 else ())
                    tmm = b.emit("te", lambda h, kt=kt, pa=pa, wsl=wsl,
                                 t0=t0: h.matmul(
                                     pa[:, 0:TSC],
                                     wsl[:, kt * 128:(kt + 1) * 128],
                                     BIG[:, kt * T + t0:kt * T + t0 + TSC],
                                     start=(kt == 0), stop=(kt == NDT - 1)),
                                 deps=d, inc=1)
                wpar_tok[wi % 2] = tmm
                return pa, tmm, bk
            pa, tmm, bk = mk("lam", 0)
            b.emit("ve", lambda h, pa=pa, ot=ot: h.tensor_scalar(
                LAMT, pa[:, 0:TSC], COLS[:, 8 + ot:9 + ot], None, ALU.add),
                deps=(tmm, tg, dout))
            tvf = ve_flush(st)
            tsg = b.emit("se", lambda h: h.activation(
                SCR, LAMT, AF.Sigmoid), deps=(tvf,), inc=1)
            tsf = se_flush(st, deps=(tsg,))
            tlam = b.emit("ve", lambda h: h.tensor_copy(LAMT, SCR),
                          deps=(tsf,), inc=1)
            b.bank_done(bk, tlam)
            pa2, tmm2, bk2 = mk("in", 1)
            b.emit("ve", lambda h, pa2=pa2, ot=ot: h.tensor_scalar(
                INPT, pa2[:, 0:TSC], COLS[:, ot:ot + 1], None, ALU.add),
                deps=(tmm2,))
            tvf = ve_flush(st)
            tsg = b.emit("se", lambda h: h.activation(
                SCR, INPT, AF.Sigmoid), deps=(tvf,), inc=1)
            tsf = se_flush(st, deps=(tsg,))
            b.emit("ve", lambda h: h.tensor_tensor(
                INPT, INPT, SCR, ALU.mult), deps=(tsf,))
            b.emit("ve", lambda h: h.tensor_tensor(
                SCR, LAMT, INPT, ALU.mult))
            tinp = b.emit("ve", lambda h: h.tensor_tensor(
                INPT, INPT, SCR, ALU.subtract), inc=1)
            b.bank_done(bk2, tinp)
            d1 = b.dma("po", cc_in[ot * 128:(ot + 1) * 128, t0:t0 + TSC],
                       INPT, deps=(tinp, dout), stream="l2o")
            d2 = b.dma("po",
                       cc_in[1024 + ot * 128:1024 + (ot + 1) * 128,
                             t0:t0 + TSC],
                       LAMT, deps=(tinp,), stream="l2o")
            pa3, tmm3, bk3 = mk("gate", 0)
            b.emit("ve", lambda h, pa3=pa3, ot=ot: h.tensor_scalar(
                INPT, pa3[:, 0:TSC], COLS[:, 16 + ot:17 + ot], None,
                ALU.add), deps=(tmm3, d1, d2))
            tvf = ve_flush(st)
            tsg = b.emit("se", lambda h: h.activation(
                SCR, INPT, AF.Sigmoid), deps=(tvf,), inc=1)
            tsf = se_flush(st, deps=(tsg,))
            tgate = b.emit("ve", lambda h: h.tensor_tensor(
                INPT, INPT, SCR, ALU.mult), deps=(tsf,), inc=1)
            b.bank_done(bk3, tgate)
            dout = b.dma("po", gate_hbm[ot * 128:(ot + 1) * 128,
                                        t0:t0 + TSC],
                         INPT, deps=(tgate,), stream="l2o")
    tcc = b.emit("po", lambda h: h.collective_compute(
        "AllGather", ALU.bypass, replica_groups=[list(range(8))],
        ins=[cc_in.ap().opt()], outs=[cc_out.ap().opt()]),
        deps=(dout,), inc=1, sem="s_cc")
    tsel = None
    for dt in range(NDT):
        for j in range(8):
            li = b.dma("sy", IJ,
                       cc_out[j * 2048 + dt * 128:
                              j * 2048 + dt * 128 + 128, :],
                       deps=(tcc, tsel), stream="l2sc")
            ll = b.dma("sy", LJ,
                       cc_out[j * 2048 + 1024 + dt * 128:
                              j * 2048 + 1024 + dt * 128 + 128, :],
                       deps=(li,), stream="l2sc")
            if j == 0:
                b.emit("ve", lambda h: h.tensor_copy(HH, IJ),
                       deps=(li, ll))
                tsel = b.emit("ve", lambda h, dt=dt: h.tensor_scalar(
                    BIG[:, dt * T:(dt + 1) * T], HH, COLS[:, 48:49],
                    None, ALU.mult), inc=1)
            else:
                b.emit("ve", lambda h: h.tensor_tensor(
                    HH, HH, LJ, ALU.mult), deps=(li, ll))
                b.emit("ve", lambda h: h.tensor_tensor(
                    HH, HH, IJ, ALU.add))
                tsel = b.emit("ve", lambda h, dt=dt, j=j:
                              h.scalar_tensor_tensor(
                                  BIG[:, dt * T:(dt + 1) * T], HH,
                                  COLS[:, 48 + j:49 + j],
                                  BIG[:, dt * T:(dt + 1) * T],
                                  ALU.mult, ALU.add), inc=1)
    tln = tsel
    for tc in range(NTC):
        t0 = tc * TSC
        bk_mu, rstd_ap, tstat = ln_stats_chunk(
            st, lambda dt, t0=t0: BIG[:, dt * T + t0:dt * T + t0 + TSC],
            deps=(tln,))
        pm = b.psum[bk_mu]
        for dt in range(NDT):
            sl = BIG[:, dt * T + t0:dt * T + t0 + TSC]
            lgt = b.dma("sy", GT[:, dt * TSC:(dt + 1) * TSC],
                        gate_hbm[dt * 128:(dt + 1) * 128, t0:t0 + TSC],
                        deps=(tln,), stream="l2sc")
            b.emit("ve", lambda h, sl=sl, pm=pm: h.tensor_tensor(
                sl, sl, pm[:, 0:TSC], ALU.subtract),
                deps=(tstat,) if dt == 0 else ())
            b.emit("ve", lambda h, sl=sl, rstd_ap=rstd_ap: h.tensor_tensor(
                sl, sl, rstd_ap, ALU.mult))
            b.emit("ve", lambda h, sl=sl, dt=dt: h.tensor_scalar(
                sl, sl, COLS[:, 32 + dt:33 + dt], COLS[:, 40 + dt:41 + dt],
                ALU.mult, ALU.add))
            tln = b.emit("ve", lambda h, sl=sl, dt=dt: h.tensor_tensor(
                sl, sl, GT[:, dt * TSC:(dt + 1) * TSC], ALU.mult),
                deps=(lgt,), inc=1)
        b.bank_done(bk_mu, tln)
        st["sq_war"][0] = tln
    out_tok = None
    wpt = None
    for tc in range(NTC):
        t0 = tc * TSC
        for ot in range(NDT):
            par = ot % 2
            wsl = WPAR[:, par * 1024:(par + 1) * 1024]
            lw = b.dma("sy", wsl.rearrange("p (kt m) -> p kt m", kt=NDT),
                       hbm["L2A_WoutT"][:, ot * 128:(ot + 1) * 128]
                       .rearrange("(kt p) m -> p kt m", p=128),
                       deps=(wpar_tok.get(par), tln), stream=f"l2aw{par}")
            bk, war = b.take_bank()
            pa = b.psum[bk]
            tmm = None
            for kt in range(NDT):
                d = war + ((lw, tln) if kt == 0 else ())
                tmm = b.emit("te", lambda h, kt=kt, pa=pa, wsl=wsl, t0=t0:
                             h.matmul(pa[:, 0:TSC],
                                      wsl[:, kt * 128:(kt + 1) * 128],
                                      BIG[:, kt * T + t0:kt * T + t0 + TSC],
                                      start=(kt == 0), stop=(kt == NDT - 1)),
                             deps=d, inc=1)
            wpar_tok[par] = tmm
            tcp = b.emit("ve", lambda h, pa=pa, ot=ot: h.tensor_scalar(
                OUTC[:, ot * TSC:(ot + 1) * TSC], pa[:, 0:TSC],
                COLS[:, 24 + ot:25 + ot], None, ALU.add),
                deps=(tmm,), inc=1)
            b.bank_done(bk, tcp)
        out_tok = postln_residual_chunk(
            st, tc, lambda ot: OUTC[:, ot * TSC:(ot + 1) * TSC], (tcp,))
    return out_tok


def emit_l0_attn(st, deps):
    b = st["b"]
    nc = st["nc"]
    cn = st["cn"]
    hbm = st["hbm"]
    XT = st["XT"]
    BIG = st["BIG"]
    GEN = st["GEN"]
    C = 32
    NCH = T // C

    QH = GEN[:, 0:4096]
    KH = GEN[:, 4096:8192]
    VH = GEN[:, 8192:12288]
    BB = GEN[:, 12288:14336]
    SS = GEN[:, 14336:14848]
    RM = GEN[:, 14848:15040]
    TMV = GEN[:, 15040:16320]
    WFM = GEN[:, 16320:16384]
    KBC = GEN[:, 16384:16448]
    CV = GEN[:, 16448:16544]
    TRI = GEN[0:32, 16544:16640]
    NW = st["CONSTS_T"][:, 417:425]
    EYE32 = cn["ident"][0:32, 0:32]
    assert GENW - 2 * TSC >= 16640

    q_hbm = nc.dram_tensor("l0_q", [1024, T], F32)
    k_hbm = nc.dram_tensor("l0_k", [1024, T], F32)
    v_hbm = nc.dram_tensor("l0_v", [1024, T], F32)
    bm_hbm = nc.dram_tensor("l0_bm", [8, T], F32)
    o_hbm = nc.dram_tensor("l0_o", [T, 1024], F32)

    tg = b.dma("sy", NW, hbm["L0_normwT"].ap(), stream="l0c", deps=deps)
    tg2 = b.dma("sy", CV, hbm["L0_convT"].ap(), stream="l0c", deps=(tg,))
    tg3 = b.dma("sy", TRI, hbm["L0_TRIL"].ap(), stream="l0c", deps=(tg2,))

    tk = None
    for tc in range(NTC):
        tk = preln_chunk(
            st, tc, lambda dt, tc=tc: BIG[:, dt * T + tc * TSC:
                                          dt * T + tc * TSC + TSC],
            deps=deps if tc == 0 else ())
    txh = tk

    wts = (("q", hbm["L0_WqT"], q_hbm), ("k", hbm["L0_WkT"], k_hbm),
           ("v", hbm["L0_WvT"], v_hbm))
    wpt = {}
    dout = None
    for wi, (nm, wt, dst) in enumerate(wts):
        for ot in range(NDT):
            par = ot % 2
            wsl = (QH if par == 0 else KH)[:, 0:1024]
            lw = b.dma("sy", wsl.rearrange("p (kt m) -> p kt m", kt=NDT),
                       wt[:, ot * 128:(ot + 1) * 128].rearrange(
                           "(kt p) m -> p kt m", p=128),
                       deps=(wpt.get(par), txh), stream=f"l0w{par}")
            for tc in range(NTC):
                t0 = tc * TSC
                bk, war = b.take_bank()
                pa = b.psum[bk]
                tmm = None
                for kt in range(NDT):
                    d = war + ((lw,) if kt == 0 else ())
                    tmm = b.emit("te", lambda h, kt=kt, pa=pa, wsl=wsl,
                                 t0=t0: h.matmul(
                                     pa[:, 0:TSC],
                                     wsl[:, kt * 128:(kt + 1) * 128],
                                     BIG[:, kt * T + t0:kt * T + t0 + TSC],
                                     start=(kt == 0), stop=(kt == NDT - 1)),
                                 deps=d, inc=1)
                wpt[par] = tmm
                tcp = b.emit("ve", lambda h, pa=pa: h.tensor_copy(
                    VH[:, 0:TSC], pa[:, 0:TSC]), deps=(tmm, dout))
                b.bank_done(bk, tcp)
                dout = b.dma("po", dst[ot * 128:(ot + 1) * 128,
                                       t0:t0 + TSC], VH[:, 0:TSC],
                             deps=(tcp,), stream="l0o")
    lwb = b.dma("sy", QH[:, 0:1024].rearrange("p (kt m) -> p kt m", kt=NDT),
                hbm["L0_WbmT"].ap().rearrange("(kt p) m -> p kt m", p=128),
                deps=(wpt.get(0), txh), stream="l0w0")
    for tc in range(NTC):
        t0 = tc * TSC
        bk, war = b.take_bank()
        pa = b.psum[bk]
        tmm = None
        for kt in range(NDT):
            d = war + ((lwb,) if kt == 0 else ())
            tmm = b.emit("te", lambda h, kt=kt, pa=pa, t0=t0: h.matmul(
                pa[:, 0:TSC], QH[:, kt * 128:(kt + 1) * 128],
                BIG[:, kt * T + t0:kt * T + t0 + TSC],
                start=(kt == 0), stop=(kt == NDT - 1)), deps=d, inc=1)
        tcp = b.emit("ve", lambda h, pa=pa: h.tensor_copy(
            VH[:, 0:TSC], pa[:, 0:TSC]), deps=(tmm, dout))
        tvf = ve_flush(st)
        tsg = b.emit("se", lambda h: h.activation(
            VH[:, 0:TSC], VH[:, 0:TSC], AF.Sigmoid), deps=(tvf,), inc=1)
        tsf = se_flush(st, deps=(tsg,))
        b.bank_done(bk, tsf)
        dout = b.dma("po", bm_hbm[0:8, t0:t0 + TSC], VH[0:8, 0:TSC],
                     deps=(tsf,), stream="l0o")
    t_mm = dout

    tprev = t_mm
    for wi, (nm, wt, dst) in enumerate(wts):
        for dt in range(NDT):
            tld = b.dma("sy", QH[:, 0:T], dst[dt * 128:(dt + 1) * 128, :],
                        deps=(tprev,), stream="l0cv")
            cb = 32 * wi + dt * 4
            tu = b.emit("ve", lambda h, cb=cb: h.tensor_scalar(
                KH[:, 0:T], QH[:, 0:T], CV[:, cb + 3:cb + 4], None,
                ALU.mult), deps=(tld, tg2))
            for k in range(3):
                w_ = 3 - k
                tu = b.emit("ve", lambda h, cb=cb, k=k, w_=w_:
                            h.scalar_tensor_tensor(
                                KH[:, w_:T], QH[:, 0:T - w_],
                                CV[:, cb + k:cb + k + 1],
                                KH[:, w_:T], ALU.mult, ALU.add))
            tvf = ve_flush(st, deps=(tu,))
            tsg = b.emit("se", lambda h: h.activation(
                VH[:, 0:T], KH[:, 0:T], AF.Sigmoid), deps=(tvf,), inc=1)
            tsf = se_flush(st, deps=(tsg,))
            tu2 = b.emit("ve", lambda h: h.tensor_tensor(
                KH[:, 0:T], KH[:, 0:T], VH[:, 0:T], ALU.mult), deps=(tsf,))
            tprev = b.dma("po", dst[dt * 128:(dt + 1) * 128, :], KH[:, 0:T],
                          deps=(tu2,), stream="l0o")
    t_cv = tprev

    def l2norm_pair(buf, eps, dep0):
        tok = dep0
        for tc in range(NTC):
            t0 = tc * TSC
            bk1, war1 = b.take_bank()
            p1 = b.psum[bk1]
            tqs = None
            for sub in range(2):
                tq_ = b.emit("se", lambda h, sub=sub, t0=t0: h.activation(
                    st["SQ"][:, sub * TSC:(sub + 1) * TSC],
                    buf[:, sub * T + t0:sub * T + t0 + TSC], AF.Square),
                    deps=(tok, st["sq_war"][sub]) if sub == 0 else
                    (st["sq_war"][sub],), inc=1)
                tqs = se_flush(st, deps=(tq_,))
            for sub in range(2):
                tmm = b.emit("te", lambda h, sub=sub, p1=p1: h.matmul(
                    p1[:, 0:TSC], cn["ones128"],
                    st["SQ"][:, sub * TSC:(sub + 1) * TSC],
                    start=(sub == 0), stop=(sub == 1)),
                    deps=(tqs,) + (war1 if sub == 0 else ()), inc=1)
            varb = st["SQ"][:, TSC:2 * TSC]
            tv = b.emit("ve", lambda h, p1=p1: h.tensor_scalar(
                varb, p1[:, 0:TSC], 1.0, float(eps), ALU.mult, ALU.add),
                deps=(tmm,))
            trc = b.emit("ve", lambda h: h.reciprocal(varb, varb), inc=1)
            tsq = b.emit("se", lambda h: h.activation(
                st["SQ"][:, 0:TSC], varb, AF.Sqrt), deps=(trc,), inc=1)
            tsf = se_flush(st, deps=(tsq,))
            st["sq_war"][1] = tsf
            for sub in range(2):
                tok = b.emit("ve", lambda h, sub=sub, t0=t0: h.tensor_tensor(
                    buf[:, sub * T + t0:sub * T + t0 + TSC],
                    buf[:, sub * T + t0:sub * T + t0 + TSC],
                    st["SQ"][:, 0:TSC], ALU.mult), deps=(tsf,), inc=1)
            st["sq_war"][0] = tok
            b.bank_done(bk1, tok)
        return tok

    BR = st["RRT"][0:1, 128:128 + T]
    tdone = t_cv
    for hd in range(4):
        tq1 = b.dma("sy", QH[:, 0:T], q_hbm[hd * 256:hd * 256 + 128, :],
                    deps=(tdone,), stream="l0cv")
        tq2 = b.dma("sy", QH[:, T:2 * T],
                    q_hbm[hd * 256 + 128:hd * 256 + 256, :],
                    deps=(tq1,), stream="l0cv")
        tk1 = b.dma("sy", KH[:, 0:T], k_hbm[hd * 256:hd * 256 + 128, :],
                    deps=(tq2,), stream="l0cv")
        tk2 = b.dma("sy", KH[:, T:2 * T],
                    k_hbm[hd * 256 + 128:hd * 256 + 256, :],
                    deps=(tk1,), stream="l0cv")
        tv1 = b.dma("sy", VH[:, 0:T], v_hbm[hd * 256:hd * 256 + 128, :],
                    deps=(tk2,), stream="l0cv")
        tv2 = b.dma("sy", VH[:, T:2 * T],
                    v_hbm[hd * 256 + 128:hd * 256 + 256, :],
                    deps=(tv1,), stream="l0cv")
        tqn = l2norm_pair(QH, 1e-06, tq2)
        tkn = l2norm_pair(KH, 1e-06, tk2)
        tbr = b.dma("sy", BR, bm_hbm[hd:hd + 1, :], deps=(t_mm,),
                    stream="l0br")
        tbb = None
        for tc in range(NTC):
            t0 = tc * TSC
            bk, war = b.take_bank()
            pb = b.psum[bk]
            tb_ = b.emit("te", lambda h, pb=pb, t0=t0: h.matmul(
                pb[:, 0:TSC], cn["ones_row"], BR[0:1, t0:t0 + TSC],
                start=True, stop=True), deps=war + (tbr,), inc=1)
            tbb = b.emit("ve", lambda h, pb=pb, t0=t0: h.tensor_copy(
                BB[:, t0:t0 + TSC], pb[:, 0:TSC]), deps=(tb_,), inc=1)
            b.bank_done(bk, tbb)
        tvb = None
        for sub in range(2):
            tvb = b.emit("ve", lambda h, sub=sub: h.tensor_tensor(
                VH[:, sub * T:(sub + 1) * T], VH[:, sub * T:(sub + 1) * T],
                BB[:, 0:T], ALU.mult), deps=(tbb, tv2), inc=1)
        ts0 = b.emit("ve", lambda h: h.memset(SS, 0.0), inc=1)
        AT = RM[0:32, 0:32]
        ANT = RM[0:32, 32:64]
        R = RM[0:32, 64:96]
        RT = RM[0:32, 96:128]
        N2 = RM[0:32, 128:160]
        NT2 = RM[0:32, 160:192]
        VTM = TMV[0:32, 0:256]
        KBTM = TMV[0:32, 256:512]
        KTM = TMV[0:32, 512:768]
        UNEW = TMV[0:32, 768:1024]
        OTM = TMV[0:32, 1024:1280]
        tlast = (tqn, tkn, tvb, ts0, tg3)
        for ci in range(NCH):
            c0 = ci * C
            tkb = None
            for sub in range(2):
                tkb = b.emit("ve", lambda h, sub=sub, c0=c0:
                             h.tensor_tensor(
                                 KBC[:, sub * 32:(sub + 1) * 32],
                                 KH[:, sub * T + c0:sub * T + c0 + C],
                                 BB[:, c0:c0 + C], ALU.mult),
                             deps=tlast if ci == 0 else (), inc=1)
            trs = []
            for sub in range(2):
                for src, dst_ in ((VH[:, sub * T + c0:sub * T + c0 + C],
                                   VTM[0:32, sub * 128:(sub + 1) * 128]),
                                  (KBC[:, sub * 32:(sub + 1) * 32],
                                   KBTM[0:32, sub * 128:(sub + 1) * 128]),
                                  (KH[:, sub * T + c0:sub * T + c0 + C],
                                   KTM[0:32, sub * 128:(sub + 1) * 128])):
                    bk, war = b.take_bank()
                    pt = b.psum[bk]
                    tt = b.emit("te", lambda h, src=src, pt=pt: h.transpose(
                        pt[0:32, 0:128], src, cn["ident"]),
                        deps=war + ((tkb,) if not trs else ()), inc=1)
                    tcpy = b.emit("ve", lambda h, pt=pt, dst_=dst_:
                                  h.tensor_copy(dst_, pt[0:32, 0:128]),
                                  deps=(tt,), inc=1)
                    b.bank_done(bk, tcpy)
                    trs.append(tcpy)
            ttr = trs[-1]
            bk, war = b.take_bank()
            pa = b.psum[bk]
            for sub in range(2):
                tmm = b.emit("te", lambda h, sub=sub, pa=pa, c0=c0:
                             h.matmul(pa[0:32, 0:32],
                                      KBC[:, sub * 32:(sub + 1) * 32],
                                      KH[:, sub * T + c0:sub * T + c0 + C],
                                      start=(sub == 0), stop=(sub == 1)),
                             deps=war + ((tkb,) if sub == 0 else ()), inc=1)
            tat = b.emit("ve", lambda h, pa=pa: h.tensor_tensor(
                AT, pa[0:32, 0:32], TRI[0:32, 0:32], ALU.mult), deps=(tmm,))
            tat = b.emit("ve", lambda h: h.tensor_scalar(
                AT, AT, -1.0, None, ALU.mult), inc=1)
            b.bank_done(bk, tat)
            bk, war = b.take_bank()
            pa2 = b.psum[bk]
            for sub in range(2):
                tmm = b.emit("te", lambda h, sub=sub, pa2=pa2, c0=c0:
                             h.matmul(pa2[0:32, 0:32],
                                      KH[:, sub * T + c0:sub * T + c0 + C],
                                      KBC[:, sub * 32:(sub + 1) * 32],
                                      start=(sub == 0), stop=(sub == 1)),
                             deps=war + ((tkb,) if sub == 0 else ()), inc=1)
            tant = b.emit("ve", lambda h, pa2=pa2: h.tensor_tensor(
                ANT, pa2[0:32, 0:32], TRI[0:32, 64:96], ALU.mult),
                deps=(tmm,))
            tant = b.emit("ve", lambda h: h.tensor_scalar(
                ANT, ANT, -1.0, None, ALU.mult), inc=1)
            b.bank_done(bk, tant)
            b.emit("ve", lambda h: h.tensor_tensor(
                R, AT, EYE32, ALU.add), deps=(tat, tant))
            tr_ = b.emit("ve", lambda h: h.tensor_tensor(
                RT, ANT, EYE32, ALU.add), inc=1)
            cur, curT = AT, ANT
            for lv in range(4):
                bk, war = b.take_bank()
                p1 = b.psum[bk]
                t1 = b.emit("te", lambda h, p1=p1, cur=cur, curT=curT:
                            h.matmul(p1[0:32, 0:32], curT, cur,
                                     start=True, stop=True),
                            deps=war + (tr_,), inc=1)
                bk2, war2 = b.take_bank()
                p2 = b.psum[bk2]
                t2_ = b.emit("te", lambda h, p2=p2, cur=cur, curT=curT:
                             h.matmul(p2[0:32, 0:32], cur, curT,
                                      start=True, stop=True),
                             deps=war2, inc=1)
                tn2 = b.emit("ve", lambda h, p1=p1: h.tensor_copy(
                    N2, p1[0:32, 0:32]), deps=(t1,))
                tnt2 = b.emit("ve", lambda h, p2=p2: h.tensor_copy(
                    NT2, p2[0:32, 0:32]), deps=(t2_,), inc=1)
                b.bank_done(bk, tnt2)
                b.bank_done(bk2, tnt2)
                bk, war = b.take_bank()
                p1 = b.psum[bk]
                t1 = b.emit("te", lambda h, p1=p1: h.matmul(
                    p1[0:32, 0:32], NT2, R, start=True, stop=True),
                    deps=war + (tnt2,), inc=1)
                bk2, war2 = b.take_bank()
                p2 = b.psum[bk2]
                t2_ = b.emit("te", lambda h, p2=p2: h.matmul(
                    p2[0:32, 0:32], R, NT2, start=True, stop=True),
                    deps=war2, inc=1)
                b.emit("ve", lambda h, p1=p1: h.tensor_tensor(
                    R, R, p1[0:32, 0:32], ALU.add), deps=(t1, t2_))
                tr_ = b.emit("ve", lambda h, p2=p2: h.tensor_tensor(
                    RT, RT, p2[0:32, 0:32], ALU.add), inc=1)
                b.bank_done(bk, tr_)
                b.bank_done(bk2, tr_)
                cur, curT = N2, NT2
            bk, war = b.take_bank()
            pw = b.psum[bk]
            tw1 = b.emit("te", lambda h, pw=pw: h.matmul(
                pw[:, 0:32], KBTM[0:32, 0:128], RT, start=True, stop=True),
                deps=war + (tr_, ttr), inc=1)
            tw2 = b.emit("te", lambda h, pw=pw: h.matmul(
                pw[:, 32:64], KBTM[0:32, 128:256], RT, start=True,
                stop=True, skip_group_check=True), deps=(), inc=1)
            twf = b.emit("ve", lambda h, pw=pw: h.tensor_scalar(
                WFM, pw[:, 0:64], -1.0, None, ALU.mult),
                deps=(tw1, tw2), inc=1)
            b.bank_done(bk, twf)
            bk, war = b.take_bank()
            pu = b.psum[bk]
            tmm = b.emit("te", lambda h, pu=pu: h.matmul(
                pu[0:32, 0:256], RT, VTM, start=True, stop=False),
                deps=war + (twf,), inc=1)
            for sub in range(2):
                tmm = b.emit("te", lambda h, pu=pu, sub=sub: h.matmul(
                    pu[0:32, 0:256], WFM[:, sub * 32:(sub + 1) * 32],
                    SS[:, sub * 256:(sub + 1) * 256],
                    start=False, stop=(sub == 1)), deps=(), inc=1)
            tun = b.emit("ve", lambda h, pu=pu: h.tensor_copy(
                UNEW, pu[0:32, 0:256]), deps=(tmm,), inc=1)
            b.bank_done(bk, tun)
            bk, war = b.take_bank()
            pq = b.psum[bk]
            for sub in range(2):
                tmm = b.emit("te", lambda h, pq=pq, sub=sub, c0=c0:
                             h.matmul(pq[0:32, 0:32],
                                      KH[:, sub * T + c0:sub * T + c0 + C],
                                      QH[:, sub * T + c0:sub * T + c0 + C],
                                      start=(sub == 0), stop=(sub == 1)),
                             deps=war, inc=1)
            taq = b.emit("ve", lambda h, pq=pq: h.tensor_tensor(
                N2, pq[0:32, 0:32], TRI[0:32, 64:96], ALU.mult),
                deps=(tmm,))
            taq = b.emit("ve", lambda h, pq=pq: h.tensor_tensor(
                NT2, pq[0:32, 0:32], EYE32, ALU.mult))
            taq = b.emit("ve", lambda h: h.tensor_tensor(
                N2, N2, NT2, ALU.add), inc=1)
            b.bank_done(bk, taq)
            bk, war = b.take_bank()
            po = b.psum[bk]
            tmm = None
            for sub in range(2):
                tmm = b.emit("te", lambda h, po=po, sub=sub, c0=c0:
                             h.matmul(po[0:32, 0:256],
                                      QH[:, sub * T + c0:sub * T + c0 + C],
                                      SS[:, sub * 256:(sub + 1) * 256],
                                      start=(sub == 0), stop=False),
                             deps=war + ((tun, taq) if sub == 0 else ()),
                             inc=1)
            tmm = b.emit("te", lambda h, po=po: h.matmul(
                po[0:32, 0:256], N2, UNEW, start=False, stop=True),
                deps=(), inc=1)
            tot = b.emit("ve", lambda h, po=po: h.tensor_copy(
                OTM, po[0:32, 0:256]), deps=(tmm,), inc=1)
            b.bank_done(bk, tot)
            tdo = b.dma("po", o_hbm[c0:c0 + C, hd * 256:hd * 256 + 256],
                        OTM, deps=(tot,), stream="l0o")
            tsu = None
            for sub in range(2):
                bk, war = b.take_bank()
                ps = b.psum[bk]
                tmm = b.emit("te", lambda h, ps=ps, sub=sub: h.matmul(
                    ps[:, 0:256], KTM[0:32, sub * 128:(sub + 1) * 128],
                    UNEW, start=True, stop=True),
                    deps=war + (tun, tot), inc=1)
                tsu = b.emit("ve", lambda h, ps=ps, sub=sub: h.tensor_tensor(
                    SS[:, sub * 256:(sub + 1) * 256],
                    SS[:, sub * 256:(sub + 1) * 256], ps[:, 0:256],
                    ALU.add), deps=(tmm,), inc=1)
                b.bank_done(bk, tsu)
            tlast = (tsu, tdo)
        tdone = tsu

    trl = tdone
    for ft in range(NDT):
        for tt16 in range(16):
            r0 = tt16 * 128
            tld = b.dma("sy", QH[:, 0:128],
                        o_hbm[r0:r0 + 128, ft * 128:(ft + 1) * 128],
                        deps=(trl,), stream="l0cv")
            bk, war = b.take_bank()
            pt = b.psum[bk]
            ttp = b.emit("te", lambda h, pt=pt: h.transpose(
                pt[:, 0:128], QH[:, 0:128], cn["ident"]),
                deps=war + (tld,), inc=1)
            trl = b.emit("ve", lambda h, pt=pt, ft=ft, r0=r0: h.tensor_copy(
                BIG[:, ft * T + r0:ft * T + r0 + 128], pt[:, 0:128]),
                deps=(ttp,), inc=1)
            b.bank_done(bk, trl)
    t_ofm = trl

    tmixdone = t_ofm
    for hd in range(4):
        tbr = b.dma("sy", BR, bm_hbm[4 + hd:5 + hd, :], deps=(tmixdone,),
                    stream="l0br")
        tbb = None
        for tc in range(NTC):
            t0 = tc * TSC
            bk, war = b.take_bank()
            pb = b.psum[bk]
            tb_ = b.emit("te", lambda h, pb=pb, t0=t0: h.matmul(
                pb[:, 0:TSC], cn["ones_row"], BR[0:1, t0:t0 + TSC],
                start=True, stop=True), deps=war + (tbr,), inc=1)
            tbb = b.emit("ve", lambda h, pb=pb, t0=t0: h.tensor_copy(
                BB[:, t0:t0 + TSC], pb[:, 0:TSC]), deps=(tb_,), inc=1)
            b.bank_done(bk, tbb)
        for sub in range(2):
            dt = 2 * hd + sub
            tld = b.dma("sy", QH[:, 0:T], v_hbm[dt * 128:(dt + 1) * 128, :],
                        deps=(tmixdone,), stream="l0cv")
            sl = BIG[:, dt * T:(dt + 1) * T]
            b.emit("ve", lambda h, sl=sl: h.tensor_tensor(
                sl, sl, QH[:, 0:T], ALU.subtract), deps=(tld, tbb))
            b.emit("ve", lambda h, sl=sl: h.tensor_tensor(
                sl, sl, BB[:, 0:T], ALU.mult))
            tmixdone = b.emit("ve", lambda h, sl=sl: h.tensor_tensor(
                sl, sl, QH[:, 0:T], ALU.add), inc=1)
        for tc in range(NTC):
            t0 = tc * TSC
            bk1, war1 = b.take_bank()
            p1 = b.psum[bk1]
            tqs = None
            for sub in range(2):
                dt = 2 * hd + sub
                tq_ = b.emit("se", lambda h, sub=sub, dt=dt, t0=t0:
                             h.activation(
                                 st["SQ"][:, sub * TSC:(sub + 1) * TSC],
                                 BIG[:, dt * T + t0:dt * T + t0 + TSC],
                                 AF.Square),
                             deps=(tmixdone, st["sq_war"][sub]) if sub == 0
                             else (st["sq_war"][sub],), inc=1)
                tqs = se_flush(st, deps=(tq_,))
            for sub in range(2):
                tmm = b.emit("te", lambda h, sub=sub, p1=p1: h.matmul(
                    p1[:, 0:TSC], cn["ones128"],
                    st["SQ"][:, sub * TSC:(sub + 1) * TSC],
                    start=(sub == 0), stop=(sub == 1)),
                    deps=(tqs,) + (war1 if sub == 0 else ()), inc=1)
            varb = st["SQ"][:, TSC:2 * TSC]
            tv = b.emit("ve", lambda h, p1=p1: h.tensor_scalar(
                varb, p1[:, 0:TSC], 1.0 / 256.0, 1e-05, ALU.mult, ALU.add),
                deps=(tmm,))
            trc = b.emit("ve", lambda h: h.reciprocal(varb, varb), inc=1)
            tsq = b.emit("se", lambda h: h.activation(
                st["SQ"][:, 0:TSC], varb, AF.Sqrt), deps=(trc,), inc=1)
            tsf = se_flush(st, deps=(tsq,))
            st["sq_war"][1] = tsf
            for sub in range(2):
                dt = 2 * hd + sub
                b.emit("ve", lambda h, dt=dt, t0=t0: h.tensor_tensor(
                    BIG[:, dt * T + t0:dt * T + t0 + TSC],
                    BIG[:, dt * T + t0:dt * T + t0 + TSC],
                    st["SQ"][:, 0:TSC], ALU.mult), deps=(tsf,))
                tmixdone = b.emit("ve", lambda h, dt=dt, t0=t0:
                                  h.tensor_scalar(
                                      BIG[:, dt * T + t0:dt * T + t0 + TSC],
                                      BIG[:, dt * T + t0:dt * T + t0 + TSC],
                                      NW[:, dt:dt + 1], None, ALU.mult),
                                  deps=(tg,), inc=1)
            st["sq_war"][0] = tmixdone
            b.bank_done(bk1, tmixdone)

    out_tok = None
    wpt2 = {}
    for tc in range(NTC):
        t0 = tc * TSC
        for ot in range(NDT):
            par = ot % 2
            wsl = (QH if par == 0 else KH)[:, 0:1024]
            lw = b.dma("sy", wsl.rearrange("p (kt m) -> p kt m", kt=NDT),
                       hbm["L0_WoT"][:, ot * 128:(ot + 1) * 128].rearrange(
                           "(kt p) m -> p kt m", p=128),
                       deps=(wpt2.get(par), tmixdone), stream=f"l0w{par}")
            bk, war = b.take_bank()
            pa = b.psum[bk]
            tmm = None
            for kt in range(NDT):
                d = war + ((lw, tmixdone) if kt == 0 else ())
                tmm = b.emit("te", lambda h, kt=kt, pa=pa, wsl=wsl, t0=t0:
                             h.matmul(pa[:, 0:TSC],
                                      wsl[:, kt * 128:(kt + 1) * 128],
                                      BIG[:, kt * T + t0:kt * T + t0 + TSC],
                                      start=(kt == 0), stop=(kt == NDT - 1)),
                             deps=d, inc=1)
            wpt2[par] = tmm
            tcp = b.emit("ve", lambda h, pa=pa, ot=ot: h.tensor_copy(
                VH[:, ot * TSC:(ot + 1) * TSC], pa[:, 0:TSC]),
                deps=(tmm,), inc=1)
            b.bank_done(bk, tcp)
        out_tok = postln_residual_chunk(
            st, tc, lambda ot: VH[:, ot * TSC:(ot + 1) * TSC], (tcp,))
    return out_tok



def build_program(wshapes, stages):
    nc = bass.Bass("TRN2", target_bir_lowering=False)
    es = ExitStack()
    es.__enter__()
    b = B(nc, es)

    hbm = {}
    for name, (shape, dt) in wshapes.items():
        hbm[name] = nc.dram_tensor(name, list(shape), dt,
                                   kind="ExternalInput")
    out_x = nc.dram_tensor("out_x", [D, T], F32, kind="ExternalOutput")

    block = es.enter_context(nc.Block())

    XT = b.sb("XT", [128, NDT * T])
    BIG = b.sb("BIG", [128, NDT * T])
    GEN = b.sb("GEN", [128, GENW])
    CONSTS = b.sb("CONSTS", [128, 512])
    RR = b.sb("RR", [1, 2176])
    SEDUMMY = RR[0:1, 0:64]
    VEDUMMY = RR[0:1, 64:128]

    cn = {
        "ident": CONSTS[:, 0:128],
        "ones128": CONSTS[:, 128:256],
        "ones_col": CONSTS[:, 128:129],
        "chi64": CONSTS[:, 256:257],
        "chi32": CONSTS[:, 320:321],
        "ones_row": CONSTS[0:1, 128:256],
    }
    chi64w = CONSTS[:, 256:320]
    chi32w = CONSTS[:, 320:384]

    tok_x = b.dma("sy", XT[:, :].rearrange("p (dt t) -> p dt t", dt=NDT),
                  hbm["x_fm"][:, :].rearrange("(dt p) t -> p dt t", p=128),
                  stream="xload")
    tok_c = b.dma("sy", CONSTS[:, 0:512], hbm["consts"][:, :],
                  stream="consts")
    b.emit("ve", lambda h: h.tensor_scalar(chi64w, chi64w, C2, C2,
                                           ALU.add, ALU.subtract),
           deps=(tok_c,))
    tok_chi = b.emit("ve", lambda h: h.tensor_scalar(
        chi32w, chi64w, -1.0, 1.0, ALU.mult, ALU.add), inc=1)

    st = {"b": b, "nc": nc, "hbm": hbm, "cn": cn, "CONSTS_T": CONSTS,
          "XT": XT, "BIG": BIG, "GEN": GEN,
          "SEDUMMY": SEDUMMY, "VEDUMMY": VEDUMMY, "RRT": RR,
          "MU": RR[0:1, TSC:2 * TSC], "RS": RR[0:1, 2 * TSC:3 * TSC],
          "ROWSCR": RR[0:1, 3 * TSC:3 * TSC + 256],
          "ROWSCR2": RR[0:1, 3 * TSC + 256:3 * TSC + 512],
          "SQ": GEN[:, GENW - 2 * TSC:GENW],
          "sq_war": [None, None], "rows_war": None, "rs_war": None,
          "mu_war": None}

    tok_last = (tok_x, tok_chi)

    for stg in stages:
        if stg == "stats_dbg":
            st["dbg_sq2"] = GEN[:, 0:TSC]
            st["dbg_var"] = GEN[:, TSC:2 * TSC]
            st["dbg_musq"] = GEN[:, 2 * TSC:3 * TSC]
            tk = None
            for tc in range(NTC):
                t0 = tc * TSC
                bk_mu, rstd_ap, tstat = ln_stats_chunk(
                    st, lambda dt, t0=t0: XT[:, dt * T + t0:dt * T + t0 + TSC],
                    deps=tok_last if tc == 0 else ())
                pm = b.psum[bk_mu]
                for dt in range(NDT):
                    if dt % 2 == 0:
                        tk = b.emit("ve", lambda h, dt=dt, t0=t0, pm=pm:
                                    h.tensor_copy(
                                        st["BIG"][:, dt * T + t0:
                                                  dt * T + t0 + TSC],
                                        pm[:, 0:TSC]),
                                    deps=(tstat,) if dt == 0 else (), inc=1)
                    elif dt == 5:
                        tk = b.emit("ve", lambda h, dt=dt, t0=t0:
                                    h.tensor_copy(
                                        st["BIG"][:, dt * T + t0:
                                                  dt * T + t0 + TSC],
                                        st["dbg_var"]),
                                    deps=(), inc=1)
                    elif dt == 7:
                        tk = b.emit("ve", lambda h, dt=dt, t0=t0:
                                    h.tensor_copy(
                                        st["BIG"][:, dt * T + t0:
                                                  dt * T + t0 + TSC],
                                        st["dbg_musq"]),
                                    deps=(), inc=1)
                    elif dt == 3:
                        tk = b.emit("ve", lambda h, dt=dt, t0=t0:
                                    h.tensor_copy(
                                        st["BIG"][:, dt * T + t0:
                                                  dt * T + t0 + TSC],
                                        st["dbg_sq2"]),
                                    deps=(), inc=1)
                    else:
                        tk = b.emit("ve", lambda h, dt=dt, t0=t0,
                                    rstd_ap=rstd_ap:
                                    h.tensor_copy(
                                        st["BIG"][:, dt * T + t0:
                                                  dt * T + t0 + TSC],
                                        rstd_ap),
                                    deps=(), inc=1)
                b.bank_done(bk_mu, tk)
                st["sq_war"][1] = tk
            cp = None
            for dt in range(NDT):
                cp = b.emit("ve", lambda h, dt=dt: h.tensor_copy(
                    XT[:, dt * T:(dt + 1) * T],
                    st["BIG"][:, dt * T:(dt + 1) * T]),
                    deps=(tk,) if dt == 0 else (), inc=1)
            tok_last = (cp,)
        elif stg == "preln_only":
            tk = None
            for tc in range(NTC):
                tk = preln_chunk(
                    st, tc,
                    lambda dt, tc=tc: st["BIG"][:, dt * T + tc * TSC:
                                                dt * T + tc * TSC + TSC],
                    deps=tok_last if tc == 0 else ())
            cp = None
            for dt in range(NDT):
                cp = b.emit("ve", lambda h, dt=dt: h.tensor_copy(
                    XT[:, dt * T:(dt + 1) * T],
                    st["BIG"][:, dt * T:(dt + 1) * T]),
                    deps=(tk,) if dt == 0 else (), inc=1)
            tok_last = (cp,)
        elif stg == "l0ffn":
            tok_last = (sandwich_pwl_ffn(st, "L0_ffn", tok_last),)
        elif stg == "l1ffn":
            tok_last = (sandwich_pwl_ffn(st, "L1_ffn", tok_last),)
        elif stg == "l1attn":
            tok_last = (emit_l1_attn(st, tok_last),)
        elif stg == "l0attn":
            tok_last = (emit_l0_attn(st, tok_last),)
        elif stg == "l2attn":
            tok_last = (emit_l2_attn(st, tok_last),)
        elif stg == "l3attn":
            tok_last = (emit_l3_attn(st, tok_last),)
        elif stg == "l2ffn":
            tok_last = (emit_l2_ffn(st, tok_last),)
        elif stg == "l3ffn":
            tok_last = (emit_l3_ffn(st, tok_last),)
        else:
            raise ValueError(stg)

    stok = b.dma("po", out_x[:, :].rearrange("(dt p) t -> p dt t", p=128),
                 XT[:, :].rearrange("p (dt t) -> p dt t", dt=NDT),
                 deps=tok_last, stream="out")
    b.emit("po", lambda h: h.engine_nop(), deps=(stok,))

    b.finalize(block)
    return nc



def _consts_np():
    c = np.zeros((128, 512), np.float32)
    c[:, 0:128] = np.eye(128, dtype=np.float32)
    c[:, 128:256] = 1.0
    c[:, 256:320] = 0.75
    assert len(_CONST_VALS) <= 33
    for i, v in enumerate(_CONST_VALS):
        c[:, CONST_BASE + i] = v
    c[:, 480:512] = np.float32(1.0 / 7.5)
    return c


def _default_stages():
    return ["l0attn", "l0ffn", "l1attn", "l1ffn",
            "l2attn", "l2ffn", "l3attn", "l3ffn"]


def kernel(**inputs):
    global _CONST_VALS
    _CONST_VALS = []
    stages = _STAGES if _STAGES is not None else _default_stages()
    x = np.ascontiguousarray(inputs["L0_attn_hidden_states"])

    wshapes = {"x_fm": ((D, T), F32), "consts": ((128, 512), F32)}
    shared = {}

    def addw(name, arr):
        arr = np.ascontiguousarray(np.asarray(arr, np.float32))
        wshapes[name] = (arr.shape, F32)
        shared[name] = arr

    if "l0ffn" in stages:
        for sfx in ("w1", "b1", "w2", "b2"):
            addw(f"L0_ffn_{sfx}", inputs[f"L0_ffn_{sfx}"])
    if "l1ffn" in stages:
        for sfx in ("w1", "b1", "w2", "b2"):
            addw(f"L1_ffn_{sfx}", inputs[f"L1_ffn_{sfx}"])
    if "l0attn" in stages:
        for nm in ("Wq", "Wk", "Wv"):
            addw(f"L0_{nm}T",
                 np.asarray(inputs[f"L0_attn_{nm}"], np.float32).T)
        addw("L0_WoT", np.asarray(inputs["L0_attn_Wo"], np.float32).T)
        wbm = np.zeros((1024, 128), np.float32)
        wbm[:, 0:4] = np.asarray(inputs["L0_attn_Wb"], np.float32).T
        wbm[:, 4:8] = np.asarray(inputs["L0_attn_Wmix"], np.float32).T
        addw("L0_WbmT", wbm)
        cv = np.zeros((128, 96), np.float32)
        for wi, nm in enumerate(("conv_q", "conv_k", "conv_v")):
            cw = np.asarray(inputs[f"L0_attn_{nm}"],
                            np.float32).reshape(8, 128, 4)
            cv[:, 32 * wi:32 * wi + 32] = cw.transpose(1, 0, 2).reshape(
                128, 32)
        addw("L0_convT", cv)
        tri = np.zeros((32, 96), np.float32)
        tri[:, 0:32] = np.tril(np.ones((32, 32)), -1)
        tri[:, 32:64] = np.tril(np.ones((32, 32)), 0)
        tri[:, 64:96] = np.triu(np.ones((32, 32)), 1)
        addw("L0_TRIL", tri)
        nw = np.asarray(inputs["L0_attn_norm_w"], np.float32)
        nwt = np.zeros((128, 8), np.float32)
        for dt in range(8):
            nwt[:, dt] = nw[(dt % 2) * 128:(dt % 2) * 128 + 128]
        addw("L0_normwT", nwt)
    if "l2attn" in stages:
        for nm, key in (("WinT", "W_in"), ("WlamT", "W_lam"),
                        ("WgateT", "W_gate"), ("WoutT", "W_out")):
            addw(f"L2A_{nm}",
                 np.asarray(inputs[f"L2_attn_{key}"], np.float32).T)
        for nm, key in (("binT", "b_in"), ("blamT", "b_lam"),
                        ("bgateT", "b_gate"), ("boutT", "b_out"),
                        ("gamT", "gamma"), ("betT", "beta")):
            addw(f"L2A_{nm}",
                 np.asarray(inputs[f"L2_attn_{key}"],
                            np.float32).reshape(8, 128).T)
        wshapes["bsel"] = ((128, 8), F32)
    if "l3attn" in stages:
        addw("L3_attn_W_in", inputs["L3_attn_W_in"])
        wxpP = np.zeros((2048, 256), np.float32)
        wxpP[:, 0:192] = np.asarray(inputs["L3_attn_W_xproj"], np.float32)
        addw("L3A_WXP", wxpP)
        wdtP = np.zeros((128, 2048), np.float32)
        wdtP[0:64, :] = np.asarray(inputs["L3_attn_W_dt"], np.float32)
        addw("L3A_WDTP", wdtP)
        addw("L3_attn_W_out", inputs["L3_attn_W_out"])
        addw("L3A_ALOG",
             np.asarray(inputs["L3_attn_A_log"],
                        np.float32).reshape(16, 128, 64).transpose(
                            1, 0, 2).reshape(128, 1024))
        addw("L3A_bdt", np.asarray(inputs["L3_attn_b_dt"],
                                   np.float32).reshape(16, 128).T)
        addw("L3A_D", np.asarray(inputs["L3_attn_D"],
                                 np.float32).reshape(16, 128).T)
        addw("L3A_convT",
             np.asarray(inputs["L3_attn_conv_w"],
                        np.float32).reshape(16, 128, 4).transpose(
                            1, 0, 2).reshape(128, 64))
        addw("L3A_convb", np.asarray(inputs["L3_attn_conv_b"],
                                     np.float32).reshape(16, 128).T)
        addw("L3A_lnw", np.asarray(inputs["L3_attn_ln_w"],
                                   np.float32).reshape(8, 128).T)
        addw("L3A_lnb", np.asarray(inputs["L3_attn_ln_b"],
                                   np.float32).reshape(8, 128).T)
    if "l2ffn" in stages:
        addw("L2_ffn_qT",
             np.transpose(np.asarray(inputs["L2_ffn_qweights"],
                                     np.float32), (0, 2, 1)))
        scr = np.repeat(np.asarray(inputs["L2_ffn_scales"], np.float32),
                        16, axis=2)
        addw("L2_ffn_scT", np.transpose(scr, (0, 2, 1)))
        addw("L2_ffn_biasT",
             np.asarray(inputs["L2_ffn_biases"],
                        np.float32).reshape(6, 8, 128).transpose(0, 2, 1))
        la = np.asarray(inputs["L2_ffn_lora_a"], np.float32)
        lap = np.zeros((6, 1024, 128), np.float32)
        lap[:, :, 0:32] = np.transpose(la, (0, 2, 1))
        addw("L2_ffn_laT", lap)
        lb = np.asarray(inputs["L2_ffn_lora_b"], np.float32)
        lbp = np.zeros((6, 128, 1024), np.float32)
        lbp[:, 0:32, :] = np.transpose(lb, (0, 2, 1))
        addw("L2_ffn_lbT", lbp)
        addw("L2_ffn_ln_gT",
             np.asarray(inputs["L2_ffn_ln_gamma"],
                        np.float32).reshape(3, 8, 128).transpose(0, 2, 1))
        addw("L2_ffn_ln_bT",
             np.asarray(inputs["L2_ffn_ln_beta"],
                        np.float32).reshape(3, 8, 128).transpose(0, 2, 1))
    if "l3ffn" in stages:
        addw("L3_ffn_weight", inputs["L3_ffn_weight"])
        addw("L3_ffn_biasT",
             np.asarray(inputs["L3_ffn_bias"],
                        np.float32).reshape(8, 128).T)
    if "l1attn" in stages:
        wxp_t = np.asarray(inputs["L1_attn_W_xproj"], np.float32).T
        wxp_pad = np.zeros((1024, 128), np.float32)
        wxp_pad[:, 0:96] = wxp_t
        addw("L1_attn_W_xprojT", wxp_pad)
        addw("L1_attn_W_dtT", inputs["L1_attn_W_dt"].T)
        addw("L1_attn_b_dt", inputs["L1_attn_b_dt"])
        addw("L1_attn_A_log", inputs["L1_attn_A_log"])
        addw("L1_attn_D_param", inputs["L1_attn_D_param"])

    with allow_ici_all_to_all():
        nc = build_program(wshapes, stages)
        shared["consts"] = _consts_np()
        in_maps = []
        for c in range(NCORES):
            m = {"x_fm": np.ascontiguousarray(x[c].T)}
            m.update(shared)
            if "bsel" in wshapes:
                bs = np.zeros((128, 8), np.float32)
                bs[:, c] = 1.0
                m["bsel"] = bs
            in_maps.append(m)
        res = run_bass_kernel_spmd(nc, in_maps, core_ids=list(range(NCORES)))

    out = np.stack([np.ascontiguousarray(res.results[c]["out_x"].T)
                    for c in range(NCORES)])
    return out
